# version 06
# baseline (speedup 1.0000x reference)
import contextlib
import math
import numpy as np
import ml_dtypes
import concourse.bass as bass
import concourse.mybir as mybir
from concourse.bass_utils import run_bass_kernel_spmd

F32 = mybir.dt.float32
BF16 = mybir.dt.bfloat16
I32 = mybir.dt.int32
AF = mybir.ActivationFunctionType
ALU = mybir.AluOpType
AX = mybir.AxisListType

ENGINES = ("sync", "scalar", "vector", "tensor", "gpsimd")


class Ctr:
    __slots__ = ("sem", "n", "step", "name")

    def __init__(self, sem, step, name):
        self.sem, self.n, self.step, self.name = sem, 0, step, name


class Buf:
    __slots__ = ("name", "wr", "rd")

    def __init__(self, name):
        self.name = name
        self.wr = {}
        self.rd = {}


class Op:
    __slots__ = ("engine", "fn", "waits", "incs")

    def __init__(self, engine, fn, waits, incs):
        self.engine, self.fn, self.waits, self.incs = engine, fn, waits, incs


class Rec:

    def __init__(self, nc):
        self.nc = nc
        self.ops = []
        self.stack = contextlib.ExitStack()
        self._ctrs = {}
        self.ec = {e: self.ctr("c_" + e, 1) for e in ENGINES}
        self.floor = {e: {} for e in ENGINES}
        self.n_sbuf_bytes = 0

    def ctr(self, name, step=1):
        if name in self._ctrs:
            return self._ctrs[name]
        sem = self.stack.enter_context(self.nc.semaphore(name))
        c = Ctr(sem, step, name)
        self._ctrs[name] = c
        return c

    def sbuf(self, name, shape, dtype):
        itemsz = {F32: 4, BF16: 2, I32: 4}[dtype]
        nb = int(np.prod(shape[1:])) * itemsz
        if getattr(self, "arena", None) is not None:
            off = (self.acur + 3) & ~3
            self.acur = off + nb
            assert self.acur <= self.arena_bytes, \
                f"arena overflow at {name}: {self.acur} > {self.arena_bytes}"
            self.apeak = max(getattr(self, "apeak", 0), self.acur)
            view = self.arena[0:shape[0], off:off + nb].bitcast(dtype)
            if len(shape) > 2:
                dims = " ".join(f"d{i}" for i in range(1, len(shape)))
                view = view.rearrange(
                    f"p ({dims}) -> p {dims}",
                    **{f"d{i}": shape[i] for i in range(1, len(shape) - 1)})
            return view
        t = self.stack.enter_context(self.nc.sbuf_tensor(name, shape, dtype))
        self.n_sbuf_bytes += nb
        return t

    def use_arena(self, nbytes):
        self.arena = self.stack.enter_context(
            self.nc.sbuf_tensor("arena", [128, nbytes], mybir.dt.int8))
        self.arena_bytes = nbytes
        self.acur = 0
        self.apeak = 0

    def open_region(self):
        if getattr(self, "arena", None) is not None:
            self.acur = self.region_base

    def set_region_base(self):
        self.region_base = (self.acur + 3) & ~3

    def psum(self, name, shape, dtype=F32):
        return self.stack.enter_context(self.nc.psum_tensor(name, shape, dtype))

    def op(self, engine, fn, reads=(), writes=(), ctr=None, inc=None):
        myctr = ctr if ctr is not None else self.ec[engine]
        waits = {}

        def need(tokmap):
            for c, v in tokmap.items():
                if waits.get(c, 0) < v:
                    waits[c] = v

        for b in reads:
            need(b.wr)
        for b in writes:
            need(b.wr)
            need(b.rd)
        for c in list(waits):
            if c.step == 16 and waits[c] < c.n:
                waits[c] = c.n
        fl = self.floor[engine]
        wl = []
        for c, v in waits.items():
            if fl.get(c, 0) < v:
                wl.append((c, v))
                fl[c] = v
        inc_amt = myctr.step if inc is None else inc
        myctr.n += inc_amt
        tok = (myctr, myctr.n)
        for b in writes:
            b.wr[myctr] = myctr.n
            b.rd = {}
        for b in reads:
            b.rd[myctr] = myctr.n
        self.ops.append(Op(engine, fn, wl, [(myctr, inc_amt)]))
        return tok

    def dma(self, queue, out, in_, ctr, reads=(), writes=(), **kw):
        def fn(eng):
            return eng.dma_start(out=out, in_=in_, **kw)
        return self.op(queue, fn, reads=reads, writes=writes, ctr=ctr, inc=16)

    def drain(self, engine, ctrs):
        self.ops.append(Op(engine, None, [(c, c.n) for c in ctrs if c.n], []))

    def full_barrier(self):
        bar = self.ctr("c_bar", 1)
        for e in ENGINES:
            self.ops.append(Op(e, lambda eng: eng.nop(), [], [(bar, 1)]))
            bar.n += 1
        dma_waits = [(c, c.n) for c in self._ctrs.values() if c.step == 16 and c.n]
        for e in ENGINES:
            wl = [(bar, bar.n)] + dma_waits
            fl = self.floor[e]
            keep = [(c, v) for c, v in wl if fl.get(c, 0) < v]
            for c, v in keep:
                fl[c] = v
            self.ops.append(Op(e, None, keep, []))

    def emit(self):
        nc = self.nc
        per = {e: [] for e in ENGINES}
        for o in self.ops:
            per[o.engine].append(o)
        with nc.Block() as block:
            for ename in ENGINES:
                items = per[ename]
                if not items:
                    continue

                def make(items):
                    def emit_fn(eng):
                        for it in items:
                            for (c, v) in it.waits:
                                eng.wait_ge(c.sem, v)
                            if it.fn is not None:
                                instr = it.fn(eng)
                                for (c, k) in it.incs:
                                    instr.then_inc(c.sem, k)
                    return emit_fn

                getattr(block, ename)(make(items))


class NS:
    pass



def emit_common(R: Rec):
    C = NS()
    C.ident = R.sbuf("c_ident", [128, 128], BF16)
    C.b_ident = Buf("ident")
    C.onesf = R.sbuf("c_ones", [128, 128], F32)
    C.b_onesf = Buf("onesf")
    idf = R.sbuf("c_identf", [128, 128], F32)
    b_idf = Buf("identf")
    C.idf, C.b_idf = idf, b_idf
    R.op("vector", lambda e: e.memset(C.onesf[:], 1.0), writes=[C.b_onesf])
    R.op("gpsimd", lambda e: e.affine_select(
        out=idf[:, :], in_=C.onesf[:, :], pattern=[[-1, 128]],
        compare_op=ALU.is_equal, fill=0.0, base=0, channel_multiplier=1),
        reads=[C.b_onesf], writes=[b_idf])
    R.op("vector", lambda e: e.tensor_copy(C.ident[:], idf[:]),
         reads=[b_idf], writes=[C.b_ident])
    C.ones1 = R.sbuf("c_ones1", [1, 512], BF16)
    C.b_ones1 = Buf("ones1")
    R.op("vector", lambda e: e.memset(C.ones1[:], 1.0), writes=[C.b_ones1])
    C.ps_tr = R.psum("c_ptr", [128, 2, 1024], BF16)
    C.b_ptr = [Buf(f"c_ptr{i}") for i in range(2)]
    C.ps_big = R.psum("c_pbig", [128, 4, 512], F32)
    C.b_big = [Buf(f"c_pbig{i}") for i in range(4)]
    C.ps_ab = R.psum("c_pab", [128, 2, 512], F32)
    C.b_ab = [Buf(f"c_pab{i}") for i in range(2)]
    C.cpool = R.sbuf("c_pool", [128, 8], F32)
    C.b_cpool = Buf("cpool")
    C._consts = {}

    def const(val):
        val = float(val)
        if val not in C._consts:
            idx = len(C._consts)
            assert idx < 8
            ap = C.cpool[:, idx:idx + 1]
            R.op("vector", lambda e, ap=ap, val=val: e.memset(ap, val),
                 writes=[C.b_cpool])
            R.nc.const_aps.register(F32, val, ap)
            C._consts[val] = ap
        return val
    C.const = const
    return C


def big_all(C):
    return [C.b_big[0], C.b_big[1], C.b_big[2], C.b_big[3]]



def ln_state(R: Rec, name, sq=None, b_sq=None, xln=None, b_xlnb=None):
    st = NS()
    if sq is None:
        st.sq = R.sbuf(name + "_sq", [128, 1024], BF16)
        st.b_sq = Buf(name + "_sq")
    else:
        st.sq, st.b_sq = sq, b_sq
    st.st = R.sbuf(name + "_st", [128, 6], F32)
    st.b_st = Buf(name + "_st")
    if xln is None:
        st.xln = R.sbuf(name + "_xln", [128, 1, 1024], BF16)
        st.b_xln = [Buf(name + "_xln0")]
    else:
        st.xln = xln
        st.b_xln = [b_xlnb]
    st.b_xln.append(st.b_xln[0])
    return st


def emit_ln_xlnt(R: Rec, C, st, x_ap_fn, b_x, xlnt_t, b_xlnt, n_tiles=16,
                 dump_xln=None):
    for i in range(n_tiles):
        xi = x_ap_fn(i)
        R.op("scalar", lambda e, xi=xi: e.activation(
            out=st.sq[:], in_=xi, func=AF.Square, accum_out=st.st[:, 1:2]),
            reads=[b_x], writes=[st.b_sq, st.b_st])
        R.op("vector", lambda e, xi=xi: e.reduce_sum(
            st.st[:, 0:1], xi, axis=AX.X), reads=[b_x], writes=[st.b_st])
        R.op("vector", lambda e: e.tensor_scalar_mul(st.st[:, 2:3], st.st[:, 0:1], 1.0 / 1024),
             reads=[st.b_st], writes=[st.b_st])
        R.op("vector", lambda e: e.tensor_mul(st.st[:, 3:4], st.st[:, 2:3], st.st[:, 2:3]),
             reads=[st.b_st], writes=[st.b_st])
        R.op("vector", lambda e: e.scalar_tensor_tensor(
            out=st.st[:, 3:4], in0=st.st[:, 1:2], scalar=1.0 / 1024, in1=st.st[:, 3:4],
            op0=ALU.mult, op1=ALU.subtract), reads=[st.b_st], writes=[st.b_st])
        R.op("scalar", lambda e: e.activation(
            out=st.st[:, 4:5], in_=st.st[:, 3:4], func=AF.Sqrt, bias=1e-5),
            reads=[st.b_st, C.b_cpool], writes=[st.b_st])
        R.op("vector", lambda e: e.reciprocal(st.st[:, 5:6], st.st[:, 4:5]),
             reads=[st.b_st], writes=[st.b_st])
        xo = st.xln[:, 0, :]
        R.op("vector", lambda e, xi=xi, xo=xo: e.tensor_scalar(
            out=xo, in0=xi, scalar1=st.st[:, 2:3], scalar2=st.st[:, 5:6],
            op0=ALU.subtract, op1=ALU.mult),
            reads=[b_x, st.b_st], writes=[st.b_xln[i % 2]])
        if dump_xln is not None:
            R.dma("sync", dump_xln[0][dump_xln[1], :, i, :], xo, dump_xln[2],
                  reads=[st.b_xln[i % 2]], writes=[dump_xln[3]])
        for c in range(8):
            pt = C.ps_tr[:, c % 2, 0:128]
            xoc = xo.rearrange("p (c f) -> p c f", f=128)[:, c, :]
            R.op("tensor", lambda e, xoc=xoc, pt=pt: e.transpose(pt, xoc, C.ident[:]),
                 reads=[st.b_xln[i % 2], C.b_ident], writes=[C.b_ptr[c % 2]])
            R.op("vector", lambda e, c=c, i=i, pt=pt: e.tensor_copy(
                xlnt_t[:, c, i * 128:(i + 1) * 128], pt),
                reads=[C.b_ptr[c % 2]], writes=[b_xlnt])



def emit_int_to_bf16(R: Rec, tile, b_tile, c_ld, c_st, src_ap, dst_ap,
                     rows, cols, b_dst):
    assert rows % 128 == 0
    for r in range(rows // 128):
        s = r % 2
        R.dma("gpsimd", tile[:, s, 0:cols], src_ap[r * 128:(r + 1) * 128, :],
              c_ld[s], writes=[b_tile[s]])
        R.dma("gpsimd", dst_ap[r * 128:(r + 1) * 128, :], tile[:, s, 0:cols],
              c_st[s], reads=[b_tile[s]], writes=[b_dst])



def l0ffn_state(R: Rec, name):
    st = NS()
    st.xlnt = R.sbuf(name + "_xlnt", [128, 8, 2048], BF16)
    st.b_xlnt = Buf(name + "_xlnt")
    st.wg_s = R.sbuf(name + "_wg", [128, 2, 8, 128], BF16)
    st.wu_s = R.sbuf(name + "_wu", [128, 2, 8, 128], BF16)
    st.wd_s = R.sbuf(name + "_wd", [128, 2, 1024], BF16)
    st.b_wg = [Buf(name + f"_wg{i}") for i in range(2)]
    st.b_wu = [Buf(name + f"_wu{i}") for i in range(2)]
    st.b_wd = [Buf(name + f"_wd{i}") for i in range(2)]
    st.c_gu = [R.ctr(name + f"_cgu{i}", 16) for i in range(2)]
    st.c_wd = [R.ctr(name + f"_cwd{i}", 16) for i in range(2)]
    st.h_s = R.sbuf(name + "_h", [128, 24, 1024], BF16)
    st.b_h = Buf(name + "_h")
    st.sg = R.sbuf(name + "_sg", [128, 512], F32)
    st.b_sg = Buf(name + "_sg")
    st.ln = ln_state(R, name + "_ln")
    return st


def emit_l0ffn(R: Rec, C, st, x_t, b_x, wgT_bf, wuT_bf, wdT_bf, b_w):
    S = C.const(1.0 / 127.0)
    NT, NHB = 16, 24
    emit_ln_xlnt(R, C, st.ln, lambda i: x_t[:, i, :], b_x, st.xlnt, st.b_xlnt, NT)
    ps_g, ps_u = C.ps_ab[:, 0, :], C.ps_ab[:, 1, :]
    b_pg, b_pu = C.b_ab[0], C.b_ab[1]
    ps_y = [C.ps_big[:, 0:2, :].rearrange("p a b -> p (a b)"),
            C.ps_big[:, 2:4, :].rearrange("p a b -> p (a b)")]
    b_py = [[C.b_big[0], C.b_big[1]], [C.b_big[2], C.b_big[3]]]

    def load_gu(hb, slot):
        R.dma("scalar", st.wg_s[:, slot, :, :],
              wgT_bf[:, hb * 128:(hb + 1) * 128].rearrange("(c k) h -> k c h", k=128),
              st.c_gu[slot], reads=[b_w], writes=[st.b_wg[slot]])
        R.dma("scalar", st.wu_s[:, slot, :, :],
              wuT_bf[:, hb * 128:(hb + 1) * 128].rearrange("(c k) h -> k c h", k=128),
              st.c_gu[slot], reads=[b_w], writes=[st.b_wu[slot]])

    def load_wd(hb, slot):
        R.dma("sync", st.wd_s[:, slot, :], wdT_bf[hb * 128:(hb + 1) * 128, :],
              st.c_wd[slot], reads=[b_w], writes=[st.b_wd[slot]])

    for half in range(2):
        hbase = half * 1024
        load_gu(0, 0)
        for hb in range(NHB):
            slot = hb % 2
            if hb + 1 < NHB:
                load_gu(hb + 1, (hb + 1) % 2)
            for th in range(2):
                rhs = st.xlnt[:, :, hbase + th * 512: hbase + (th + 1) * 512]

                def mm_gu(e, slot=slot, rhs=rhs):
                    for cch in range(8):
                        e.matmul(ps_g, st.wg_s[:, slot, cch, :],
                                 rhs[:, cch, :], start=(cch == 0), stop=(cch == 7))
                    for cch in range(8):
                        last = e.matmul(ps_u, st.wu_s[:, slot, cch, :],
                                        rhs[:, cch, :], start=(cch == 0), stop=(cch == 7))
                    return last
                R.op("tensor", mm_gu, reads=[st.b_wg[slot], st.b_wu[slot], st.b_xlnt],
                     writes=[b_pg, b_pu])
                R.op("scalar", lambda e: e.activation(
                    out=st.sg[:, :], in_=ps_g, func=AF.Silu, scale=S),
                    reads=[b_pg, C.b_cpool], writes=[st.b_sg])
                R.op("vector", lambda e, th=th, hb=hb: e.scalar_tensor_tensor(
                    out=st.h_s[:, hb, th * 512:(th + 1) * 512], in0=ps_u,
                    scalar=S, in1=st.sg[:, :], op0=ALU.mult, op1=ALU.mult),
                    reads=[b_pu, st.b_sg], writes=[st.b_h])
        for o in range(4):
            load_wd(0, 0)
            for hb in range(NHB):
                slot = hb % 2
                if hb + 1 < NHB:
                    load_wd(hb + 1, (hb + 1) % 2)
                for tt in range(2):
                    col = o * 256 + tt * 128

                    def mm_d(e, hb=hb, slot=slot, tt=tt, col=col):
                        e.matmul(ps_y[tt][:, 0:512], st.h_s[:, hb, col:col + 128],
                                 st.wd_s[:, slot, 0:512],
                                 start=(hb == 0), stop=(hb == NHB - 1))
                        return e.matmul(ps_y[tt][:, 512:1024],
                                        st.h_s[:, hb, col:col + 128],
                                        st.wd_s[:, slot, 512:1024],
                                        start=(hb == 0), stop=(hb == NHB - 1))
                    R.op("tensor", mm_d, reads=[st.b_h, st.b_wd[slot]],
                         writes=b_py[tt])
            for tt in range(2):
                ti = half * 8 + o * 2 + tt
                R.op("vector", lambda e, tt=tt, ti=ti: e.scalar_tensor_tensor(
                    out=x_t[:, ti, :], in0=ps_y[tt], scalar=S, in1=x_t[:, ti, :],
                    op0=ALU.mult, op1=ALU.add),
                    reads=b_py[tt] + [b_x], writes=[b_x])



def l1attn_state(R: Rec, C, name):
    st = NS()
    st.xlnt = R.sbuf(name + "_xlnt", [128, 8, 2048], BF16)
    st.b_xlnt = Buf(name + "_xlnt")
    st.v = R.sbuf(name + "_v", [128, 16, 16 * 65], BF16)
    st.b_v = Buf(name + "_v")
    st.q = R.sbuf(name + "_q", [128, 8, 2048], BF16)
    st.k = R.sbuf(name + "_k", [128, 8, 2048], BF16)
    st.b_q = Buf(name + "_q")
    st.b_k = Buf(name + "_k")
    st.w = R.sbuf(name + "_w", [128, 8, 512], BF16)
    st.b_w = [Buf(name + "_w0")]
    st.c_w = [R.ctr(name + "_cw0", 16)]
    st.att = st.w[:, :, :].rearrange("p a b -> p (a b)")[:, 0:2048].rearrange(
        "p (a b) -> p a b", a=16)
    st.b_att = st.b_w[0]
    vflat = st.v[:, :, :].rearrange("p a b -> p (a b)")
    st.q2 = vflat[:, 9216:9728]
    st.b_q2 = st.b_v
    st.nrm = vflat[0:16, 0:4096].rearrange("p (a b) -> p a b", a=2)
    st.b_nrm = st.b_v
    st.br = R.sbuf(name + "_br", [1, 2, 1024], BF16)
    st.b_br = Buf(name + "_br")
    st.c_br = R.ctr(name + "_cbr", 16)
    st.sums = R.sbuf(name + "_sums", [128, 2], F32)
    st.b_sums = Buf(name + "_sums")
    st.aop = R.sbuf(name + "_aop", [128, 128], BF16)
    st.b_aop = Buf(name + "_aop")
    st.maskT = R.sbuf(name + "_maskT", [128, 128], F32)
    st.b_maskT = Buf(name + "_maskT")
    st.BO = R.sbuf(name + "_BO", [128, 8, 16], BF16)
    st.b_BO = Buf(name + "_BO")
    st.EX = vflat[0:16, 4096:5120].rearrange("p (a b) -> p a b", a=8)
    st.b_EX = st.b_v
    st.ln = ln_state(R, name + "_ln",
                     sq=vflat[:, 7168:8192], b_sq=st.b_v,
                     xln=vflat[:, 6144:7168].rearrange("p (a b) -> p a b", a=1),
                     b_xlnb=st.b_v)
    return st


def emit_l1attn_setup(R: Rec, C, st, name):
    scrf = C.idf[:, 0:32]
    b_scrf = C.b_idf
    zer, b_zer = C.idf, C.b_idf
    R.op("vector", lambda e: e.memset(zer[:], 0.0), reads=[b_zer], writes=[b_zer])
    R.op("gpsimd", lambda e: e.affine_select(
        out=st.maskT[:, :], in_=zer[:, :], pattern=[[1, 128]],
        compare_op=ALU.is_ge, fill=-10000.0, base=0, channel_multiplier=-1),
        reads=[b_zer], writes=[st.b_maskT])
    scrf16 = zer[0:16, :]
    b_s16 = b_zer
    ones16 = C.onesf[0:16, :]
    b_o16 = C.b_onesf
    for c in range(8):
        R.op("gpsimd", lambda e, c=c: e.affine_select(
            out=scrf[:, 0:16], in_=C.onesf[:, 0:16], pattern=[[-64, 16]],
            compare_op=ALU.is_ge, fill=0.0, base=128 * c, channel_multiplier=1),
            reads=[C.b_onesf], writes=[b_scrf])
        R.op("gpsimd", lambda e, c=c: e.affine_select(
            out=scrf[:, 16:32], in_=scrf[:, 0:16], pattern=[[64, 16]],
            compare_op=ALU.is_ge, fill=0.0, base=63 - 128 * c, channel_multiplier=-1),
            reads=[b_scrf], writes=[b_scrf])
        R.op("vector", lambda e, c=c: e.tensor_copy(st.BO[:, c, :], scrf[:, 16:32]),
             reads=[b_scrf], writes=[st.b_BO])



def emit_l1attn(R: Rec, C, st, x_ap_fn, b_x, xadd_fn, WqT, WkT, WvT, WoT,
                bq, bk, bv, bo, dump=None):
    def dmp(key, sb_ap, b):
        if dump is not None:
            R.dma("sync", dump[key][dump["p"]], sb_ap, dump["c"],
                  reads=[b], writes=[dump["b"]])
    NT = 16
    ESC = C.const(1.0 / (math.sqrt(128.0) * 1.5))
    emit_ln_xlnt(R, C, st.ln, x_ap_fn, b_x, st.xlnt, st.b_xlnt, NT,
                 dump_xln=(dump or {}).get("xlnhook"))

    dmp("d_xlnt", st.xlnt[:, :, :], st.b_xlnt)
    R.dma("gpsimd", st.br[0:1, 0, :], bq[None, :], st.c_br,
          reads=[st.b_br], writes=[st.b_br])
    R.dma("gpsimd", st.br[0:1, 1, :], bk[None, :], st.c_br,
          reads=[st.b_br], writes=[st.b_br])

    def loadW(W, slot, col0):
        R.dma("gpsimd", st.w[:, :, :],
              W[:, col0:col0 + 512].rearrange("(c k) n -> k c n", k=128),
              st.c_w[0], writes=[st.b_w[0]])

    for iqk, (W, brow, dst, b_dst) in enumerate(
            [(WqT, 0, st.q, st.b_q), (WkT, 1, st.k, st.b_k)]):
        for hg in range(2):
            loadW(W, iqk, hg * 512)
            for tc in range(4):
                for c in range(4):
                    cg = hg * 4 + c
                    ps = C.ps_ab[:, iqk, :]
                    b_ps = C.b_ab[iqk]

                    def mm_qk(e, c=c, tc=tc, iqk=iqk, brow=brow, cg=cg, ps=ps):
                        for cc in range(8):
                            e.matmul(ps, st.w[:, cc, c * 128:(c + 1) * 128],
                                     st.xlnt[:, cc, tc * 512:(tc + 1) * 512],
                                     start=(cc == 0), stop=False)
                        return e.matmul(ps, st.br[0:1, brow, cg * 128:(cg + 1) * 128],
                                        C.ones1[0:1, 0:512], start=False, stop=True)
                    R.op("tensor", mm_qk, reads=[st.b_w[0], st.b_xlnt, st.b_br,
                                                 C.b_ones1], writes=[b_ps])
                    R.op("vector", lambda e, tc=tc, cg=cg, dst=dst, ps=ps: e.tensor_copy(
                        dst[:, cg, tc * 512:(tc + 1) * 512], ps),
                        reads=[b_ps], writes=[b_dst])
                    R.op("scalar", lambda e, tc=tc, cg=cg, dst=dst: e.activation(
                        out=st.q2[:, :], in_=dst[:, cg, tc * 512:(tc + 1) * 512],
                        func=AF.Square),
                        reads=[b_dst], writes=[st.b_q2])
                    R.op("tensor", lambda e, cg=cg, tc=tc: e.matmul(
                        C.ps_big[0:16, tc, :], st.BO[:, cg, :], st.q2[:, :],
                        start=(cg == 0), stop=(cg == 7)),
                        reads=[st.b_BO, st.b_q2], writes=[C.b_big[tc]])
        for tc in range(4):
            R.op("vector", lambda e, tc=tc, iqk=iqk: e.tensor_copy(
                st.nrm[:, iqk, tc * 512:(tc + 1) * 512], C.ps_big[0:16, tc, :]),
                reads=[C.b_big[tc]], writes=[st.b_nrm])
    nf = st.nrm[:, :, :].rearrange("p a b -> p (a b)")
    R.op("vector", lambda e: e.tensor_scalar_max(nf, nf, 1e-16),
         reads=[st.b_nrm], writes=[st.b_nrm])
    R.op("scalar", lambda e: e.activation(out=nf, in_=nf, func=AF.Ln),
         reads=[st.b_nrm, C.b_cpool], writes=[st.b_nrm])
    R.op("scalar", lambda e: e.activation(out=nf, in_=nf, func=AF.Exp, scale=C.const(-0.5)),
         reads=[st.b_nrm, C.b_cpool], writes=[st.b_nrm])
    scrf16 = C.idf[0:16, :]
    for c in range(8):
        R.op("gpsimd", lambda e, c=c: e.affine_select(
            out=scrf16[:, :], in_=C.onesf[0:16, :], pattern=[[1, 128]],
            compare_op=ALU.is_ge, fill=0.0, base=128 * c, channel_multiplier=-64),
            reads=[C.b_onesf, C.b_idf], writes=[C.b_idf])
        R.op("gpsimd", lambda e, c=c: e.affine_select(
            out=scrf16[:, :], in_=scrf16[:, :], pattern=[[-1, 128]],
            compare_op=ALU.is_ge, fill=0.0, base=63 - 128 * c, channel_multiplier=64),
            reads=[C.b_idf], writes=[C.b_idf])
        R.op("vector", lambda e, c=c: e.tensor_copy(st.EX[:, c, :], scrf16[:, :]),
             reads=[C.b_idf], writes=[st.b_EX])
    for iqk, (dst, b_dst) in enumerate([(st.q, st.b_q), (st.k, st.b_k)]):
        for cg in range(8):
            for tc in range(4):
                ps = C.ps_ab[:, iqk, :]
                b_ps = C.b_ab[iqk]
                R.op("tensor", lambda e, cg=cg, tc=tc, iqk=iqk, ps=ps: e.matmul(
                    ps, st.EX[:, cg, :], st.nrm[:, iqk, tc * 512:(tc + 1) * 512],
                    start=True, stop=True),
                    reads=[st.b_EX, st.b_nrm], writes=[b_ps])
                R.op("vector", lambda e, cg=cg, tc=tc, dst=dst, ps=ps: e.tensor_tensor(
                    out=dst[:, cg, tc * 512:(tc + 1) * 512],
                    in0=dst[:, cg, tc * 512:(tc + 1) * 512], in1=ps, op=ALU.mult),
                    reads=[b_ps, b_dst], writes=[b_dst])

    dmp("d_q", st.q[:, :, :], st.b_q)
    dmp("d_k", st.k[:, :, :], st.b_k)
    R.dma("gpsimd", st.br[0:1, 0, :], bv[None, :], st.c_br,
          reads=[st.b_br], writes=[st.b_br])
    R.op("vector", lambda e: e.memset(
        st.v[:, :, :].rearrange("p a (h w) -> p a h w", w=65)[:, :, :, 64:65], 1.0),
        reads=[st.b_v], writes=[st.b_v])
    for nh in range(2):
        loadW(WvT, 0, nh * 512)
        for t in range(NT):
            ps = C.ps_ab[:, t % 2, :]
            b_ps = C.b_ab[t % 2]

            def mm_v(e, t=t, nh=nh, ps=ps):
                for cc in range(8):
                    e.matmul(ps, st.xlnt[:, cc, t * 128:(t + 1) * 128],
                             st.w[:, cc, :], start=(cc == 0), stop=False)
                return e.matmul(ps, C.ones1[0:1, 0:128],
                                st.br[0:1, 0, nh * 512:(nh + 1) * 512],
                                start=False, stop=True)
            R.op("tensor", mm_v, reads=[st.b_w[0], st.b_xlnt, C.b_ones1, st.b_br],
                 writes=[b_ps])
            vout = st.v[:, t, :].rearrange("p (h w) -> p h w", w=65)[
                :, nh * 8:(nh + 1) * 8, 0:64]
            R.op("vector", lambda e, vout=vout, ps=ps: e.tensor_copy(
                vout, ps.rearrange("p (h w) -> p h w", w=64)),
                reads=[b_ps], writes=[st.b_v])

    dmp("d_v", st.v[:, :, :], st.b_v)
    aoT, b_aoT = st.xlnt, st.b_xlnt
    scT = C.ps_big[:, :, :].rearrange("p a b -> p (a b)")
    for hp in range(8):
        for qt in range(NT):
            ntk = qt + 1
            for sub in range(2):
                h = 2 * hp + sub
                qs = st.q[sub * 64:sub * 64 + 64, hp, :]
                ks = st.k[sub * 64:sub * 64 + 64, hp, :]

                def mm_sc(e, qs=qs, ks=ks, qt=qt, ntk=ntk):
                    last = None
                    for j in range(ntk):
                        last = e.matmul(scT[:, j * 128:(j + 1) * 128],
                                        ks[:, j * 128:(j + 1) * 128],
                                        qs[:, qt * 128:(qt + 1) * 128],
                                        start=True, stop=True)
                    return last
                R.op("tensor", mm_sc, reads=[st.b_q, st.b_k], writes=big_all(C))
                R.op("vector", lambda e, qt=qt: e.tensor_tensor(
                    out=scT[:, qt * 128:(qt + 1) * 128],
                    in0=scT[:, qt * 128:(qt + 1) * 128], in1=st.maskT[:, :],
                    op=ALU.add),
                    reads=[st.b_maskT] + big_all(C), writes=big_all(C))
                R.op("scalar", lambda e, ntk=ntk: e.activation(
                    out=st.att[:, 0:ntk, :].rearrange("p a b -> p (a b)"),
                    in_=scT[:, 0:ntk * 128], func=AF.Exp, scale=ESC),
                    reads=big_all(C) + [C.b_cpool], writes=[st.b_att])

                ps = C.ps_ab[:, sub, :]
                b_ps = C.b_ab[sub]

                def mm_ao(e, h=h, ntk=ntk, ps=ps):
                    last = None
                    for j in range(ntk):
                        last = e.matmul(ps[:, 0:65], st.att[:, j, :],
                                        st.v[:, j, h * 65:(h + 1) * 65],
                                        start=(j == 0), stop=(j == ntk - 1))
                    return last
                R.op("tensor", mm_ao, reads=[st.b_att, st.b_v], writes=[b_ps])
                R.op("vector", lambda e, sub=sub, ps=ps: e.reciprocal(
                    st.sums[:, sub:sub + 1], ps[:, 64:65]),
                    reads=[b_ps], writes=[st.b_sums])
                R.op("vector", lambda e, sub=sub, ps=ps: e.tensor_scalar(
                    out=st.aop[:, sub * 64:sub * 64 + 64], in0=ps[:, 0:64],
                    scalar1=st.sums[:, sub:sub + 1], scalar2=None, op0=ALU.mult),
                    reads=[b_ps, st.b_sums], writes=[st.b_aop])
            R.op("tensor", lambda e, qt=qt: e.transpose(
                C.ps_tr[:, qt % 2, 0:128], st.aop[:, :], C.ident[:]),
                reads=[st.b_aop, C.b_ident], writes=[C.b_ptr[qt % 2]])
            R.op("vector", lambda e, hp=hp, qt=qt: e.tensor_copy(
                aoT[:, hp, qt * 128:(qt + 1) * 128], C.ps_tr[:, qt % 2, 0:128]),
                reads=[C.b_ptr[qt % 2]], writes=[b_aoT])

    dmp("d_ao", st.xlnt[:, :, :], st.b_xlnt)
    R.dma("gpsimd", st.br[0:1, 0, :], bo[None, :], st.c_br,
          reads=[st.b_br], writes=[st.b_br])
    for nh in range(2):
        loadW(WoT, nh % 2, nh * 512)
        for t in range(NT):
            ps = C.ps_ab[:, t % 2, :]
            b_ps = C.b_ab[t % 2]

            def mm_o(e, t=t, nh=nh, ps=ps):
                for cc in range(8):
                    e.matmul(ps, aoT[:, cc, t * 128:(t + 1) * 128],
                             st.w[:, cc, :], start=(cc == 0), stop=False)
                return e.matmul(ps, C.ones1[0:1, 0:128],
                                st.br[0:1, 0, nh * 512:(nh + 1) * 512],
                                start=False, stop=True)
            R.op("tensor", mm_o, reads=[st.b_w[0], b_aoT, C.b_ones1, st.b_br],
                 writes=[b_ps])
            xadd_fn(t, nh, ps, b_ps)



def l1ffn_pre_state(R: Rec, name):
    st = NS()
    st.cent = R.sbuf(name + "_cent", [128, 8, 8], F32)
    st.b_cent = Buf(name + "_cent")
    st.xf = R.sbuf(name + "_xf", [128, 1024], F32)
    st.b_xf = Buf(name + "_xf")
    st.iot = R.sbuf(name + "_iot", [128, 8], F32)
    st.b_iot = Buf(name + "_iot")
    st.ftT = R.sbuf(name + "_ftT", [128, 8, 128], F32)
    st.b_ftT = Buf(name + "_ftT")
    st.sc = R.sbuf(name + "_sc", [128, 5 * 8], F32)
    st.b_sc = Buf(name + "_sc")
    st.fhat = R.sbuf(name + "_fhat", [128, 1024], BF16)
    st.b_fhat = Buf(name + "_fhat")
    st.fhT = R.sbuf(name + "_fhT", [128, 8, 128], BF16)
    st.b_fhT = Buf(name + "_fhT")
    st.c_f = R.ctr(name + "_cf", 16)
    st.base = R.sbuf(name + "_base", [128, 2, 1024], F32)
    st.b_base = [Buf(name + f"_base{i}") for i in range(2)]
    st.c_ld = R.ctr(name + "_cld", 16)
    st.ln = ln_state(R, name + "_ln")
    return st


def l1ffn_main_state(R: Rec, name):
    st = NS()
    st.hrT = R.sbuf(name + "_hrT", [128, 32, 1024], BF16)
    st.b_hrT = Buf(name + "_hrT")
    st.he = R.sbuf(name + "_he", [128, 8, 1024], BF16)
    st.b_he = Buf(name + "_he")
    st.fsl = R.sbuf(name + "_fsl", [128, 8, 1024], BF16)
    st.b_fsl = Buf(name + "_fsl")
    st.c_fsl = R.ctr(name + "_cfsl", 16)
    st.w1 = R.sbuf(name + "_w1", [128, 3, 8, 128], BF16)
    st.b_w1 = [Buf(name + f"_w1{i}") for i in range(3)]
    st.c_w1 = [R.ctr(name + f"_cw1{i}", 16) for i in range(3)]
    st.w2 = R.sbuf(name + "_w2", [128, 8, 512], BF16)
    st.b_w2 = [Buf(name + f"_w2{i}") for i in range(8)]
    st.c_w2 = [R.ctr(name + f"_cw2{i}", 16) for i in range(8)]
    st.stage = R.sbuf(name + "_stage", [128, 8, 512], F32)
    st.b_stage = [Buf(name + f"_stage{i}") for i in range(8)]
    st.c_acc = R.ctr(name + "_cacc", 16)
    st.gb = R.sbuf(name + "_gb", [128, 2, 8, 8], F32)
    st.b_gb = Buf(name + "_gb")
    st.b1 = R.sbuf(name + "_b1", [128, 32, 8], F32)
    st.b_b1 = Buf(name + "_b1")
    st.b2r = R.sbuf(name + "_b2r", [1, 8, 1024], BF16)
    st.b_b2r = Buf(name + "_b2r")
    st.c_ld = R.ctr(name + "_cld2", 16)
    return st


def emit_l1ffn_pre_setup(R: Rec, C, st, centT):
    R.dma("gpsimd", st.cent[:, :, :],
          centT.rearrange("(c k) e -> k c e", k=128), st.c_ld,
          reads=[st.b_cent], writes=[st.b_cent])
    R.op("gpsimd", lambda e: e.iota(st.iot[:, :], pattern=[[1, 8]], base=0,
                                    channel_multiplier=0,
                                    allow_small_or_imprecise_dtypes=True),
         reads=[st.b_iot], writes=[st.b_iot])
    R.op("vector", lambda e: e.memset(st.sc[:, 32:40], 0.0),
         reads=[st.b_sc], writes=[st.b_sc])


def emit_l1ffn_main_setup(R: Rec, C, st, gbT, b1T, b2):
    R.dma("gpsimd", st.gb[:, :, :, :],
          gbT.rearrange("(c k) g e -> k g c e", k=128), st.c_ld, writes=[st.b_gb])
    R.dma("gpsimd", st.b1[:, :, :],
          b1T.rearrange("(t k) e -> k t e", k=128), st.c_ld, writes=[st.b_b1])
    R.dma("gpsimd", st.b2r[0:1, :, :], b2[None, :, :], st.c_ld, writes=[st.b_b2r])


def emit_l1ffn_pre(R: Rec, C, st, x_ap_fn, b_x, out_hbm, b_out, c_out, toff,
                   fhat_d, b_fd, aE, b_aE):
    lnst = st.ln
    R.op("gpsimd", lambda e: e.affine_select(
        out=C.idf[:, :], in_=C.onesf[:, :], pattern=[[-1, 128]],
        compare_op=ALU.is_equal, fill=0.0, base=0, channel_multiplier=1),
        reads=[C.b_onesf, C.b_idf], writes=[C.b_idf])
    for i in range(16):
        ti = toff + i
        xi = x_ap_fn(i)
        R.op("scalar", lambda e, xi=xi: e.activation(
            out=lnst.sq[:], in_=xi, func=AF.Square, accum_out=lnst.st[:, 1:2]),
            reads=[b_x], writes=[lnst.b_sq, lnst.b_st])
        R.op("vector", lambda e, xi=xi: e.reduce_sum(
            lnst.st[:, 0:1], xi, axis=AX.X), reads=[b_x], writes=[lnst.b_st])
        R.op("vector", lambda e: e.tensor_scalar_mul(
            lnst.st[:, 2:3], lnst.st[:, 0:1], 1.0 / 1024),
            reads=[lnst.b_st], writes=[lnst.b_st])
        R.op("vector", lambda e: e.tensor_mul(
            lnst.st[:, 3:4], lnst.st[:, 2:3], lnst.st[:, 2:3]),
            reads=[lnst.b_st], writes=[lnst.b_st])
        R.op("vector", lambda e: e.scalar_tensor_tensor(
            out=lnst.st[:, 3:4], in0=lnst.st[:, 1:2], scalar=1.0 / 1024,
            in1=lnst.st[:, 3:4], op0=ALU.mult, op1=ALU.subtract),
            reads=[lnst.b_st], writes=[lnst.b_st])
        R.op("scalar", lambda e: e.activation(
            out=lnst.st[:, 4:5], in_=lnst.st[:, 3:4], func=AF.Sqrt, bias=1e-5),
            reads=[lnst.b_st, C.b_cpool], writes=[lnst.b_st])
        R.op("vector", lambda e: e.reciprocal(lnst.st[:, 5:6], lnst.st[:, 4:5]),
             reads=[lnst.b_st], writes=[lnst.b_st])
        xo = lnst.xln[:, 0, :]
        R.op("vector", lambda e, xi=xi: e.tensor_scalar(
            out=st.xf[:, :], in0=xi, scalar1=lnst.st[:, 2:3], scalar2=lnst.st[:, 5:6],
            op0=ALU.subtract, op1=ALU.mult),
            reads=[b_x, lnst.b_st], writes=[st.b_xf])
        bslot = i % 2
        R.op("vector", lambda e, xi=xi, bslot=bslot: e.tensor_add(
            st.base[:, bslot, :], xi, st.xf[:, :]),
            reads=[b_x, st.b_xf], writes=[st.b_base[bslot]])
        R.dma("sync", out_hbm[ti * 128:(ti + 1) * 128, :], st.base[:, bslot, :],
              c_out, reads=[st.b_base[bslot]], writes=[b_out[ti]])
        R.op("scalar", lambda e: e.activation(
            out=lnst.sq[:], in_=st.xf[:, :], func=AF.Square, accum_out=lnst.st[:, 1:2]),
            reads=[st.b_xf], writes=[lnst.b_sq, lnst.b_st])
        R.op("vector", lambda e: e.reduce_sum(
            lnst.st[:, 0:1], st.xf[:, :], axis=AX.X), reads=[st.b_xf], writes=[lnst.b_st])
        R.op("vector", lambda e: e.tensor_scalar_mul(
            lnst.st[:, 2:3], lnst.st[:, 0:1], 1.0 / 1024),
            reads=[lnst.b_st], writes=[lnst.b_st])
        R.op("vector", lambda e: e.tensor_mul(
            lnst.st[:, 3:4], lnst.st[:, 2:3], lnst.st[:, 2:3]),
            reads=[lnst.b_st], writes=[lnst.b_st])
        R.op("vector", lambda e: e.scalar_tensor_tensor(
            out=lnst.st[:, 3:4], in0=lnst.st[:, 1:2], scalar=1.0 / 1024,
            in1=lnst.st[:, 3:4], op0=ALU.mult, op1=ALU.subtract),
            reads=[lnst.b_st], writes=[lnst.b_st])
        R.op("scalar", lambda e: e.activation(
            out=lnst.st[:, 4:5], in_=lnst.st[:, 3:4], func=AF.Sqrt, bias=1e-5),
            reads=[lnst.b_st, C.b_cpool], writes=[lnst.b_st])
        R.op("vector", lambda e: e.reciprocal(lnst.st[:, 5:6], lnst.st[:, 4:5]),
             reads=[lnst.b_st], writes=[lnst.b_st])
        R.op("vector", lambda e: e.tensor_scalar(
            out=st.fhat[:, :], in0=st.xf[:, :], scalar1=lnst.st[:, 2:3],
            scalar2=lnst.st[:, 5:6], op0=ALU.subtract, op1=ALU.mult),
            reads=[st.b_xf, lnst.b_st], writes=[st.b_fhat])
        for c in range(8):
            pt = C.ps_tr[:, c % 2, 0:128]
            R.op("tensor", lambda e, c=c, pt=pt: e.transpose(
                pt, st.fhat[:, c * 128:(c + 1) * 128], C.ident[:]),
                reads=[st.b_fhat, C.b_ident], writes=[C.b_ptr[c % 2]])
            R.op("vector", lambda e, c=c, pt=pt: e.tensor_copy(
                st.fhT[:, c, :], pt),
                reads=[C.b_ptr[c % 2]], writes=[st.b_fhT])
        R.dma("sync", fhat_d[:, :, ti * 128:(ti + 1) * 128], st.fhT[:, :, :],
              st.c_f, reads=[st.b_fhT], writes=[b_fd])
        for c in range(8):
            pt = C.ps_big[:, c % 2, 0:128]
            R.op("tensor", lambda e, c=c, pt=pt: e.transpose(
                pt, st.xf[:, c * 128:(c + 1) * 128], C.idf[:, :]),
                reads=[st.b_xf, C.b_idf], writes=[C.b_big[c % 2]])
            R.op("vector", lambda e, c=c, pt=pt: e.tensor_copy(
                st.ftT[:, c, :], pt), reads=[C.b_big[c % 2]], writes=[st.b_ftT])
        def mm_s(e):
            last = None
            for c in range(8):
                last = e.matmul(C.ps_ab[:, 0, 0:8], st.ftT[:, c, :],
                                st.cent[:, c, :], start=(c == 0), stop=(c == 7))
            return last
        R.op("tensor", mm_s, reads=[st.b_ftT, st.b_cent], writes=[C.b_ab[0]])
        R.op("vector", lambda e: e.tensor_copy(st.sc[:, 0:8], C.ps_ab[:, 0, 0:8]),
             reads=[C.b_ab[0]], writes=[st.b_sc])
        R.op("vector", lambda e: e.reduce_max(
            st.sc[:, 8:9], st.sc[:, 0:8], axis=AX.X), reads=[st.b_sc], writes=[st.b_sc])
        R.op("vector", lambda e: e.tensor_scalar(
            out=st.sc[:, 16:24], in0=st.sc[:, 0:8], scalar1=st.sc[:, 8:9],
            scalar2=None, op0=ALU.is_equal), reads=[st.b_sc], writes=[st.b_sc])
        R.op("vector", lambda e: e.tensor_tensor_scan(
            out=st.sc[:, 24:32], data0=st.sc[:, 16:24], data1=st.sc[:, 32:40],
            initial=0.0, op0=ALU.add, op1=ALU.add),
            reads=[st.b_sc], writes=[st.b_sc])
        R.op("vector", lambda e: e.tensor_scalar(
            out=st.sc[:, 24:32], in0=st.sc[:, 24:32], scalar1=1.0, scalar2=None,
            op0=ALU.is_equal), reads=[st.b_sc], writes=[st.b_sc])
        R.op("vector", lambda e: e.tensor_mul(
            st.sc[:, 16:24], st.sc[:, 16:24], st.sc[:, 24:32]),
            reads=[st.b_sc], writes=[st.b_sc])
        R.op("scalar", lambda e: e.activation(
            out=st.sc[:, 9:10], in_=st.sc[:, 8:9], func=AF.Sigmoid),
            reads=[st.b_sc], writes=[st.b_sc])
        R.op("vector", lambda e, ti=ti: e.tensor_scalar(
            out=aE[:, ti, :], in0=st.sc[:, 16:24], scalar1=st.sc[:, 9:10],
            scalar2=None, op0=ALU.mult), reads=[st.b_sc, b_aE], writes=[b_aE])


def emit_l1ffn_main(R: Rec, C, st, W1B, W2B, b_wbf, out_hbm, b_out, fhat_d, b_fd,
                    aE, b_aE):
    ps_y = [C.ps_big[:, 0, :], C.ps_big[:, 1, :], C.ps_big[:, 2, :], C.ps_big[:, 3, :],
            C.ps_ab[:, 0, :], C.ps_ab[:, 1, :],
            C.ps_tr[:, 0, :].bitcast(F32), C.ps_tr[:, 1, :].bitcast(F32)]
    b_y = [C.b_big[0], C.b_big[1], C.b_big[2], C.b_big[3],
           C.b_ab[0], C.b_ab[1], C.b_ptr[0], C.b_ptr[1]]

    def load_w1(e, nt):
        slot = nt % 3
        R.dma("sync", st.w1[:, slot, :, :],
              W1B[e][nt * 128:(nt + 1) * 128, :].rearrange("k (c n) -> k c n", c=8),
              st.c_w1[slot], reads=[b_wbf], writes=[st.b_w1[slot]])

    def load_w2(e, nt, dh):
        slot = nt % 8
        R.dma("scalar", st.w2[:, slot, :],
              W2B[e][nt * 128:(nt + 1) * 128, dh * 512:(dh + 1) * 512],
              st.c_w2[slot], reads=[b_wbf], writes=[st.b_w2[slot]])

    def emit_fslhe(e, tg):
        R.dma("gpsimd", st.fsl[:, :, :], fhat_d[:, :, tg * 1024:(tg + 1) * 1024],
              st.c_fsl, reads=[b_fd, st.b_fsl], writes=[st.b_fsl])
        for c in range(8):
            R.op("vector", lambda en, c=c, e=e: en.tensor_scalar(
                out=st.he[:, c, :], in0=st.fsl[:, c, :],
                scalar1=st.gb[:, 0, c, e:e + 1], scalar2=st.gb[:, 1, c, e:e + 1],
                op0=ALU.mult, op1=ALU.add),
                reads=[st.b_fsl, st.b_gb], writes=[st.b_he])

    for k in range(3):
        load_w1(0, k)
    for k in range(8):
        load_w2(0, k, 0)
    emit_fslhe(0, 0)
    for e in range(8):
        for tg in range(4):
            for nt in range(32):
                slot = nt % 3
                for tc in range(2):
                    pi = (nt * 2 + tc) % 4
                    ps = ps_y[pi]
                    b_ps = b_y[pi]

                    def mm1(en, slot=slot, tc=tc, ps=ps):
                        last = None
                        for cc in range(8):
                            last = en.matmul(ps, st.w1[:, slot, cc, :],
                                             st.he[:, cc, tc * 512:(tc + 1) * 512],
                                             start=(cc == 0), stop=(cc == 7))
                        return last
                    R.op("tensor", mm1, reads=[st.b_w1[slot], st.b_he], writes=[b_ps])
                    R.op("vector", lambda en, nt=nt, tc=tc, e=e, ps=ps: en.tensor_scalar(
                        out=st.hrT[:, nt, tc * 512:(tc + 1) * 512], in0=ps,
                        scalar1=st.b1[:, nt, e:e + 1], scalar2=0.0,
                        op0=ALU.add, op1=ALU.max),
                        reads=[b_ps, st.b_b1], writes=[st.b_hrT])
                if nt + 3 < 32:
                    load_w1(e, nt + 3)
            for dh in range(2):
                for nt in range(32):
                    slot = nt % 8
                    for tt in range(8):
                        def mm2(en, nt=nt, slot=slot, tt=tt, e=e, dh=dh):
                            ins = en.matmul(ps_y[tt], st.hrT[:, nt, tt * 128:(tt + 1) * 128],
                                            st.w2[:, slot, :],
                                            start=(nt == 0), stop=False)
                            if nt == 31:
                                ins = en.matmul(ps_y[tt],
                                                C.ones1[0:1, 0:128],
                                                st.b2r[0:1, e, dh * 512:(dh + 1) * 512],
                                                start=False, stop=True)
                            return ins
                        R.op("tensor", mm2,
                             reads=[st.b_hrT, st.b_w2[slot], C.b_ones1, st.b_b2r],
                             writes=[b_y[tt]])
                    if nt + 8 < 32:
                        load_w2(e, nt + 8, dh)
                    elif dh == 0 and nt >= 24:
                        load_w2(e, nt - 24, 1)
                ne, ntg = (e, tg + 1) if tg < 3 else (e + 1, 0)
                if dh == 0:
                    if ne < 8:
                        emit_fslhe(ne, ntg)
                if dh == 1:
                    if ne < 8:
                        for k in range(3):
                            load_w1(ne, k)
                        for k in range(8):
                            load_w2(ne, k, 0)
                for tt in range(8):
                    ti = tg * 8 + tt
                    R.op("scalar", lambda en, tt=tt, ti=ti, e=e:
                         en.activation(
                             out=st.stage[:, tt, :],
                             in_=ps_y[tt], func=AF.Copy,
                             scale=aE[:, ti, e:e + 1]),
                         reads=[b_y[tt], b_aE], writes=[st.b_stage[tt]])
                    R.dma("gpsimd",
                          out_hbm[ti * 128:(ti + 1) * 128, dh * 512:(dh + 1) * 512],
                          st.stage[:, tt, :],
                          st.c_acc, accum_op=ALU.add,
                          reads=[st.b_stage[tt]], writes=[b_out[ti]])



def l0attn_state(R: Rec, name):
    st = NS()
    st.q = R.sbuf(name + "_q", [128, 8, 2048], BF16)
    st.k = R.sbuf(name + "_k", [128, 8, 2048], BF16)
    st.v = R.sbuf(name + "_v", [128, 8, 2048], BF16)
    st.gstg = R.sbuf(name + "_gstg", [128, 2, 512], BF16)
    st.b_q, st.b_k = Buf(name + "q"), Buf(name + "k")
    st.b_v, st.b_g = Buf(name + "v"), Buf(name + "gg")
    st.c_g = R.ctr(name + "_cg", 16)
    st.rowsA = R.sbuf(name + "_rowsA", [96, 2048], F32)
    st.rowsB = R.sbuf(name + "_rowsB", [96, 2048], F32)
    st.b_rA, st.b_rB = Buf(name + "rA"), Buf(name + "rB")
    st.cols = R.sbuf(name + "_cols", [128, 2, 80], F32)
    st.b_cols = Buf(name + "cols")
    st.fscr = R.sbuf(name + "_fscr", [128, 1024], F32)
    st.b_fscr = Buf(name + "fscr")
    st.gA = st.fscr[:, 512:1024].rearrange("p (a b) -> p a b", b=64)
    st.betA = R.sbuf(name + "_betA", [128, 8, 64], F32)
    st.egA = R.sbuf(name + "_egA", [128, 8, 64], BF16)
    st.b_gA, st.b_betA, st.b_egA = st.b_fscr, Buf(name + "bA"), Buf(name + "eA")
    st.c_bc = R.ctr(name + "_cbc", 16)
    st.LTf = st.fscr[:, 0:512].rearrange("p (a b) -> p a b", b=64)
    st.b_LTf = st.b_fscr
    st.LT = R.sbuf(name + "_LT", [128, 3, 8, 64], BF16)
    st.b_LT = Buf(name + "LT")
    st.mask = R.sbuf(name + "_mask", [128, 3, 64], BF16)
    st.b_mask = Buf(name + "mask")
    st.P = R.sbuf(name + "_P", [128, 2, 8, 64], BF16)
    st.b_P = Buf(name + "P")
    st.pool = R.sbuf(name + "_pool", [128, 8, 576], BF16)
    st.b_pool = Buf(name + "pool")
    st.w = st.pool[:, :, 0:512]
    st.b_w = st.b_pool
    st.U = st.pool[:, :, 0:128]
    st.b_U = st.b_pool
    st.B = st.pool[:, :, 128:256]
    st.b_B = st.b_pool
    st.attnT = st.pool[:, :, 256:320]
    st.b_attnT = st.b_pool
    st.kb = st.pool[:, :, 320:384]
    st.b_kb = st.b_pool
    st.kd = st.pool[:, :, 384:448]
    st.b_kd = st.b_pool
    st.kcdT = st.pool[:, :, 448:512]
    st.b_kcdT = st.b_pool
    st.vn = st.pool[:, :, 512:576]
    st.b_vn = st.b_pool
    st.S32 = R.sbuf(name + "_S32", [128, 8, 64], F32)
    st.b_S32 = Buf(name + "S32")
    st.Sbf = R.sbuf(name + "_Sbf", [128, 8, 64], BF16)
    st.b_Sbf = Buf(name + "Sbf")
    st.c_w = R.ctr(name + "_cw", 16)
    st.q2 = R.sbuf(name + "_q2", [128, 1024], BF16)
    st.b_q2 = Buf(name + "q2")
    st.nrm = None
    st.b_nrm = None
    st.BO = R.sbuf(name + "_BO", [128, 8, 16], BF16)
    st.b_BO = Buf(name + "BO")
    st.EX = R.sbuf(name + "_EX", [16, 8, 128], BF16)
    st.b_EX = Buf(name + "EX")
    st.nrm = st.rowsA[0:16, :].bitcast(BF16)[:, 0:2048]
    st.b_nrm = st.b_rA
    st.smalls = R.sbuf(name + "_smalls", [128, 4], F32)
    st.b_smalls = Buf(name + "smalls")
    st.c_sm = R.ctr(name + "_csm", 16)
    st.ln = ln_state(R, name + "_ln", sq=st.q2[:, :], b_sq=st.b_q2,
                     xln=st.v[:, 0, 0:1024].rearrange("p (a b) -> p a b", a=1),
                     b_xlnb=st.b_v)
    return st


def emit_l0attn_setup(R: Rec, C, st):
    scrf = C.idf[:, 0:32]
    for c in range(8):
        R.op("gpsimd", lambda e, c=c: e.affine_select(
            out=scrf[:, 0:16], in_=C.onesf[:, 0:16], pattern=[[-64, 16]],
            compare_op=ALU.is_ge, fill=0.0, base=128 * c, channel_multiplier=1),
            reads=[C.b_onesf, C.b_idf], writes=[C.b_idf])
        R.op("gpsimd", lambda e, c=c: e.affine_select(
            out=scrf[:, 16:32], in_=scrf[:, 0:16], pattern=[[64, 16]],
            compare_op=ALU.is_ge, fill=0.0, base=63 - 128 * c, channel_multiplier=-1),
            reads=[C.b_idf], writes=[C.b_idf])
        R.op("vector", lambda e, c=c: e.tensor_copy(st.BO[:, c, :], scrf[:, 16:32]),
             reads=[C.b_idf], writes=[st.b_BO])
    s16 = C.idf[0:16, :]
    for c in range(8):
        R.op("gpsimd", lambda e, c=c: e.affine_select(
            out=s16[:, :], in_=C.onesf[0:16, :], pattern=[[1, 128]],
            compare_op=ALU.is_ge, fill=0.0, base=128 * c, channel_multiplier=-64),
            reads=[C.b_onesf, C.b_idf], writes=[C.b_idf])
        R.op("gpsimd", lambda e, c=c: e.affine_select(
            out=s16[:, :], in_=s16[:, :], pattern=[[-1, 128]],
            compare_op=ALU.is_ge, fill=0.0, base=63 - 128 * c, channel_multiplier=64),
            reads=[C.b_idf], writes=[C.b_idf])
        R.op("vector", lambda e, c=c: e.tensor_copy(st.EX[:, c, :], s16[:, :]),
             reads=[C.b_idf], writes=[st.b_EX])
    for mi, (cm_sign, b0) in enumerate([(1, 0), (1, 1), (-1, 1)]):
        for half in range(2):
            out = C.idf[half * 64:(half + 1) * 64, 0:64]
            R.op("gpsimd", lambda e, cm_sign=cm_sign, b0=b0, half=half, out=out:
                 e.affine_select(
                     out=out, in_=C.onesf[half * 64:(half + 1) * 64, 0:64],
                     pattern=[[cm_sign, 64]], compare_op=ALU.is_ge, fill=0.0,
                     base=-b0, channel_multiplier=-cm_sign),
                 reads=[C.b_onesf, C.b_idf], writes=[C.b_idf])
        R.op("vector", lambda e, mi=mi: e.tensor_copy(st.mask[:, mi, :], C.idf[:, 0:64]),
             reads=[C.b_idf], writes=[st.b_mask])


def emit_l0attn(R: Rec, C, st, x_ap_fn, b_xsrc, xadd_fn, o_fm, b_o, xlnt, b_xlnt,
                Wq, Wk, Wv, Wg, Wo, Wb, Wa, dt_bias, A_log, norm_w,
                g_dram, beta_dram, b_gd, gate_d, b_gt, dump=None, n_chunks=32,
                hook=None):
    nc = R.nc
    gR = st.rowsA[0:16, :]
    betaR = st.rowsA[32:48, :]
    eglgR = st.rowsA[64:80, :]
    sbgR = st.rowsB[0:16, :]
    eglR = st.rowsB[32:48, :]
    arawR = st.rowsB[64:80, :]
    b_rA, b_rB = st.b_rA, st.b_rB

    R.op("gpsimd", lambda e: e.affine_select(
        out=C.idf[:, :], in_=C.onesf[:, :], pattern=[[-1, 128]],
        compare_op=ALU.is_equal, fill=0.0, base=0, channel_multiplier=1),
        reads=[C.b_onesf, C.b_idf], writes=[C.b_idf])

    R.dma("sync", st.smalls[0:16, 0:1], dt_bias[:, None], st.c_sm,
          reads=[st.b_smalls], writes=[st.b_smalls], allow_slow_non_contiguous=True)
    R.dma("sync", st.smalls[0:16, 1:2], A_log[:, None], st.c_sm,
          reads=[st.b_smalls], writes=[st.b_smalls], allow_slow_non_contiguous=True)
    R.dma("sync", st.smalls[0:64, 2:3], norm_w[:, None], st.c_sm,
          reads=[st.b_smalls], writes=[st.b_smalls], allow_slow_non_contiguous=True)
    R.dma("sync", st.smalls[64:128, 2:3], norm_w[:, None], st.c_sm,
          reads=[st.b_smalls], writes=[st.b_smalls], allow_slow_non_contiguous=True)
    R.op("scalar", lambda e: e.activation(
        out=st.smalls[0:16, 3:4], in_=st.smalls[0:16, 1:2], func=AF.Exp),
        reads=[st.b_smalls], writes=[st.b_smalls])
    R.op("vector", lambda e: e.tensor_scalar_mul(
        st.smalls[0:16, 3:4], st.smalls[0:16, 3:4], -1.0),
        reads=[st.b_smalls], writes=[st.b_smalls])

    emit_ln_xlnt(R, C, st.ln, x_ap_fn, b_xsrc, xlnt, b_xlnt, 16)

    def loadW(W, col0, ncol=512):
        R.dma("gpsimd", st.w[:, :, 0:ncol],
              W[:, col0:col0 + ncol].rearrange("(c k) n -> k c n", k=128),
              st.c_w, reads=[st.b_w], writes=[st.b_w])

    for (W, dst, b_dst) in [(Wq, st.q, st.b_q), (Wk, st.k, st.b_k),
                            (Wv, st.v, st.b_v), (Wg, None, st.b_g)]:
        for hg in range(2):
            loadW(W, hg * 512)
            for tc in range(4):
                for c in range(4):
                    cg = hg * 4 + c
                    ps = C.ps_ab[:, c % 2, :]
                    b_ps = C.b_ab[c % 2]

                    def mm_p(e, c=c, tc=tc, ps=ps):
                        last = None
                        for cc in range(8):
                            last = e.matmul(ps, st.w[:, cc, c * 128:(c + 1) * 128],
                                            xlnt[:, cc, tc * 512:(tc + 1) * 512],
                                            start=(cc == 0), stop=(cc == 7))
                        return last
                    R.op("tensor", mm_p, reads=[st.b_w, b_xlnt], writes=[b_ps])
                    if dst is None:
                        gs = (tc + cg) % 2
                        R.op("vector", lambda e, gs=gs, ps=ps: e.tensor_copy(
                            st.gstg[:, gs, :], ps),
                            reads=[b_ps], writes=[st.b_g])
                        R.dma("scalar", gate_d[:, cg, tc * 512:(tc + 1) * 512],
                              st.gstg[:, gs, :], st.c_g,
                              reads=[st.b_g], writes=[b_gt])
                    else:
                        R.op("vector", lambda e, tc=tc, cg=cg, dst=dst, ps=ps:
                             e.tensor_copy(dst[:, cg, tc * 512:(tc + 1) * 512], ps),
                             reads=[b_ps], writes=[b_dst])

    for (W, outrow, b_or) in [(Wb, st.rowsB[64:80, :], b_rB),
                              (Wa, st.rowsA[64:80, :], b_rA)]:
        R.dma("gpsimd", st.w[:, :, 0:16],
              W.rearrange("(c k) n -> k c n", k=128), st.c_w,
              reads=[st.b_w], writes=[st.b_w])
        for tc in range(4):
            ps = C.ps_ab[0:16, 0, :]

            def mm_ba(e, tc=tc, ps=ps):
                last = None
                for cc in range(8):
                    last = e.matmul(ps, st.w[:, cc, 0:16],
                                    xlnt[:, cc, tc * 512:(tc + 1) * 512],
                                    start=(cc == 0), stop=(cc == 7))
                return last
            R.op("tensor", mm_ba, reads=[st.b_w, b_xlnt], writes=[C.b_ab[0]])
            R.op("vector", lambda e, tc=tc, outrow=outrow, ps=ps: e.tensor_copy(
                outrow[:, tc * 512:(tc + 1) * 512], ps),
                reads=[C.b_ab[0]], writes=[b_or])

    for iqk, (dst, b_dst) in enumerate([(st.q, st.b_q), (st.k, st.b_k)]):
        nfi = st.nrm[:, :]
        for tc in range(4):
            for cg in range(8):
                R.op("scalar", lambda e, tc=tc, cg=cg, dst=dst: e.activation(
                    out=st.q2[:, 0:512], in_=dst[:, cg, tc * 512:(tc + 1) * 512],
                    func=AF.Square), reads=[b_dst], writes=[st.b_q2])
                R.op("tensor", lambda e, cg=cg, tc=tc: e.matmul(
                    C.ps_big[0:16, tc, :], st.BO[:, cg, :], st.q2[:, 0:512],
                    start=(cg == 0), stop=(cg == 7)),
                    reads=[st.b_BO, st.b_q2], writes=[C.b_big[tc]])
            R.op("vector", lambda e, tc=tc: e.tensor_copy(
                st.nrm[:, tc * 512:(tc + 1) * 512], C.ps_big[0:16, tc, :]),
                reads=[C.b_big[tc]], writes=[st.b_nrm])
        R.op("vector", lambda e, nfi=nfi: e.tensor_scalar_add(nfi, nfi, 1e-6),
             reads=[st.b_nrm], writes=[st.b_nrm])
        R.op("scalar", lambda e, nfi=nfi: e.activation(out=nfi, in_=nfi, func=AF.Ln),
             reads=[st.b_nrm, C.b_cpool], writes=[st.b_nrm])
        R.op("scalar", lambda e, nfi=nfi: e.activation(
            out=nfi, in_=nfi, func=AF.Exp, scale=C.const(-0.5)),
            reads=[st.b_nrm, C.b_cpool], writes=[st.b_nrm])
        if iqk == 0:
            R.op("vector", lambda e, nfi=nfi: e.tensor_scalar_mul(nfi, nfi, 0.125),
                 reads=[st.b_nrm], writes=[st.b_nrm])
        for cg in range(8):
            for tc in range(4):
                ps = C.ps_ab[:, iqk, :]
                R.op("tensor", lambda e, cg=cg, tc=tc, ps=ps: e.matmul(
                    ps, st.EX[:, cg, :], st.nrm[:, tc * 512:(tc + 1) * 512],
                    start=True, stop=True),
                    reads=[st.b_EX, st.b_nrm], writes=[C.b_ab[iqk]])
                R.op("vector", lambda e, cg=cg, tc=tc, dst=dst, ps=ps: e.tensor_tensor(
                    out=dst[:, cg, tc * 512:(tc + 1) * 512],
                    in0=dst[:, cg, tc * 512:(tc + 1) * 512], in1=ps, op=ALU.mult),
                    reads=[C.b_ab[iqk], b_dst], writes=[b_dst])

    R.op("scalar", lambda e: e.activation(out=betaR, in_=st.rowsB[64:80, :],
                                          func=AF.Sigmoid),
         reads=[b_rB], writes=[b_rA])
    R.op("vector", lambda e: e.tensor_scalar(
        out=st.rowsA[64:80, :], in0=st.rowsA[64:80, :],
        scalar1=st.smalls[0:16, 0:1], scalar2=None, op0=ALU.add),
        reads=[b_rA, st.b_smalls], writes=[b_rA])
    R.op("scalar", lambda e: e.activation(out=arawR, in_=st.rowsA[64:80, :], func=AF.Abs),
         reads=[b_rA], writes=[b_rB])
    R.op("scalar", lambda e: e.activation(out=arawR, in_=arawR, func=AF.Exp, scale=-1.0),
         reads=[b_rB], writes=[b_rB])
    R.op("scalar", lambda e: e.activation(out=arawR, in_=arawR, func=AF.Ln, bias=1.0),
         reads=[b_rB], writes=[b_rB])
    R.op("vector", lambda e: e.tensor_scalar_max(
        st.rowsA[64:80, :], st.rowsA[64:80, :], 0.0),
        reads=[b_rA], writes=[b_rA])
    R.op("vector", lambda e: e.tensor_add(arawR, arawR, st.rowsA[64:80, :]),
         reads=[b_rA, b_rB], writes=[b_rB])
    R.op("vector", lambda e: e.tensor_scalar(
        out=arawR, in0=arawR, scalar1=st.smalls[0:16, 3:4], scalar2=None,
        op0=ALU.mult), reads=[b_rB, st.b_smalls], writes=[b_rB])
    R.op("vector", lambda e: e.tensor_tensor_scan(
        out=eglgR, data0=arawR, data1=arawR, initial=0.0,
        op0=ALU.add, op1=ALU.bypass),
        reads=[b_rB], writes=[b_rA])
    R.op("vector", lambda e: e.tensor_copy(gR, eglgR), reads=[b_rA], writes=[b_rA])
    R.op("vector", lambda e: e.memset(sbgR[:, 0:1], 0.0),
         reads=[b_rB], writes=[b_rB])
    R.op("vector", lambda e: e.tensor_copy(sbgR[:, 1:32], eglgR[:, 63:2047:64]),
         reads=[b_rA, b_rB], writes=[b_rB])
    R.op("vector", lambda e: e.tensor_tensor(
        out=gR.rearrange("p (a b) -> p a b", b=64),
        in0=gR.rearrange("p (a b) -> p a b", b=64),
        in1=sbgR[:, 0:32].unsqueeze(2).broadcast_to([16, 32, 64]),
        op=ALU.subtract), reads=[b_rA, b_rB], writes=[b_rA])
    R.op("vector", lambda e: e.tensor_tensor(
        out=eglgR.rearrange("p (a b) -> p a b", b=64),
        in0=gR[:, 63:2048:64].unsqueeze(2).broadcast_to([16, 32, 64]),
        in1=gR.rearrange("p (a b) -> p a b", b=64), op=ALU.subtract),
        reads=[b_rA], writes=[b_rA])
    R.op("scalar", lambda e: e.activation(out=eglgR, in_=eglgR, func=AF.Exp),
         reads=[b_rA], writes=[b_rA])
    R.op("scalar", lambda e: e.activation(out=eglR, in_=gR, func=AF.Exp),
         reads=[b_rA], writes=[b_rB])
    R.op("vector", lambda e: e.tensor_mul(sbgR, eglR, betaR),
         reads=[b_rA, b_rB], writes=[b_rB])
    R.op("vector", lambda e: e.tensor_copy(
        eglR.rearrange("p (a b) -> p a b", b=64),
        gR[:, 63:2048:64].unsqueeze(2).broadcast_to([16, 32, 64])),
        reads=[b_rA], writes=[b_rB])
    R.op("scalar", lambda e: e.activation(out=eglR, in_=eglR, func=AF.Exp),
         reads=[b_rB], writes=[b_rB])
    R.dma("sync", g_dram[:, :], gR, st.c_sm, reads=[b_rA], writes=[b_gd])
    R.dma("sync", beta_dram[:, :], betaR, st.c_sm, reads=[b_rA], writes=[b_gd])

    if dump is not None:
        dl = [("d_q", st.q[:, :, :], st.b_q), ("d_k", st.k[:, :, :], st.b_k),
              ("d_v", st.v[:, :, :], st.b_v)]
        for r0 in (0, 32, 64):
            dl.append((("d_rA", r0), st.rowsA[r0:r0 + 16, :], st.b_rA))
            dl.append((("d_rB", r0), st.rowsB[r0:r0 + 16, :], st.b_rB))
        for key, ap, b in dl:
            if isinstance(key, tuple):
                if key[0] in dump:
                    R.dma("sync", dump[key[0]][key[1]:key[1] + 16, :], ap, dump["c"],
                          reads=[b], writes=[dump["b"]])
            elif key in dump:
                R.dma("sync", dump[key], ap, dump["c"], reads=[b], writes=[dump["b"]])
    R.op("vector", lambda e: e.memset(st.S32[:, :, :], 0.0),
         reads=[st.b_S32], writes=[st.b_S32])
    R.op("vector", lambda e: e.memset(st.Sbf[:, :, :], 0.0),
         reads=[st.b_Sbf], writes=[st.b_Sbf])

    varrows = [(st.rowsA, 0, b_rA), (st.rowsB, 0, b_rB), (st.rowsA, 32, b_rA),
               (st.rowsB, 32, b_rB), (st.rowsA, 64, b_rA)]
    VAR_G, VAR_SBG, VAR_BETA, VAR_EGL, VAR_EGLG = 0, 1, 2, 3, 4

    def colsel(var, par2, cs):
        return st.cols[par2 * 64:(par2 + 1) * 64, cs, var * 16 + par2:var * 16 + 16:2]

    def phase_A(ci):
        cb = ci * 64
        cs = ci % 2
        lvl = globals().get("L0A_LVL", 99)
        for half in range(2):
            R.dma("scalar", st.gA[half * 64:(half + 1) * 64, :, :],
                  g_dram[half::2, cb:cb + 64].unsqueeze(0)
                  .partition_broadcast(64).squeeze(1),
                  st.c_bc, reads=[b_gd, st.b_gA], writes=[st.b_gA])
            R.dma("scalar", st.betA[half * 64:(half + 1) * 64, :, :],
                  beta_dram[half::2, cb:cb + 64].unsqueeze(0)
                  .partition_broadcast(64).squeeze(1),
                  st.c_bc, reads=[b_gd, st.b_betA], writes=[st.b_betA])
        R.op("scalar", lambda e: e.activation(
            out=st.egA[:, :, :], in_=st.gA[:, :, :], func=AF.Exp),
            reads=[st.b_gA], writes=[st.b_egA])
        if lvl < 2: return
        vbank = {0: 1, 32: 2, 64: 3}
        vcol = [0, 16, 0, 16, 0]
        for vi, (rt, r0, b_rt) in enumerate(varrows):
            R.op("tensor", lambda e, vi=vi, rt=rt, r0=r0, cb=cb:
                 e.transpose(
                     C.ps_big[0:64, vbank[r0], vcol[vi]:vcol[vi] + 16],
                     rt[r0:r0 + 16, cb:cb + 64],
                     C.idf[r0:r0 + 16, r0:r0 + 16],
                     tile_position=(r0, 0)),
                 reads=[b_rt, C.b_idf], writes=[C.b_big[vbank[r0]]])
        for half in range(2):
            hs = slice(half * 64, (half + 1) * 64)
            R.op("vector", lambda e, cs=cs, hs=hs: e.tensor_copy(
                st.cols[hs, cs, 0:32], C.ps_big[0:64, 1, 0:32]),
                reads=[C.b_big[1]], writes=[st.b_cols])
            R.op("vector", lambda e, cs=cs, hs=hs: e.tensor_copy(
                st.cols[hs, cs, 32:64], C.ps_big[0:64, 2, 0:32]),
                reads=[C.b_big[2]], writes=[st.b_cols])
            R.op("vector", lambda e, cs=cs, hs=hs: e.tensor_copy(
                st.cols[hs, cs, 64:80], C.ps_big[0:64, 3, 0:16]),
                reads=[C.b_big[3]], writes=[st.b_cols])
        R.op("vector", lambda e, cb=cb: e.tensor_tensor(
            out=st.kb[:, :, :], in0=st.k[:, :, cb:cb + 64], in1=st.betA[:, :, :],
            op=ALU.mult), reads=[st.b_k, st.b_betA], writes=[st.b_kb])
        if lvl < 3: return
        for par2 in range(2):
            sel = colsel(0, par2, cs)
            R.op("vector", lambda e, par2=par2, sel=sel: e.tensor_tensor(
                out=st.LTf[par2 * 64:(par2 + 1) * 64, :, :],
                in0=st.gA[par2 * 64:(par2 + 1) * 64, :, :],
                in1=sel.unsqueeze(2).broadcast_to([64, 8, 64]),
                op=ALU.subtract),
                reads=[st.b_gA, st.b_cols], writes=[st.b_LTf])
        R.op("vector", lambda e: e.tensor_scalar_min(
            st.betA[:, :, :], st.LTf[:, :, :], 0.0),
            reads=[st.b_LTf, st.b_kb], writes=[st.b_betA])
        R.op("scalar", lambda e: e.activation(
            out=st.LT[:, 0, :, :], in_=st.betA[:, :, :], func=AF.Exp),
            reads=[st.b_betA], writes=[st.b_LT])
        R.op("vector", lambda e: e.tensor_tensor(
            out=st.LT[:, 1, :, :], in0=st.LT[:, 0, :, :],
            in1=st.mask[:, 1, :].unsqueeze(1).broadcast_to([128, 8, 64]),
            op=ALU.mult), reads=[st.b_LT, st.b_mask], writes=[st.b_LT])
        R.op("vector", lambda e: e.tensor_tensor(
            out=st.LT[:, 0, :, :], in0=st.LT[:, 0, :, :],
            in1=st.mask[:, 0, :].unsqueeze(1).broadcast_to([128, 8, 64]),
            op=ALU.mult), reads=[st.b_LT, st.b_mask], writes=[st.b_LT])
        R.op("vector", lambda e: e.tensor_scalar_max(
            st.betA[:, :, :], st.LTf[:, :, :], 0.0),
            reads=[st.b_LTf], writes=[st.b_betA])
        R.op("scalar", lambda e: e.activation(
            out=st.LT[:, 2, :, :], in_=st.betA[:, :, :], func=AF.Exp, scale=-1.0),
            reads=[st.b_betA], writes=[st.b_LT])
        R.op("vector", lambda e: e.tensor_tensor(
            out=st.LT[:, 2, :, :], in0=st.LT[:, 2, :, :],
            in1=st.mask[:, 2, :].unsqueeze(1).broadcast_to([128, 8, 64]),
            op=ALU.mult), reads=[st.b_LT, st.b_mask], writes=[st.b_LT])
        if lvl < 4: return
        for kind, (lhs, b_lhs, rhs, b_rhs, bank) in enumerate([
                (st.kb, st.b_kb, None, st.b_k, 0),
                (None, st.b_k, st.kb, st.b_kb, 1),
                (None, st.b_k, None, st.b_q, 2)]):
            def mm_3(e, kind=kind, cb=cb, bank=bank):
                last = None
                for sp in range(8):
                    for par in range(2):
                        sl = slice(par * 64, (par + 1) * 64)
                        if kind == 0:
                            lt = st.kb[sl, sp, :]
                            rt = st.k[sl, sp, cb:cb + 64]
                        elif kind == 1:
                            lt = st.k[sl, sp, cb:cb + 64]
                            rt = st.kb[sl, sp, :]
                        else:
                            lt = st.k[sl, sp, cb:cb + 64]
                            rt = st.q[sl, sp, cb:cb + 64]
                        last = e.matmul(
                            C.ps_big[sl, bank, sp * 64:(sp + 1) * 64], lt, rt,
                            start=True, stop=True, tile_position=(par * 64, par * 64))
                return last
            R.op("tensor", mm_3, reads=[st.b_kb, st.b_k, st.b_q],
                 writes=[C.b_big[bank]])
        R.op("vector", lambda e, cb=cb: e.tensor_tensor(
            out=st.q[:, :, cb:cb + 64], in0=st.q[:, :, cb:cb + 64],
            in1=st.egA[:, :, :], op=ALU.mult),
            reads=[st.b_q, st.b_egA], writes=[st.b_q])
        R.op("vector", lambda e: e.tensor_tensor(
            out=st.P[:, 0, :, :],
            in0=C.ps_big[:, 0, :].rearrange("p (a b) -> p a b", b=64),
            in1=st.LT[:, 2, :, :], op=ALU.mult),
            reads=[C.b_big[0], st.b_LT], writes=[st.b_P])
        R.op("vector", lambda e: e.tensor_tensor(
            out=st.P[:, 1, :, :],
            in0=C.ps_big[:, 1, :].rearrange("p (a b) -> p a b", b=64),
            in1=st.LT[:, 1, :, :], op=ALU.mult),
            reads=[C.b_big[1], st.b_LT], writes=[st.b_P])
        R.op("vector", lambda e: e.tensor_tensor(
            out=st.attnT[:, :, :],
            in0=C.ps_big[:, 2, :].rearrange("p (a b) -> p a b", b=64),
            in1=st.LT[:, 0, :, :], op=ALU.mult),
            reads=[C.b_big[2], st.b_LT], writes=[st.b_attnT])
        if lvl < 5: return
        def mm_ktr(e, cb=cb):
            last = None
            for sp in range(8):
                for par in range(2):
                    sl = slice(par * 64, (par + 1) * 64)
                    last = e.transpose(
                        C.ps_tr[0:64, par, sp * 64:(sp + 1) * 64],
                        st.k[sl, sp, cb:cb + 64],
                        C.ident[sl, par * 64:(par + 1) * 64],
                        tile_position=(par * 64, 0))
            return last
        R.op("tensor", mm_ktr, reads=[st.b_k, C.b_ident],
             writes=[C.b_ptr[0], C.b_ptr[1]])
        for par2 in range(2):
            sl = slice(par2 * 64, (par2 + 1) * 64)
            src = C.ps_tr[0:64, par2, 0:512]
            if lvl == 5:
                R.op("vector", lambda e, sl=sl, src=src: e.tensor_copy(
                    st.kd[sl, :, :], src.rearrange("p (a b) -> p a b", b=64)),
                    reads=[C.b_ptr[0]], writes=[st.b_kd])
                continue
            R.op("vector", lambda e, par2=par2, sl=sl, cs=cs, src=src: e.tensor_tensor(
                out=st.kd[sl, :, :],
                in0=src.rearrange("p (a b) -> p a b", b=64),
                in1=colsel(4, par2, cs).unsqueeze(2).broadcast_to([64, 8, 64]),
                op=ALU.mult), reads=[C.b_ptr[par2], st.b_cols], writes=[st.b_kd])
            R.op("vector", lambda e, par2=par2, sl=sl, cs=cs, src=src: e.tensor_tensor(
                out=st.B[sl, :, 64:128],
                in0=src.rearrange("p (a b) -> p a b", b=64),
                in1=colsel(1, par2, cs).unsqueeze(2).broadcast_to([64, 8, 64]),
                op=ALU.mult), reads=[C.b_ptr[par2], st.b_cols], writes=[st.b_B])
        if lvl < 5.7: return
        def mm_vtr(e, cb=cb):
            last = None
            for sp in range(8):
                for par in range(2):
                    sl = slice(par * 64, (par + 1) * 64)
                    last = e.transpose(
                        C.ps_tr[0:64, par, 512 + sp * 64:512 + (sp + 1) * 64],
                        st.v[sl, sp, cb:cb + 64],
                        C.ident[sl, par * 64:(par + 1) * 64],
                        tile_position=(par * 64, 0))
            return last
        R.op("tensor", mm_vtr, reads=[st.b_v, C.b_ident],
             writes=[C.b_ptr[0], C.b_ptr[1]])
        for par2 in range(2):
            sl = slice(par2 * 64, (par2 + 1) * 64)
            src = C.ps_tr[0:64, par2, 512:1024]
            R.op("vector", lambda e, par2=par2, sl=sl, cs=cs, src=src: e.tensor_tensor(
                out=st.B[sl, :, 0:64],
                in0=src.rearrange("p (a b) -> p a b", b=64),
                in1=colsel(2, par2, cs).unsqueeze(2).broadcast_to([64, 8, 64]),
                op=ALU.mult), reads=[C.b_ptr[par2], st.b_cols], writes=[st.b_B])
        if lvl < 7: return
        Ups = C.ps_big[:, 0:2, :].rearrange("p a b -> p (a b)")

        def emit_Umm(rhs_t, b_rhs_t):
            def mm(e, rhs_t=rhs_t):
                last = None
                for sp in range(8):
                    for par in range(2):
                        sl = slice(par * 64, (par + 1) * 64)
                        last = e.matmul(
                            Ups[sl, sp * 128:(sp + 1) * 128],
                            st.P[sl, 1, sp, :], rhs_t[sl, sp, :],
                            start=True, stop=True,
                            tile_position=(par * 64, par * 64))
                return last
            R.op("tensor", mm, reads=[st.b_P, b_rhs_t],
                 writes=[C.b_big[0], C.b_big[1]])
            R.op("vector", lambda e, rhs_t=rhs_t: e.tensor_tensor(
                out=st.U[:, :, :],
                in0=Ups.rearrange("p (a b) -> p a b", b=128),
                in1=rhs_t, op=ALU.add),
                reads=[C.b_big[0], C.b_big[1], b_rhs_t], writes=[st.b_U])
        emit_Umm(st.B, st.b_B)
        for it in range(5):
            def mm_sq(e):
                last = None
                for sp in range(8):
                    for par in range(2):
                        sl = slice(par * 64, (par + 1) * 64)
                        e.matmul(C.ps_big[sl, 2, sp * 64:(sp + 1) * 64],
                                 st.P[sl, 1, sp, :], st.P[sl, 0, sp, :],
                                 start=True, stop=True,
                                 tile_position=(par * 64, par * 64))
                        last = e.matmul(C.ps_big[sl, 3, sp * 64:(sp + 1) * 64],
                                        st.P[sl, 0, sp, :], st.P[sl, 1, sp, :],
                                        start=True, stop=True,
                                        tile_position=(par * 64, par * 64))
                return last
            R.op("tensor", mm_sq, reads=[st.b_P], writes=[C.b_big[2], C.b_big[3]])
            R.op("vector", lambda e: e.tensor_copy(
                st.P[:, 0, :, :],
                C.ps_big[:, 2, :].rearrange("p (a b) -> p a b", b=64)),
                reads=[C.b_big[2]], writes=[st.b_P])
            R.op("vector", lambda e: e.tensor_copy(
                st.P[:, 1, :, :],
                C.ps_big[:, 3, :].rearrange("p (a b) -> p a b", b=64)),
                reads=[C.b_big[3]], writes=[st.b_P])
            emit_Umm(st.U, st.b_U)
        if lvl < 8: return
        def mm_kcdtr(e):
            last = None
            for sp in range(8):
                for par in range(2):
                    sl = slice(par * 64, (par + 1) * 64)
                    last = e.transpose(
                        C.ps_tr[0:64, par, sp * 64:(sp + 1) * 64],
                        st.U[sl, sp, 64:128],
                        C.ident[sl, par * 64:(par + 1) * 64],
                        tile_position=(par * 64, 0))
            return last
        R.op("tensor", mm_kcdtr, reads=[st.b_U, C.b_ident],
             writes=[C.b_ptr[0], C.b_ptr[1]])
        for par2 in range(2):
            sl = slice(par2 * 64, (par2 + 1) * 64)
            R.op("vector", lambda e, par2=par2, sl=sl: e.tensor_copy(
                st.kcdT[sl, :, :],
                C.ps_tr[0:64, par2, 0:512].rearrange(
                    "p (a b) -> p a b", b=64)),
                reads=[C.b_ptr[par2]], writes=[st.b_kcdT])

    def phase_B(ci):
        cb = ci * 64
        cs = ci % 2
        if globals().get("L0A_LVL", 99) < 9: return
        def mm_1(e):
            last = None
            for sp in range(8):
                for par in range(2):
                    sl = slice(par * 64, (par + 1) * 64)
                    last = e.matmul(C.ps_ab[sl, 0, sp * 64:(sp + 1) * 64],
                                    st.kcdT[sl, sp, :], st.Sbf[sl, sp, :],
                                    start=True, stop=True,
                                    tile_position=(par * 64, par * 64))
            return last
        R.op("tensor", mm_1, reads=[st.b_kcdT, st.b_Sbf], writes=[C.b_ab[0]])
        R.op("vector", lambda e: e.tensor_tensor(
            out=st.vn[:, :, :], in0=st.U[:, :, 0:64],
            in1=C.ps_ab[:, 0, :].rearrange("p (a b) -> p a b", b=64),
            op=ALU.subtract), reads=[st.b_U, C.b_ab[0]], writes=[st.b_vn])
        def mm_o(e):
            last = None
            for sp in range(8):
                for par in range(2):
                    sl = slice(par * 64, (par + 1) * 64)
                    e.matmul(C.ps_ab[sl, 1, sp * 64:(sp + 1) * 64],
                             st.Sbf[sl, sp, :], st.q[sl, sp, cb:cb + 64],
                             start=True, stop=False,
                             tile_position=(par * 64, par * 64))
                    last = e.matmul(C.ps_ab[sl, 1, sp * 64:(sp + 1) * 64],
                                    st.vn[sl, sp, :], st.attnT[sl, sp, :],
                                    start=False, stop=True,
                                    tile_position=(par * 64, par * 64))
            return last
        R.op("tensor", mm_o, reads=[st.b_Sbf, st.b_q, st.b_vn, st.b_attnT],
             writes=[C.b_ab[1]])
        R.op("vector", lambda e, cb=cb: e.tensor_copy(
            o_fm[:, :, cb:cb + 64],
            C.ps_ab[:, 1, :].rearrange("p (a b) -> p a b", b=64)),
            reads=[C.b_ab[1]], writes=[b_o])
        def mm_S(e):
            last = None
            for sp in range(8):
                for par in range(2):
                    sl = slice(par * 64, (par + 1) * 64)
                    last = e.matmul(C.ps_ab[sl, 0, sp * 64:(sp + 1) * 64],
                                    st.kd[sl, sp, :], st.vn[sl, sp, :],
                                    start=True, stop=True,
                                    tile_position=(par * 64, par * 64))
            return last
        R.op("tensor", mm_S, reads=[st.b_kd, st.b_vn], writes=[C.b_ab[0]])
        for par2 in range(2):
            sl = slice(par2 * 64, (par2 + 1) * 64)
            R.op("vector", lambda e, par2=par2, sl=sl, cs=cs: e.tensor_tensor(
                out=st.S32[sl, :, :], in0=st.S32[sl, :, :],
                in1=colsel(3, par2, cs).unsqueeze(2).broadcast_to([64, 8, 64]),
                op=ALU.mult), reads=[st.b_S32, st.b_cols], writes=[st.b_S32])
        R.op("vector", lambda e: e.tensor_tensor(
            out=st.S32[:, :, :], in0=st.S32[:, :, :],
            in1=C.ps_ab[:, 0, :].rearrange("p (a b) -> p a b", b=64),
            op=ALU.add), reads=[st.b_S32, C.b_ab[0]], writes=[st.b_S32])
        R.op("vector", lambda e: e.tensor_copy(st.Sbf[:, :, :], st.S32[:, :, :]),
             reads=[st.b_S32], writes=[st.b_Sbf])

    for ci in range(n_chunks):
        if hook is not None:
            hook()
        phase_A(ci)
        if dump is not None and ci == 0:
            for key, ap, b in [("d_LT", st.LT[:, :, :, :], st.b_LT),
                               ("d_P", st.P[:, :, :, :], st.b_P),
                               ("d_B", st.B[:, :, :], st.b_pool),
                               ("d_U", st.U[:, :, :], st.b_pool),
                               ("d_attnT", st.attnT[:, :, :], st.b_pool),
                               ("d_kcdT", st.kcdT[:, :, :], st.b_pool),
                               ("d_kd", st.kd[:, :, :], st.b_pool)]:
                if key in dump:
                    R.dma("sync", dump[key], ap, dump["c"], reads=[b], writes=[dump["b"]])
        phase_B(ci)
        if dump is not None and ci == 0:
            for key, ap, b in [("d_S", st.S32[:, :, :], st.b_S32),
                               ("d_vn", st.vn[:, :, :], st.b_pool)]:
                if key in dump:
                    R.dma("sync", dump[key], ap, dump["c"], reads=[b], writes=[dump["b"]])
    if dump is not None and "d_o" in dump:
        R.dma("sync", dump["d_o"], o_fm[:, :, :], dump["c"], reads=[b_o], writes=[dump["b"]])

    C.const(1e-5)
    for tc in range(4):
        for cg in range(8):
            R.op("scalar", lambda e, tc=tc, cg=cg: e.activation(
                out=st.q2[:, 0:512], in_=o_fm[:, cg, tc * 512:(tc + 1) * 512],
                func=AF.Square), reads=[b_o], writes=[st.b_q2])
            R.op("tensor", lambda e, cg=cg, tc=tc: e.matmul(
                C.ps_big[0:16, tc, :], st.BO[:, cg, :], st.q2[:, 0:512],
                start=(cg == 0), stop=(cg == 7)),
                reads=[st.b_BO, st.b_q2], writes=[C.b_big[tc]])
        R.op("vector", lambda e, tc=tc: e.tensor_copy(
            st.nrm[:, tc * 512:(tc + 1) * 512], C.ps_big[0:16, tc, :]),
            reads=[C.b_big[tc]], writes=[st.b_nrm])
    R.op("vector", lambda e: e.tensor_scalar(
        out=st.nrm[:, :], in0=st.nrm[:, :], scalar1=1.0 / 64, scalar2=1e-5,
        op0=ALU.mult, op1=ALU.add), reads=[st.b_nrm], writes=[st.b_nrm])
    R.op("scalar", lambda e: e.activation(out=st.nrm[:, :], in_=st.nrm[:, :], func=AF.Ln),
         reads=[st.b_nrm, C.b_cpool], writes=[st.b_nrm])
    R.op("scalar", lambda e: e.activation(
        out=st.nrm[:, :], in_=st.nrm[:, :], func=AF.Exp, scale=C.const(-0.5)),
        reads=[st.b_nrm, C.b_cpool], writes=[st.b_nrm])
    for cg in range(8):
        for tc in range(4):
            ps = C.ps_ab[:, tc % 2, :]
            R.op("tensor", lambda e, cg=cg, tc=tc, ps=ps: e.matmul(
                ps, st.EX[:, cg, :], st.nrm[:, tc * 512:(tc + 1) * 512],
                start=True, stop=True),
                reads=[st.b_EX, st.b_nrm], writes=[C.b_ab[tc % 2]])
            gs = (tc + cg) % 2
            R.dma("scalar", st.gstg[:, gs, :],
                  gate_d[:, cg, tc * 512:(tc + 1) * 512], st.c_g,
                  reads=[b_gt, st.b_g], writes=[st.b_g])
            R.op("scalar", lambda e, gs=gs: e.activation(
                out=st.q2[:, 0:512], in_=st.gstg[:, gs, :],
                func=AF.Silu), reads=[st.b_g], writes=[st.b_q2])
            R.op("vector", lambda e, cg=cg, tc=tc, ps=ps: e.tensor_tensor(
                out=o_fm[:, cg, tc * 512:(tc + 1) * 512],
                in0=o_fm[:, cg, tc * 512:(tc + 1) * 512], in1=ps, op=ALU.mult),
                reads=[C.b_ab[tc % 2], b_o], writes=[b_o])
            R.op("vector", lambda e, cg=cg, tc=tc: e.scalar_tensor_tensor(
                out=o_fm[:, cg, tc * 512:(tc + 1) * 512],
                in0=o_fm[:, cg, tc * 512:(tc + 1) * 512],
                scalar=st.smalls[:, 2:3], in1=st.q2[:, 0:512],
                op0=ALU.mult, op1=ALU.mult),
                reads=[st.b_q2, b_o, st.b_smalls], writes=[b_o])

    for nh in range(2):
        loadW(Wo, nh * 512)
        for t in range(16):
            ps = C.ps_ab[:, t % 2, :]
            b_ps = C.b_ab[t % 2]

            def mm_wo(e, t=t, ps=ps):
                last = None
                for cc in range(8):
                    last = e.matmul(ps, o_fm[:, cc, t * 128:(t + 1) * 128],
                                    st.w[:, cc, :], start=(cc == 0), stop=(cc == 7))
                return last
            R.op("tensor", mm_wo, reads=[st.b_w, b_o], writes=[b_ps])
            xadd_fn(t, nh, ps, b_ps)



def build_l0ffn_program():
    nc = bass.Bass("TRN2", target_bir_lowering=False, dynamic_dma_scratch_size=16384)
    x_in = nc.dram_tensor("x_in", [4096, 1024], F32, kind="ExternalInput").ap()
    wg_i = nc.dram_tensor("wgT", [1024, 3072], I32, kind="ExternalInput").ap()
    wu_i = nc.dram_tensor("wuT", [1024, 3072], I32, kind="ExternalInput").ap()
    wd_i = nc.dram_tensor("wdT", [3072, 1024], I32, kind="ExternalInput").ap()
    x_out = nc.dram_tensor("x_out", [4096, 1024], F32, kind="ExternalOutput").ap()
    wg_b = nc.dram_tensor("wg_bf", [1024, 3072], BF16, kind="Internal").ap()
    wu_b = nc.dram_tensor("wu_bf", [1024, 3072], BF16, kind="Internal").ap()
    wd_b = nc.dram_tensor("wd_bf", [3072, 1024], BF16, kind="Internal").ap()

    R = Rec(nc)
    C = emit_common(R)
    C.const(1e-5)
    b_w = Buf("wbf")
    cvt = R.sbuf("cvt", [128, 2, 3072], BF16)
    b_cvt = [Buf(f"cvt{i}") for i in range(2)]
    c_cl = [R.ctr(f"cvt_cl{i}", 16) for i in range(2)]
    c_cs = [R.ctr(f"cvt_cs{i}", 16) for i in range(2)]
    emit_int_to_bf16(R, cvt, b_cvt, c_cl, c_cs, wg_i, wg_b, 1024, 3072, b_w)
    emit_int_to_bf16(R, cvt, b_cvt, c_cl, c_cs, wu_i, wu_b, 1024, 3072, b_w)
    emit_int_to_bf16(R, cvt, b_cvt, c_cl, c_cs, wd_i, wd_b, 3072, 1024, b_w)

    x_sb = R.sbuf("x_sb", [128, 16, 1024], F32)
    b_x = Buf("x")
    c_x = R.ctr("c_x", 16)
    c_xo = R.ctr("c_xo", 16)
    st = l0ffn_state(R, "f")
    b_xout = Buf("xout")
    for p in range(2):
        R.dma("sync", x_sb[:, :, :],
              x_in[p * 2048:(p + 1) * 2048, :].rearrange("(a p) d -> p a d", p=128),
              c_x, reads=[b_x], writes=[b_x])
        emit_l0ffn(R, C, st, x_sb, b_x, wg_b, wu_b, wd_b, b_w)
        R.dma("sync", x_out[p * 2048:(p + 1) * 2048, :].rearrange("(a p) d -> p a d", p=128),
              x_sb[:, :, :], c_xo, reads=[b_x], writes=[b_xout])
        R.full_barrier()
    R.drain("sync", [c_xo])
    R.emit()
    return nc, R


def build_l1attn_program():
    nc = bass.Bass("TRN2", target_bir_lowering=False, dynamic_dma_scratch_size=16384)
    x_in = nc.dram_tensor("x_in", [4096, 1024], F32, kind="ExternalInput").ap()
    wq = nc.dram_tensor("WqT", [1024, 1024], F32, kind="ExternalInput").ap()
    wk = nc.dram_tensor("WkT", [1024, 1024], F32, kind="ExternalInput").ap()
    wv = nc.dram_tensor("WvT", [1024, 1024], F32, kind="ExternalInput").ap()
    wo = nc.dram_tensor("WoT", [1024, 1024], F32, kind="ExternalInput").ap()
    bq = nc.dram_tensor("bq", [1024], F32, kind="ExternalInput").ap()
    bk = nc.dram_tensor("bk", [1024], F32, kind="ExternalInput").ap()
    bv = nc.dram_tensor("bv", [1024], F32, kind="ExternalInput").ap()
    bo = nc.dram_tensor("bo", [1024], F32, kind="ExternalInput").ap()
    x_out = nc.dram_tensor("x_out", [4096, 1024], F32, kind="ExternalOutput").ap()

    R = Rec(nc)
    C = emit_common(R)
    C.const(1e-5)
    x_sb = R.sbuf("x_sb", [128, 16, 1024], F32)
    b_x = Buf("x")
    c_x = R.ctr("c_x", 16)
    c_xo = R.ctr("c_xo", 16)
    st = l1attn_state(R, C, "a")
    emit_l1attn_setup(R, C, st, "a")
    b_xout = Buf("xout")
    def x_ap_fn(i):
        return x_sb[:, i, :]

    def xadd_fn(t, nh, ps, b_ps):
        R.op("vector", lambda e, t=t, nh=nh, ps=ps: e.scalar_tensor_tensor(
            out=x_sb[:, t, nh * 512:(nh + 1) * 512], in0=ps, scalar=0.5,
            in1=x_sb[:, t, nh * 512:(nh + 1) * 512], op0=ALU.mult, op1=ALU.add),
            reads=[b_ps, b_x], writes=[b_x])
    for p in range(2):
        R.dma("sync", x_sb[:, :, :],
              x_in[p * 2048:(p + 1) * 2048, :].rearrange("(a p) d -> p a d", p=128),
              c_x, reads=[b_x], writes=[b_x])
        emit_l1attn(R, C, st, x_ap_fn, b_x, xadd_fn, wq, wk, wv, wo, bq, bk, bv, bo)
        R.dma("sync", x_out[p * 2048:(p + 1) * 2048, :].rearrange("(a p) d -> p a d", p=128),
              x_sb[:, :, :], c_xo, reads=[b_x], writes=[b_xout])
        R.full_barrier()
    R.drain("sync", [c_xo])
    R.emit()
    return nc, R


def kernel_l0ffn_only(x, wg, wu, wd):
    wgT = np.ascontiguousarray(wg.T)
    wuT = np.ascontiguousarray(wu.T)
    wdT = np.ascontiguousarray(wd.T)
    nc, R = build_l0ffn_program()
    in_maps = []
    for c in range(8):
        xs = np.ascontiguousarray(x[c * 2:(c + 1) * 2].reshape(4096, 1024))
        in_maps.append({"x_in": xs, "wgT": wgT, "wuT": wuT, "wdT": wdT})
    res = run_bass_kernel_spmd(nc, in_maps, core_ids=list(range(8)))
    out = np.stack([np.asarray(r["x_out"]).reshape(2, 2048, 1024) for r in res.results])
    return out.reshape(16, 2048, 1024)


def kernel_l1attn_only(x, Wq, bq, Wk, bk, Wv, bv, Wo, bo):
    m = {"WqT": np.ascontiguousarray(Wq.T), "WkT": np.ascontiguousarray(Wk.T),
         "WvT": np.ascontiguousarray(Wv.T), "WoT": np.ascontiguousarray(Wo.T),
         "bq": bq, "bk": bk, "bv": bv, "bo": bo}
    nc, R = build_l1attn_program()
    in_maps = []
    for c in range(8):
        xs = np.ascontiguousarray(x[c * 2:(c + 1) * 2].reshape(4096, 1024))
        in_maps.append(dict(m, x_in=xs))
    res = run_bass_kernel_spmd(nc, in_maps, core_ids=list(range(8)))
    out = np.stack([np.asarray(r["x_out"]).reshape(2, 2048, 1024) for r in res.results])
    return out.reshape(16, 2048, 1024)


def build_l1ffn_program():
    nc = bass.Bass("TRN2", target_bir_lowering=False, dynamic_dma_scratch_size=16384)
    x_in = nc.dram_tensor("x_in", [4096, 1024], F32, kind="ExternalInput").ap()
    w1 = [nc.dram_tensor(f"W1T{e}", [4096, 1024], F32, kind="ExternalInput").ap()
          for e in range(8)]
    w2 = [nc.dram_tensor(f"W2T{e}", [4096, 1024], F32, kind="ExternalInput").ap()
          for e in range(8)]
    gbT = nc.dram_tensor("gbT", [1024, 2, 8], F32, kind="ExternalInput").ap()
    b1T = nc.dram_tensor("b1T", [4096, 8], F32, kind="ExternalInput").ap()
    b2 = nc.dram_tensor("b2", [8, 1024], F32, kind="ExternalInput").ap()
    centT = nc.dram_tensor("centT", [1024, 8], F32, kind="ExternalInput").ap()
    out = nc.dram_tensor("out", [4096, 1024], F32, kind="ExternalOutput").ap()
    fhat_d = nc.dram_tensor("fhat_d", [128, 8, 4096], BF16, kind="Internal").ap()
    w1b = [nc.dram_tensor(f"w1b{e}", [4096, 1024], BF16, kind="Internal").ap()
           for e in range(8)]
    w2b = [nc.dram_tensor(f"w2b{e}", [4096, 1024], BF16, kind="Internal").ap()
           for e in range(8)]

    R = Rec(nc)
    C = emit_common(R)
    C.const(1e-5)
    aE = R.sbuf("aE", [128, 32, 8], F32)
    b_aE = Buf("aE")
    b_fd = Buf("fd")
    b_wbf = Buf("wbf")
    c_conv = R.ctr("c_conv", 16)
    for e in range(8):
        for q in range(4):
            sl = slice(q * 1024, (q + 1) * 1024)
            R.dma("gpsimd", w1b[e][sl, :], w1[e][sl, :], c_conv, writes=[b_wbf])
            R.dma("gpsimd", w2b[e][sl, :], w2[e][sl, :], c_conv, writes=[b_wbf])
    stp = l1ffn_pre_state(R, "p")
    stm = l1ffn_main_state(R, "m")
    emit_l1ffn_pre_setup(R, C, stp, centT)
    emit_l1ffn_main_setup(R, C, stm, gbT, b1T, b2)
    xt = R.sbuf("xt", [128, 2, 1024], F32)
    b_xt = Buf("xt")
    c_x = R.ctr("c_x", 16)
    b_out = [Buf(f"out{t}") for t in range(32)]
    c_out = R.ctr("c_out", 16)

    def make_xfn(p):
        def xfn(i):
            s2 = i % 2
            R.dma("sync", xt[:, s2, :],
                  x_in[(p * 16 + i) * 128:(p * 16 + i + 1) * 128, :], c_x,
                  reads=[b_xt], writes=[b_xt])
            return xt[:, s2, :]
        return xfn
    for p in range(2):
        emit_l1ffn_pre(R, C, stp, make_xfn(p), b_xt, out, b_out, c_out, p * 16,
                       fhat_d, b_fd, aE, b_aE)
    R.full_barrier()
    emit_l1ffn_main(R, C, stm, w1b, w2b, b_wbf, out, b_out, fhat_d, b_fd, aE, b_aE)
    R.drain("sync", [stm.c_acc, c_out])
    R.emit()
    return nc, R


def kernel_l1ffn_only(x, centroids, gamma, beta, W1, b1, W2, b2):
    m = {"gbT": np.ascontiguousarray(np.stack([gamma.T, beta.T], axis=1)),
         "b1T": np.ascontiguousarray(b1.T), "b2": b2,
         "centT": np.ascontiguousarray(centroids.T)}
    for e in range(8):
        m[f"W1T{e}"] = np.ascontiguousarray(
            W1[e].reshape(32, 128, 8, 128).transpose(0, 3, 2, 1).reshape(4096, 1024))
        m[f"W2T{e}"] = np.ascontiguousarray(W2[e].T)
    nc, R = build_l1ffn_program()
    in_maps = []
    for c in range(8):
        xs = np.ascontiguousarray(x[c * 2:(c + 1) * 2].reshape(4096, 1024))
        in_maps.append(dict(m, x_in=xs))
    res = run_bass_kernel_spmd(nc, in_maps, core_ids=list(range(8)))
    o = np.stack([np.asarray(r["out"]).reshape(2, 2048, 1024) for r in res.results])
    return o.reshape(16, 2048, 1024)



def build_l0attn_program(n_chunks=32):
    nc = bass.Bass("TRN2", target_bir_lowering=False, dynamic_dma_scratch_size=16384)
    x_in = nc.dram_tensor("x_in", [4096, 1024], F32, kind="ExternalInput").ap()
    Wq = nc.dram_tensor("Wq", [1024, 1024], F32, kind="ExternalInput").ap()
    Wk = nc.dram_tensor("Wk", [1024, 1024], F32, kind="ExternalInput").ap()
    Wv = nc.dram_tensor("Wv", [1024, 1024], F32, kind="ExternalInput").ap()
    Wg = nc.dram_tensor("Wg", [1024, 1024], F32, kind="ExternalInput").ap()
    Wo = nc.dram_tensor("Wo", [1024, 1024], F32, kind="ExternalInput").ap()
    Wb = nc.dram_tensor("Wb", [1024, 16], F32, kind="ExternalInput").ap()
    Wa = nc.dram_tensor("Wa", [1024, 16], F32, kind="ExternalInput").ap()
    dtb = nc.dram_tensor("dtb", [16], F32, kind="ExternalInput").ap()
    alog = nc.dram_tensor("alog", [16], F32, kind="ExternalInput").ap()
    nw = nc.dram_tensor("nw", [64], F32, kind="ExternalInput").ap()
    x_out = nc.dram_tensor("x_out", [4096, 1024], F32, kind="ExternalOutput").ap()
    g_d = nc.dram_tensor("g_d", [16, 2048], F32, kind="Internal").ap()
    be_d = nc.dram_tensor("be_d", [16, 2048], F32, kind="Internal").ap()
    gt_d = nc.dram_tensor("gt_d", [128, 8, 2048], BF16, kind="Internal").ap()

    R = Rec(nc)
    C = emit_common(R)
    C.const(1e-5)
    st = l0attn_state(R, "d")
    emit_l0attn_setup(R, C, st)
    xlnt = R.sbuf("d_xlnt", [128, 8, 2048], BF16)
    b_xlnt = Buf("d_xlnt")
    xt = st.fscr.rearrange("p (a b) -> p a b", a=1)
    b_xt = st.b_fscr
    xo = st.S32
    b_xo = st.b_S32
    c_xi = R.ctr("d_cxi", 16)
    c_xo = R.ctr("d_cxo", 16)
    b_gd = Buf("d_gd")
    b_xout = Buf("d_xout")

    for p in range(2):
        def x_ap_fn(i, p=p):
            R.dma("sync", xt[:, 0, :], x_in[(p * 16 + i) * 128:(p * 16 + i + 1) * 128, :],
                  c_xi, reads=[b_xt], writes=[b_xt])
            return xt[:, 0, :]

        def xadd_fn(t, nh, ps, b_ps, p=p):
            sl = t % 2
            rowsl = slice((p * 16 + t) * 128, (p * 16 + t + 1) * 128)
            colsl = slice(nh * 512, (nh + 1) * 512)
            R.dma("sync", xt[:, 0, colsl], x_in[rowsl, colsl],
                  c_xi, reads=[b_xt], writes=[b_xt])
            R.op("vector", lambda e, colsl=colsl, ps=ps: e.tensor_add(
                xo[:, :, :], ps.rearrange("p (a b) -> p a b", b=64),
                xt[:, 0, colsl].rearrange("p (a b) -> p a b", b=64)),
                reads=[b_ps, b_xt], writes=[b_xo])
            R.dma("sync",
                  x_out[rowsl, colsl].rearrange("p (a b) -> p a b", b=64),
                  xo[:, :, :], c_xo, reads=[b_xo], writes=[b_xout])
        emit_l0attn(R, C, st, x_ap_fn, b_xt, xadd_fn, xlnt, b_xlnt, xlnt, b_xlnt,
                    Wq, Wk, Wv, Wg, Wo, Wb, Wa, dtb, alog, nw, g_d, be_d, b_gd,
                    gt_d, Buf("d_gt"), n_chunks=n_chunks)
        R.full_barrier()
    R.drain("sync", [c_xo])
    R.emit()
    return nc, R


def kernel_l0attn_only(x, Wq, Wk, Wv, Wg, Wo, Wb, Wa, dt_bias, A_log, norm_w):
    m = {"Wq": Wq, "Wk": Wk, "Wv": Wv, "Wg": Wg, "Wo": Wo, "Wb": Wb, "Wa": Wa,
         "dtb": dt_bias, "alog": A_log, "nw": norm_w}
    nc, R = build_l0attn_program()
    in_maps = []
    for c in range(8):
        xs = np.ascontiguousarray(x[c * 2:(c + 1) * 2].reshape(4096, 1024))
        in_maps.append(dict(m, x_in=xs))
    res = run_bass_kernel_spmd(nc, in_maps, core_ids=list(range(8)))
    o = np.stack([np.asarray(r["x_out"]).reshape(2, 2048, 1024) for r in res.results])
    return o.reshape(16, 2048, 1024)





def build_fused_program(arena_kb=200):
    nc = bass.Bass("TRN2", target_bir_lowering=False, dynamic_dma_scratch_size=16384)
    DT = nc.dram_tensor
    x_in = DT("x_in", [4096, 1024], F32, kind="ExternalInput").ap()
    Wq = DT("Wq", [1024, 1024], F32, kind="ExternalInput").ap()
    Wk = DT("Wk", [1024, 1024], F32, kind="ExternalInput").ap()
    Wv = DT("Wv", [1024, 1024], F32, kind="ExternalInput").ap()
    Wg = DT("Wg", [1024, 1024], F32, kind="ExternalInput").ap()
    Wo = DT("Wo", [1024, 1024], F32, kind="ExternalInput").ap()
    Wb = DT("Wb", [1024, 16], F32, kind="ExternalInput").ap()
    Wa = DT("Wa", [1024, 16], F32, kind="ExternalInput").ap()
    dtb = DT("dtb", [16], F32, kind="ExternalInput").ap()
    alog = DT("alog", [16], F32, kind="ExternalInput").ap()
    nw = DT("nw", [64], F32, kind="ExternalInput").ap()
    wg_i = DT("wgT", [1024, 3072], I32, kind="ExternalInput").ap()
    wu_i = DT("wuT", [1024, 3072], I32, kind="ExternalInput").ap()
    wd_i = DT("wdT", [3072, 1024], I32, kind="ExternalInput").ap()
    AWq = DT("AWqT", [1024, 1024], F32, kind="ExternalInput").ap()
    AWk = DT("AWkT", [1024, 1024], F32, kind="ExternalInput").ap()
    AWv = DT("AWvT", [1024, 1024], F32, kind="ExternalInput").ap()
    AWo = DT("AWoT", [1024, 1024], F32, kind="ExternalInput").ap()
    bq = DT("bq", [1024], F32, kind="ExternalInput").ap()
    bk = DT("bk", [1024], F32, kind="ExternalInput").ap()
    bv = DT("bv", [1024], F32, kind="ExternalInput").ap()
    bo = DT("bo", [1024], F32, kind="ExternalInput").ap()
    w1 = [DT(f"W1T{e}", [4096, 1024], F32, kind="ExternalInput").ap() for e in range(8)]
    w2 = [DT(f"W2T{e}", [4096, 1024], F32, kind="ExternalInput").ap() for e in range(8)]
    gbT = DT("gbT", [1024, 2, 8], F32, kind="ExternalInput").ap()
    b1T = DT("b1T", [4096, 8], F32, kind="ExternalInput").ap()
    b2 = DT("b2", [8, 1024], F32, kind="ExternalInput").ap()
    centT = DT("centT", [1024, 8], F32, kind="ExternalInput").ap()
    out = DT("out", [4096, 1024], F32, kind="ExternalOutput").ap()
    x_h = DT("x_h", [4096, 1024], F32, kind="Internal").ap()
    wg_b = DT("wg_bf", [1024, 3072], BF16, kind="Internal").ap()
    wu_b = DT("wu_bf", [1024, 3072], BF16, kind="Internal").ap()
    wd_b = DT("wd_bf", [3072, 1024], BF16, kind="Internal").ap()
    g_d = DT("g_d", [16, 2048], F32, kind="Internal").ap()
    be_d = DT("be_d", [16, 2048], F32, kind="Internal").ap()
    fhat_d = DT("fhat_d", [128, 8, 4096], BF16, kind="Internal").ap()
    gt_d = DT("gt_d", [128, 8, 2048], BF16, kind="Internal").ap()
    w1b = [DT(f"w1b{e}", [4096, 1024], BF16, kind="Internal").ap() for e in range(8)]
    w2b = [DT(f"w2b{e}", [4096, 1024], BF16, kind="Internal").ap() for e in range(8)]

    R = Rec(nc)
    R.use_arena(arena_kb * 1024)
    C = emit_common(R)
    C.const(1e-5)
    C.const(1.0 / 127.0)
    aE = R.sbuf("aE", [128, 32, 8], F32)
    b_aE = Buf("aE")
    b_fd, b_gd, b_w, b_xh = Buf("fd"), Buf("gd"), Buf("wbf"), Buf("xh")
    b_out = [Buf(f"out{t}") for t in range(32)]
    c_out = R.ctr("c_out", 16)
    c_xh = R.ctr("c_xh", 16)
    b_w12 = Buf("w12bf")
    c_conv = R.ctr("c_conv", 16)
    conv_jobs = [(wg_b, wg_i, b_w), (wu_b, wu_i, b_w), (wd_b, wd_i, b_w)]
    for e in range(8):
        for q in range(4):
            sl = slice(q * 1024, (q + 1) * 1024)
            conv_jobs.append((w1b[e][sl, :], w1[e][sl, :], b_w12))
            conv_jobs.append((w2b[e][sl, :], w2[e][sl, :], b_w12))
    conv_it = iter(conv_jobs)

    def conv_hook():
        try:
            dst, src, bw = next(conv_it)
        except StopIteration:
            return
        R.dma("gpsimd", dst, src, c_conv, writes=[bw])
    R.set_region_base()

    for p in range(2):
        R.open_region()
        st0 = l0attn_state(R, f"a{p}")
        emit_l0attn_setup(R, C, st0)
        xlnt0 = R.sbuf(f"a{p}_xlnt", [128, 8, 2048], BF16)
        b_xlnt0 = Buf(f"a{p}_xlnt")
        xt0 = st0.fscr.rearrange("p (a b) -> p a b", a=1)
        b_xt0 = st0.b_fscr
        xo0 = st0.S32
        b_xo0 = st0.b_S32

        def x_ap0(i, p=p):
            R.dma("sync", xt0[:, 0, :],
                  x_in[(p * 16 + i) * 128:(p * 16 + i + 1) * 128, :],
                  c_xh, reads=[b_xt0], writes=[b_xt0])
            return xt0[:, 0, :]

        def xadd0(t, nh, ps, b_ps, p=p):
            rowsl = slice((p * 16 + t) * 128, (p * 16 + t + 1) * 128)
            colsl = slice(nh * 512, (nh + 1) * 512)
            R.dma("sync", xt0[:, 0, colsl], x_in[rowsl, colsl],
                  c_xh, reads=[b_xt0], writes=[b_xt0])
            R.op("vector", lambda e, colsl=colsl, ps=ps: e.tensor_add(
                xo0[:, :, :], ps.rearrange("p (a b) -> p a b", b=64),
                xt0[:, 0, colsl].rearrange("p (a b) -> p a b", b=64)),
                reads=[b_ps, b_xt0], writes=[b_xo0])
            R.dma("sync", x_h[rowsl, colsl].rearrange("p (a b) -> p a b", b=64),
                  xo0[:, :, :], c_xh, reads=[b_xo0], writes=[b_xh])
        emit_l0attn(R, C, st0, x_ap0, b_xt0, xadd0, xlnt0, b_xlnt0, xlnt0, b_xlnt0,
                    Wq, Wk, Wv, Wg, Wo, Wb, Wa, dtb, alog, nw, g_d, be_d, b_gd,
                    gt_d, Buf(f"gt{p}"), hook=conv_hook)
        R.full_barrier()

        R.open_region()
        stf = l0ffn_state(R, f"f{p}")
        x_sb = R.sbuf(f"f{p}_xsb", [128, 16, 1024], F32)
        b_x = Buf(f"f{p}_x")
        R.dma("sync", x_sb[:, :, :],
              x_h[p * 2048:(p + 1) * 2048, :].rearrange("(a p) d -> p a d", p=128),
              c_xh, reads=[b_xh, b_x], writes=[b_x])
        emit_l0ffn(R, C, stf, x_sb, b_x, wg_b, wu_b, wd_b, b_w)
        R.dma("sync", x_h[p * 2048:(p + 1) * 2048, :].rearrange("(a p) d -> p a d", p=128),
              x_sb[:, :, :], c_xh, reads=[b_x], writes=[b_xh])
        R.full_barrier()

        R.open_region()
        sta = l1attn_state(R, C, f"t{p}")
        emit_l1attn_setup(R, C, sta, f"t{p}")
        xa_st = R.sbuf(f"t{p}_xst", [128, 2, 1024], F32)
        b_xa = Buf(f"t{p}_x")

        def x_ap1(i, p=p):
            R.dma("sync", xa_st[:, 0, :],
                  x_h[(p * 16 + i) * 128:(p * 16 + i + 1) * 128, :], c_xh,
                  reads=[b_xh, b_xa], writes=[b_xa])
            return xa_st[:, 0, :]

        def xadd1(t, nh, ps, b_ps, p=p):
            rowsl = slice((p * 16 + t) * 128, (p * 16 + t + 1) * 128)
            colsl = slice(nh * 512, (nh + 1) * 512)
            R.dma("sync", xa_st[:, 0, colsl], x_h[rowsl, colsl], c_xh,
                  reads=[b_xh, b_xa], writes=[b_xa])
            R.op("vector", lambda e, colsl=colsl, nh=nh, ps=ps: e.scalar_tensor_tensor(
                out=xa_st[:, 1, nh * 512:(nh + 1) * 512], in0=ps, scalar=0.5,
                in1=xa_st[:, 0, colsl], op0=ALU.mult, op1=ALU.add),
                reads=[b_ps, b_xa], writes=[b_xa])
            R.dma("sync", x_h[rowsl, colsl], xa_st[:, 1, colsl], c_xh,
                  reads=[b_xa], writes=[b_xh])
        emit_l1attn(R, C, sta, x_ap1, b_xa, xadd1, AWq, AWk, AWv, AWo, bq, bk, bv, bo)
        R.full_barrier()

        R.open_region()
        stp = l1ffn_pre_state(R, f"q{p}")
        emit_l1ffn_pre_setup(R, C, stp, centT)
        xtq = R.sbuf(f"q{p}_xt", [128, 2, 1024], F32)
        b_xtq = Buf(f"q{p}_xt")

        def x_apq(i, p=p):
            s2 = i % 2
            R.dma("sync", xtq[:, s2, :],
                  x_h[(p * 16 + i) * 128:(p * 16 + i + 1) * 128, :], c_xh,
                  reads=[b_xh, b_xtq], writes=[b_xtq])
            return xtq[:, s2, :]
        emit_l1ffn_pre(R, C, stp, x_apq, b_xtq, out, b_out, c_out, p * 16,
                       fhat_d, b_fd, aE, b_aE)
        R.full_barrier()

    for dst, src, bw in conv_it:
        R.dma("gpsimd", dst, src, c_conv, writes=[bw])
    R.open_region()
    stm = l1ffn_main_state(R, "m")
    emit_l1ffn_main_setup(R, C, stm, gbT, b1T, b2)
    emit_l1ffn_main(R, C, stm, w1b, w2b, b_w12, out, b_out, fhat_d, b_fd, aE, b_aE)
    R.drain("sync", [stm.c_acc, c_out])
    R.drain("gpsimd", [stm.c_acc])
    R.emit()
    return nc, R


def kernel_fused(inputs):
    ii = inputs
    x = np.asarray(ii["L0_attn_x"])
    ca = np.ascontiguousarray
    m = {"Wq": ca(np.asarray(ii["L0_attn_Wq"])), "Wk": ca(np.asarray(ii["L0_attn_Wk"])),
         "Wv": ca(np.asarray(ii["L0_attn_Wv"])), "Wg": ca(np.asarray(ii["L0_attn_Wg"])),
         "Wo": ca(np.asarray(ii["L0_attn_Wo"])), "Wb": ca(np.asarray(ii["L0_attn_Wb"])),
         "Wa": ca(np.asarray(ii["L0_attn_Wa"])),
         "dtb": ca(np.asarray(ii["L0_attn_dt_bias"])),
         "alog": ca(np.asarray(ii["L0_attn_A_log"])),
         "nw": ca(np.asarray(ii["L0_attn_norm_w"])),
         "wgT": ca(np.asarray(ii["L0_ffn_w_gate"]).T),
         "wuT": ca(np.asarray(ii["L0_ffn_w_up"]).T),
         "wdT": ca(np.asarray(ii["L0_ffn_w_down"]).T),
         "AWqT": ca(np.asarray(ii["L1_attn_Wq"]).T),
         "AWkT": ca(np.asarray(ii["L1_attn_Wk"]).T),
         "AWvT": ca(np.asarray(ii["L1_attn_Wv"]).T),
         "AWoT": ca(np.asarray(ii["L1_attn_Wo"]).T),
         "bq": ca(np.asarray(ii["L1_attn_bq"])), "bk": ca(np.asarray(ii["L1_attn_bk"])),
         "bv": ca(np.asarray(ii["L1_attn_bv"])), "bo": ca(np.asarray(ii["L1_attn_bo"])),
         "gbT": ca(np.stack([np.asarray(ii["L1_ffn_gamma"]).T,
                             np.asarray(ii["L1_ffn_beta"]).T], axis=1)),
         "b1T": ca(np.asarray(ii["L1_ffn_b1"]).T),
         "b2": ca(np.asarray(ii["L1_ffn_b2"])),
         "centT": ca(np.asarray(ii["L1_ffn_centroids"]).T)}
    W1 = np.asarray(ii["L1_ffn_W1"])
    W2 = np.asarray(ii["L1_ffn_W2"])
    for e in range(8):
        m[f"W1T{e}"] = ca(W1[e].reshape(32, 128, 8, 128)
                          .transpose(0, 3, 2, 1).reshape(4096, 1024))
        m[f"W2T{e}"] = ca(W2[e].T)
    nc, R = build_fused_program()
    in_maps = []
    for c in range(8):
        in_maps.append(dict(m, x_in=ca(x[c * 2:(c + 1) * 2].reshape(4096, 1024))))
    res = run_bass_kernel_spmd(nc, in_maps, core_ids=list(range(8)))
    o = np.stack([np.asarray(r["out"]).reshape(2, 2048, 1024) for r in res.results])
    return o.reshape(16, 2048, 1024)


def kernel(**inputs):
    return kernel_fused(inputs)


# version 07
# speedup vs baseline: 1.1034x; 1.1034x over previous
import contextlib
import math
import numpy as np
import ml_dtypes
import concourse.bass as bass
import concourse.mybir as mybir
from concourse.bass_utils import run_bass_kernel_spmd

F32 = mybir.dt.float32
BF16 = mybir.dt.bfloat16
I32 = mybir.dt.int32
AF = mybir.ActivationFunctionType
ALU = mybir.AluOpType
AX = mybir.AxisListType

ENGINES = ("sync", "scalar", "vector", "tensor", "gpsimd")


class Ctr:
    __slots__ = ("sem", "n", "step", "name")

    def __init__(self, sem, step, name):
        self.sem, self.n, self.step, self.name = sem, 0, step, name


class Buf:
    __slots__ = ("name", "wr", "rd")

    def __init__(self, name):
        self.name = name
        self.wr = {}
        self.rd = {}


class Op:
    __slots__ = ("engine", "fn", "waits", "incs")

    def __init__(self, engine, fn, waits, incs):
        self.engine, self.fn, self.waits, self.incs = engine, fn, waits, incs


class Rec:

    def __init__(self, nc):
        self.nc = nc
        self.ops = []
        self.stack = contextlib.ExitStack()
        self._ctrs = {}
        self.ec = {e: self.ctr("c_" + e, 1) for e in ENGINES}
        self.floor = {e: {} for e in ENGINES}
        self.n_sbuf_bytes = 0

    def ctr(self, name, step=1):
        if name in self._ctrs:
            return self._ctrs[name]
        sem = self.stack.enter_context(self.nc.semaphore(name))
        c = Ctr(sem, step, name)
        self._ctrs[name] = c
        return c

    def sbuf(self, name, shape, dtype):
        itemsz = {F32: 4, BF16: 2, I32: 4}[dtype]
        nb = int(np.prod(shape[1:])) * itemsz
        if getattr(self, "arena", None) is not None:
            off = (self.acur + 3) & ~3
            self.acur = off + nb
            assert self.acur <= self.arena_bytes, \
                f"arena overflow at {name}: {self.acur} > {self.arena_bytes}"
            self.apeak = max(getattr(self, "apeak", 0), self.acur)
            view = self.arena[0:shape[0], off:off + nb].bitcast(dtype)
            if len(shape) > 2:
                dims = " ".join(f"d{i}" for i in range(1, len(shape)))
                view = view.rearrange(
                    f"p ({dims}) -> p {dims}",
                    **{f"d{i}": shape[i] for i in range(1, len(shape) - 1)})
            return view
        t = self.stack.enter_context(self.nc.sbuf_tensor(name, shape, dtype))
        self.n_sbuf_bytes += nb
        return t

    def use_arena(self, nbytes):
        self.arena = self.stack.enter_context(
            self.nc.sbuf_tensor("arena", [128, nbytes], mybir.dt.int8))
        self.arena_bytes = nbytes
        self.acur = 0
        self.apeak = 0

    def open_region(self):
        if getattr(self, "arena", None) is not None:
            self.acur = self.region_base

    def set_region_base(self):
        self.region_base = (self.acur + 3) & ~3

    def psum(self, name, shape, dtype=F32):
        return self.stack.enter_context(self.nc.psum_tensor(name, shape, dtype))

    def op(self, engine, fn, reads=(), writes=(), ctr=None, inc=None):
        myctr = ctr if ctr is not None else self.ec[engine]
        waits = {}

        def need(tokmap):
            for c, v in tokmap.items():
                if waits.get(c, 0) < v:
                    waits[c] = v

        for b in reads:
            need(b.wr)
        for b in writes:
            need(b.wr)
            need(b.rd)
        for c in list(waits):
            if c.step == 16 and waits[c] < c.n:
                waits[c] = c.n
        fl = self.floor[engine]
        wl = []
        for c, v in waits.items():
            if fl.get(c, 0) < v:
                wl.append((c, v))
                fl[c] = v
        inc_amt = myctr.step if inc is None else inc
        myctr.n += inc_amt
        tok = (myctr, myctr.n)
        for b in writes:
            b.wr[myctr] = myctr.n
            b.rd = {}
        for b in reads:
            b.rd[myctr] = myctr.n
        self.ops.append(Op(engine, fn, wl, [(myctr, inc_amt)]))
        return tok

    def dma(self, queue, out, in_, ctr, reads=(), writes=(), **kw):
        def fn(eng):
            return eng.dma_start(out=out, in_=in_, **kw)
        return self.op(queue, fn, reads=reads, writes=writes, ctr=ctr, inc=16)

    def drain(self, engine, ctrs):
        self.ops.append(Op(engine, None, [(c, c.n) for c in ctrs if c.n], []))

    def full_barrier(self):
        bar = self.ctr("c_bar", 1)
        for e in ENGINES:
            self.ops.append(Op(e, lambda eng: eng.nop(), [], [(bar, 1)]))
            bar.n += 1
        dma_waits = [(c, c.n) for c in self._ctrs.values() if c.step == 16 and c.n]
        for e in ENGINES:
            wl = [(bar, bar.n)] + dma_waits
            fl = self.floor[e]
            keep = [(c, v) for c, v in wl if fl.get(c, 0) < v]
            for c, v in keep:
                fl[c] = v
            self.ops.append(Op(e, None, keep, []))

    def emit(self):
        nc = self.nc
        per = {e: [] for e in ENGINES}
        for o in self.ops:
            per[o.engine].append(o)
        with nc.Block() as block:
            for ename in ENGINES:
                items = per[ename]
                if not items:
                    continue

                def make(items):
                    def emit_fn(eng):
                        for it in items:
                            for (c, v) in it.waits:
                                eng.wait_ge(c.sem, v)
                            if it.fn is not None:
                                instr = it.fn(eng)
                                for (c, k) in it.incs:
                                    instr.then_inc(c.sem, k)
                    return emit_fn

                getattr(block, ename)(make(items))


class NS:
    pass



def emit_common(R: Rec):
    C = NS()
    C.ident = R.sbuf("c_ident", [128, 128], BF16)
    C.b_ident = Buf("ident")
    C.onesf = R.sbuf("c_ones", [128, 128], F32)
    C.b_onesf = Buf("onesf")
    idf = R.sbuf("c_identf", [128, 128], F32)
    b_idf = Buf("identf")
    C.idf, C.b_idf = idf, b_idf
    R.op("vector", lambda e: e.memset(C.onesf[:], 1.0), writes=[C.b_onesf])
    R.op("gpsimd", lambda e: e.affine_select(
        out=idf[:, :], in_=C.onesf[:, :], pattern=[[-1, 128]],
        compare_op=ALU.is_equal, fill=0.0, base=0, channel_multiplier=1),
        reads=[C.b_onesf], writes=[b_idf])
    R.op("vector", lambda e: e.tensor_copy(C.ident[:], idf[:]),
         reads=[b_idf], writes=[C.b_ident])
    C.ones1 = R.sbuf("c_ones1", [1, 512], BF16)
    C.b_ones1 = Buf("ones1")
    R.op("vector", lambda e: e.memset(C.ones1[:], 1.0), writes=[C.b_ones1])
    C.ps_tr = R.psum("c_ptr", [128, 2, 1024], BF16)
    C.b_ptr = [Buf(f"c_ptr{i}") for i in range(2)]
    C.ps_big = R.psum("c_pbig", [128, 4, 512], F32)
    C.b_big = [Buf(f"c_pbig{i}") for i in range(4)]
    C.ps_ab = R.psum("c_pab", [128, 2, 512], F32)
    C.b_ab = [Buf(f"c_pab{i}") for i in range(2)]
    C.cpool = R.sbuf("c_pool", [128, 8], F32)
    C.b_cpool = Buf("cpool")
    C._consts = {}

    def const(val):
        val = float(val)
        if val not in C._consts:
            idx = len(C._consts)
            assert idx < 8
            ap = C.cpool[:, idx:idx + 1]
            R.op("vector", lambda e, ap=ap, val=val: e.memset(ap, val),
                 writes=[C.b_cpool])
            R.nc.const_aps.register(F32, val, ap)
            C._consts[val] = ap
        return val
    C.const = const
    return C


def big_all(C):
    return [C.b_big[0], C.b_big[1], C.b_big[2], C.b_big[3]]



def ln_state(R: Rec, name, sq=None, b_sq=None, xln=None, b_xlnb=None):
    st = NS()
    if sq is None:
        st.sq = R.sbuf(name + "_sq", [128, 1024], BF16)
        st.b_sq = Buf(name + "_sq")
    else:
        st.sq, st.b_sq = sq, b_sq
    st.st = R.sbuf(name + "_st", [128, 6], F32)
    st.b_st = Buf(name + "_st")
    if xln is None:
        st.xln = R.sbuf(name + "_xln", [128, 1, 1024], BF16)
        st.b_xln = [Buf(name + "_xln0")]
    else:
        st.xln = xln
        st.b_xln = [b_xlnb]
    st.b_xln.append(st.b_xln[0])
    return st


def emit_ln_xlnt(R: Rec, C, st, x_ap_fn, b_x, xlnt_t, b_xlnt, n_tiles=16,
                 dump_xln=None):
    for i in range(n_tiles):
        xi = x_ap_fn(i)
        R.op("scalar", lambda e, xi=xi: e.activation(
            out=st.sq[:], in_=xi, func=AF.Square, accum_out=st.st[:, 1:2]),
            reads=[b_x], writes=[st.b_sq, st.b_st])
        R.op("vector", lambda e, xi=xi: e.reduce_sum(
            st.st[:, 0:1], xi, axis=AX.X), reads=[b_x], writes=[st.b_st])
        R.op("vector", lambda e: e.tensor_scalar_mul(st.st[:, 2:3], st.st[:, 0:1], 1.0 / 1024),
             reads=[st.b_st], writes=[st.b_st])
        R.op("vector", lambda e: e.tensor_mul(st.st[:, 3:4], st.st[:, 2:3], st.st[:, 2:3]),
             reads=[st.b_st], writes=[st.b_st])
        R.op("vector", lambda e: e.scalar_tensor_tensor(
            out=st.st[:, 3:4], in0=st.st[:, 1:2], scalar=1.0 / 1024, in1=st.st[:, 3:4],
            op0=ALU.mult, op1=ALU.subtract), reads=[st.b_st], writes=[st.b_st])
        R.op("scalar", lambda e: e.activation(
            out=st.st[:, 4:5], in_=st.st[:, 3:4], func=AF.Sqrt, bias=1e-5),
            reads=[st.b_st, C.b_cpool], writes=[st.b_st])
        R.op("vector", lambda e: e.reciprocal(st.st[:, 5:6], st.st[:, 4:5]),
             reads=[st.b_st], writes=[st.b_st])
        xo = st.xln[:, 0, :]
        R.op("vector", lambda e, xi=xi, xo=xo: e.tensor_scalar(
            out=xo, in0=xi, scalar1=st.st[:, 2:3], scalar2=st.st[:, 5:6],
            op0=ALU.subtract, op1=ALU.mult),
            reads=[b_x, st.b_st], writes=[st.b_xln[i % 2]])
        if dump_xln is not None:
            R.dma("sync", dump_xln[0][dump_xln[1], :, i, :], xo, dump_xln[2],
                  reads=[st.b_xln[i % 2]], writes=[dump_xln[3]])
        for c in range(8):
            pt = C.ps_tr[:, c % 2, 0:128]
            xoc = xo.rearrange("p (c f) -> p c f", f=128)[:, c, :]
            R.op("tensor", lambda e, xoc=xoc, pt=pt: e.transpose(pt, xoc, C.ident[:]),
                 reads=[st.b_xln[i % 2], C.b_ident], writes=[C.b_ptr[c % 2]])
            R.op("vector", lambda e, c=c, i=i, pt=pt: e.tensor_copy(
                xlnt_t[:, c, i * 128:(i + 1) * 128], pt),
                reads=[C.b_ptr[c % 2]], writes=[b_xlnt])



def emit_int_to_bf16(R: Rec, tile, b_tile, c_ld, c_st, src_ap, dst_ap,
                     rows, cols, b_dst):
    assert rows % 128 == 0
    for r in range(rows // 128):
        s = r % 2
        R.dma("gpsimd", tile[:, s, 0:cols], src_ap[r * 128:(r + 1) * 128, :],
              c_ld[s], writes=[b_tile[s]])
        R.dma("gpsimd", dst_ap[r * 128:(r + 1) * 128, :], tile[:, s, 0:cols],
              c_st[s], reads=[b_tile[s]], writes=[b_dst])



def l0ffn_state(R: Rec, name):
    st = NS()
    st.xlnt = R.sbuf(name + "_xlnt", [128, 8, 2048], BF16)
    st.b_xlnt = Buf(name + "_xlnt")
    st.wg_s = R.sbuf(name + "_wg", [128, 2, 8, 128], BF16)
    st.wu_s = R.sbuf(name + "_wu", [128, 2, 8, 128], BF16)
    st.wd_s = R.sbuf(name + "_wd", [128, 2, 1024], BF16)
    st.b_wg = [Buf(name + f"_wg{i}") for i in range(2)]
    st.b_wu = [Buf(name + f"_wu{i}") for i in range(2)]
    st.b_wd = [Buf(name + f"_wd{i}") for i in range(2)]
    st.c_gu = [R.ctr(name + f"_cgu{i}", 16) for i in range(2)]
    st.c_wd = [R.ctr(name + f"_cwd{i}", 16) for i in range(2)]
    st.h_s = R.sbuf(name + "_h", [128, 24, 1024], BF16)
    st.b_h = Buf(name + "_h")
    st.sg = R.sbuf(name + "_sg", [128, 512], F32)
    st.b_sg = Buf(name + "_sg")
    st.ln = ln_state(R, name + "_ln")
    return st


def emit_l0ffn(R: Rec, C, st, x_t, b_x, wgT_bf, wuT_bf, wdT_bf, b_w):
    S = C.const(1.0 / 127.0)
    NT, NHB = 16, 24
    emit_ln_xlnt(R, C, st.ln, lambda i: x_t[:, i, :], b_x, st.xlnt, st.b_xlnt, NT)
    ps_g, ps_u = C.ps_ab[:, 0, :], C.ps_ab[:, 1, :]
    b_pg, b_pu = C.b_ab[0], C.b_ab[1]
    ps_y = [C.ps_big[:, 0:2, :].rearrange("p a b -> p (a b)"),
            C.ps_big[:, 2:4, :].rearrange("p a b -> p (a b)")]
    b_py = [[C.b_big[0], C.b_big[1]], [C.b_big[2], C.b_big[3]]]

    def load_gu(hb, slot):
        R.dma("scalar", st.wg_s[:, slot, :, :],
              wgT_bf[:, hb * 128:(hb + 1) * 128].rearrange("(c k) h -> k c h", k=128),
              st.c_gu[slot], reads=[b_w], writes=[st.b_wg[slot]])
        R.dma("scalar", st.wu_s[:, slot, :, :],
              wuT_bf[:, hb * 128:(hb + 1) * 128].rearrange("(c k) h -> k c h", k=128),
              st.c_gu[slot], reads=[b_w], writes=[st.b_wu[slot]])

    def load_wd(hb, slot):
        R.dma("sync", st.wd_s[:, slot, :], wdT_bf[hb * 128:(hb + 1) * 128, :],
              st.c_wd[slot], reads=[b_w], writes=[st.b_wd[slot]])

    for half in range(2):
        hbase = half * 1024
        load_gu(0, 0)
        for hb in range(NHB):
            slot = hb % 2
            if hb + 1 < NHB:
                load_gu(hb + 1, (hb + 1) % 2)
            for th in range(2):
                rhs = st.xlnt[:, :, hbase + th * 512: hbase + (th + 1) * 512]

                def mm_gu(e, slot=slot, rhs=rhs):
                    for cch in range(8):
                        e.matmul(ps_g, st.wg_s[:, slot, cch, :],
                                 rhs[:, cch, :], start=(cch == 0), stop=(cch == 7))
                    for cch in range(8):
                        last = e.matmul(ps_u, st.wu_s[:, slot, cch, :],
                                        rhs[:, cch, :], start=(cch == 0), stop=(cch == 7))
                    return last
                R.op("tensor", mm_gu, reads=[st.b_wg[slot], st.b_wu[slot], st.b_xlnt],
                     writes=[b_pg, b_pu])
                R.op("scalar", lambda e: e.activation(
                    out=st.sg[:, :], in_=ps_g, func=AF.Silu, scale=S),
                    reads=[b_pg, C.b_cpool], writes=[st.b_sg])
                R.op("vector", lambda e, th=th, hb=hb: e.scalar_tensor_tensor(
                    out=st.h_s[:, hb, th * 512:(th + 1) * 512], in0=ps_u,
                    scalar=S, in1=st.sg[:, :], op0=ALU.mult, op1=ALU.mult),
                    reads=[b_pu, st.b_sg], writes=[st.b_h])
        for o in range(4):
            load_wd(0, 0)
            for hb in range(NHB):
                slot = hb % 2
                if hb + 1 < NHB:
                    load_wd(hb + 1, (hb + 1) % 2)
                for tt in range(2):
                    col = o * 256 + tt * 128

                    def mm_d(e, hb=hb, slot=slot, tt=tt, col=col):
                        e.matmul(ps_y[tt][:, 0:512], st.h_s[:, hb, col:col + 128],
                                 st.wd_s[:, slot, 0:512],
                                 start=(hb == 0), stop=(hb == NHB - 1))
                        return e.matmul(ps_y[tt][:, 512:1024],
                                        st.h_s[:, hb, col:col + 128],
                                        st.wd_s[:, slot, 512:1024],
                                        start=(hb == 0), stop=(hb == NHB - 1))
                    R.op("tensor", mm_d, reads=[st.b_h, st.b_wd[slot]],
                         writes=b_py[tt])
            for tt in range(2):
                ti = half * 8 + o * 2 + tt
                R.op("vector", lambda e, tt=tt, ti=ti: e.scalar_tensor_tensor(
                    out=x_t[:, ti, :], in0=ps_y[tt], scalar=S, in1=x_t[:, ti, :],
                    op0=ALU.mult, op1=ALU.add),
                    reads=b_py[tt] + [b_x], writes=[b_x])



def l1attn_state(R: Rec, C, name):
    st = NS()
    st.xlnt = R.sbuf(name + "_xlnt", [128, 8, 2048], BF16)
    st.b_xlnt = Buf(name + "_xlnt")
    st.v = R.sbuf(name + "_v", [128, 16, 16 * 65], BF16)
    st.b_v = Buf(name + "_v")
    st.q = R.sbuf(name + "_q", [128, 8, 2048], BF16)
    st.k = R.sbuf(name + "_k", [128, 8, 2048], BF16)
    st.b_q = Buf(name + "_q")
    st.b_k = Buf(name + "_k")
    st.w = R.sbuf(name + "_w", [128, 8, 512], BF16)
    st.b_w = [Buf(name + "_w0")]
    st.c_w = [R.ctr(name + "_cw0", 16)]
    st.att = st.w[:, :, :].rearrange("p a b -> p (a b)")[:, 0:2048].rearrange(
        "p (a b) -> p a b", a=16)
    st.b_att = st.b_w[0]
    vflat = st.v[:, :, :].rearrange("p a b -> p (a b)")
    st.q2 = vflat[:, 9216:9728]
    st.b_q2 = st.b_v
    st.nrm = vflat[0:16, 0:4096].rearrange("p (a b) -> p a b", a=2)
    st.b_nrm = st.b_v
    st.br = R.sbuf(name + "_br", [1, 2, 1024], BF16)
    st.b_br = Buf(name + "_br")
    st.c_br = R.ctr(name + "_cbr", 16)
    st.sums = R.sbuf(name + "_sums", [128, 2], F32)
    st.b_sums = Buf(name + "_sums")
    st.aop = R.sbuf(name + "_aop", [128, 128], BF16)
    st.b_aop = Buf(name + "_aop")
    st.maskT = R.sbuf(name + "_maskT", [128, 128], F32)
    st.b_maskT = Buf(name + "_maskT")
    st.BO = R.sbuf(name + "_BO", [128, 8, 16], BF16)
    st.b_BO = Buf(name + "_BO")
    st.EX = vflat[0:16, 4096:5120].rearrange("p (a b) -> p a b", a=8)
    st.b_EX = st.b_v
    st.ln = ln_state(R, name + "_ln",
                     sq=vflat[:, 7168:8192], b_sq=st.b_v,
                     xln=vflat[:, 6144:7168].rearrange("p (a b) -> p a b", a=1),
                     b_xlnb=st.b_v)
    return st


def emit_l1attn_setup(R: Rec, C, st, name):
    scrf = C.idf[:, 0:32]
    b_scrf = C.b_idf
    zer, b_zer = C.idf, C.b_idf
    R.op("vector", lambda e: e.memset(zer[:], 0.0), reads=[b_zer], writes=[b_zer])
    R.op("gpsimd", lambda e: e.affine_select(
        out=st.maskT[:, :], in_=zer[:, :], pattern=[[1, 128]],
        compare_op=ALU.is_ge, fill=-10000.0, base=0, channel_multiplier=-1),
        reads=[b_zer], writes=[st.b_maskT])
    scrf16 = zer[0:16, :]
    b_s16 = b_zer
    ones16 = C.onesf[0:16, :]
    b_o16 = C.b_onesf
    for c in range(8):
        R.op("gpsimd", lambda e, c=c: e.affine_select(
            out=scrf[:, 0:16], in_=C.onesf[:, 0:16], pattern=[[-64, 16]],
            compare_op=ALU.is_ge, fill=0.0, base=128 * c, channel_multiplier=1),
            reads=[C.b_onesf], writes=[b_scrf])
        R.op("gpsimd", lambda e, c=c: e.affine_select(
            out=scrf[:, 16:32], in_=scrf[:, 0:16], pattern=[[64, 16]],
            compare_op=ALU.is_ge, fill=0.0, base=63 - 128 * c, channel_multiplier=-1),
            reads=[b_scrf], writes=[b_scrf])
        R.op("vector", lambda e, c=c: e.tensor_copy(st.BO[:, c, :], scrf[:, 16:32]),
             reads=[b_scrf], writes=[st.b_BO])



def emit_l1attn(R: Rec, C, st, x_ap_fn, b_x, xadd_fn, WqT, WkT, WvT, WoT,
                bq, bk, bv, bo, dump=None):
    def dmp(key, sb_ap, b):
        if dump is not None:
            R.dma("sync", dump[key][dump["p"]], sb_ap, dump["c"],
                  reads=[b], writes=[dump["b"]])
    NT = 16
    ESC = C.const(1.0 / (math.sqrt(128.0) * 1.5))
    emit_ln_xlnt(R, C, st.ln, x_ap_fn, b_x, st.xlnt, st.b_xlnt, NT,
                 dump_xln=(dump or {}).get("xlnhook"))

    dmp("d_xlnt", st.xlnt[:, :, :], st.b_xlnt)
    R.dma("gpsimd", st.br[0:1, 0, :], bq[None, :], st.c_br,
          reads=[st.b_br], writes=[st.b_br])
    R.dma("gpsimd", st.br[0:1, 1, :], bk[None, :], st.c_br,
          reads=[st.b_br], writes=[st.b_br])

    def loadW(W, slot, col0):
        R.dma("gpsimd", st.w[:, :, :],
              W[:, col0:col0 + 512].rearrange("(c k) n -> k c n", k=128),
              st.c_w[0], writes=[st.b_w[0]])

    for iqk, (W, brow, dst, b_dst) in enumerate(
            [(WqT, 0, st.q, st.b_q), (WkT, 1, st.k, st.b_k)]):
        for hg in range(2):
            loadW(W, iqk, hg * 512)
            for tc in range(4):
                for c in range(4):
                    cg = hg * 4 + c
                    ps = C.ps_ab[:, iqk, :]
                    b_ps = C.b_ab[iqk]

                    def mm_qk(e, c=c, tc=tc, iqk=iqk, brow=brow, cg=cg, ps=ps):
                        for cc in range(8):
                            e.matmul(ps, st.w[:, cc, c * 128:(c + 1) * 128],
                                     st.xlnt[:, cc, tc * 512:(tc + 1) * 512],
                                     start=(cc == 0), stop=False)
                        return e.matmul(ps, st.br[0:1, brow, cg * 128:(cg + 1) * 128],
                                        C.ones1[0:1, 0:512], start=False, stop=True)
                    R.op("tensor", mm_qk, reads=[st.b_w[0], st.b_xlnt, st.b_br,
                                                 C.b_ones1], writes=[b_ps])
                    R.op("vector", lambda e, tc=tc, cg=cg, dst=dst, ps=ps: e.tensor_copy(
                        dst[:, cg, tc * 512:(tc + 1) * 512], ps),
                        reads=[b_ps], writes=[b_dst])
                    R.op("scalar", lambda e, tc=tc, cg=cg, dst=dst: e.activation(
                        out=st.q2[:, :], in_=dst[:, cg, tc * 512:(tc + 1) * 512],
                        func=AF.Square),
                        reads=[b_dst], writes=[st.b_q2])
                    R.op("tensor", lambda e, cg=cg, tc=tc: e.matmul(
                        C.ps_big[0:16, tc, :], st.BO[:, cg, :], st.q2[:, :],
                        start=(cg == 0), stop=(cg == 7)),
                        reads=[st.b_BO, st.b_q2], writes=[C.b_big[tc]])
        for tc in range(4):
            R.op("vector", lambda e, tc=tc, iqk=iqk: e.tensor_copy(
                st.nrm[:, iqk, tc * 512:(tc + 1) * 512], C.ps_big[0:16, tc, :]),
                reads=[C.b_big[tc]], writes=[st.b_nrm])
    nf = st.nrm[:, :, :].rearrange("p a b -> p (a b)")
    R.op("vector", lambda e: e.tensor_scalar_max(nf, nf, 1e-16),
         reads=[st.b_nrm], writes=[st.b_nrm])
    R.op("scalar", lambda e: e.activation(out=nf, in_=nf, func=AF.Ln),
         reads=[st.b_nrm, C.b_cpool], writes=[st.b_nrm])
    R.op("scalar", lambda e: e.activation(out=nf, in_=nf, func=AF.Exp, scale=C.const(-0.5)),
         reads=[st.b_nrm, C.b_cpool], writes=[st.b_nrm])
    scrf16 = C.idf[0:16, :]
    for c in range(8):
        R.op("gpsimd", lambda e, c=c: e.affine_select(
            out=scrf16[:, :], in_=C.onesf[0:16, :], pattern=[[1, 128]],
            compare_op=ALU.is_ge, fill=0.0, base=128 * c, channel_multiplier=-64),
            reads=[C.b_onesf, C.b_idf], writes=[C.b_idf])
        R.op("gpsimd", lambda e, c=c: e.affine_select(
            out=scrf16[:, :], in_=scrf16[:, :], pattern=[[-1, 128]],
            compare_op=ALU.is_ge, fill=0.0, base=63 - 128 * c, channel_multiplier=64),
            reads=[C.b_idf], writes=[C.b_idf])
        R.op("vector", lambda e, c=c: e.tensor_copy(st.EX[:, c, :], scrf16[:, :]),
             reads=[C.b_idf], writes=[st.b_EX])
    for iqk, (dst, b_dst) in enumerate([(st.q, st.b_q), (st.k, st.b_k)]):
        for cg in range(8):
            for tc in range(4):
                ps = C.ps_ab[:, iqk, :]
                b_ps = C.b_ab[iqk]
                R.op("tensor", lambda e, cg=cg, tc=tc, iqk=iqk, ps=ps: e.matmul(
                    ps, st.EX[:, cg, :], st.nrm[:, iqk, tc * 512:(tc + 1) * 512],
                    start=True, stop=True),
                    reads=[st.b_EX, st.b_nrm], writes=[b_ps])
                R.op("vector", lambda e, cg=cg, tc=tc, dst=dst, ps=ps: e.tensor_tensor(
                    out=dst[:, cg, tc * 512:(tc + 1) * 512],
                    in0=dst[:, cg, tc * 512:(tc + 1) * 512], in1=ps, op=ALU.mult),
                    reads=[b_ps, b_dst], writes=[b_dst])

    dmp("d_q", st.q[:, :, :], st.b_q)
    dmp("d_k", st.k[:, :, :], st.b_k)
    R.dma("gpsimd", st.br[0:1, 0, :], bv[None, :], st.c_br,
          reads=[st.b_br], writes=[st.b_br])
    R.op("vector", lambda e: e.memset(
        st.v[:, :, :].rearrange("p a (h w) -> p a h w", w=65)[:, :, :, 64:65], 1.0),
        reads=[st.b_v], writes=[st.b_v])
    for nh in range(2):
        loadW(WvT, 0, nh * 512)
        for t in range(NT):
            ps = C.ps_ab[:, t % 2, :]
            b_ps = C.b_ab[t % 2]

            def mm_v(e, t=t, nh=nh, ps=ps):
                for cc in range(8):
                    e.matmul(ps, st.xlnt[:, cc, t * 128:(t + 1) * 128],
                             st.w[:, cc, :], start=(cc == 0), stop=False)
                return e.matmul(ps, C.ones1[0:1, 0:128],
                                st.br[0:1, 0, nh * 512:(nh + 1) * 512],
                                start=False, stop=True)
            R.op("tensor", mm_v, reads=[st.b_w[0], st.b_xlnt, C.b_ones1, st.b_br],
                 writes=[b_ps])
            vout = st.v[:, t, :].rearrange("p (h w) -> p h w", w=65)[
                :, nh * 8:(nh + 1) * 8, 0:64]
            R.op("vector", lambda e, vout=vout, ps=ps: e.tensor_copy(
                vout, ps.rearrange("p (h w) -> p h w", w=64)),
                reads=[b_ps], writes=[st.b_v])

    dmp("d_v", st.v[:, :, :], st.b_v)
    aoT, b_aoT = st.xlnt, st.b_xlnt
    scT = C.ps_big[:, :, :].rearrange("p a b -> p (a b)")
    for hp in range(8):
        for qt in range(NT):
            ntk = qt + 1
            for sub in range(2):
                h = 2 * hp + sub
                qs = st.q[sub * 64:sub * 64 + 64, hp, :]
                ks = st.k[sub * 64:sub * 64 + 64, hp, :]

                def mm_sc(e, qs=qs, ks=ks, qt=qt, ntk=ntk):
                    last = None
                    for j in range(ntk):
                        last = e.matmul(scT[:, j * 128:(j + 1) * 128],
                                        ks[:, j * 128:(j + 1) * 128],
                                        qs[:, qt * 128:(qt + 1) * 128],
                                        start=True, stop=True)
                    return last
                R.op("tensor", mm_sc, reads=[st.b_q, st.b_k], writes=big_all(C))
                R.op("vector", lambda e, qt=qt: e.tensor_tensor(
                    out=scT[:, qt * 128:(qt + 1) * 128],
                    in0=scT[:, qt * 128:(qt + 1) * 128], in1=st.maskT[:, :],
                    op=ALU.add),
                    reads=[st.b_maskT] + big_all(C), writes=big_all(C))
                R.op("scalar", lambda e, ntk=ntk: e.activation(
                    out=st.att[:, 0:ntk, :].rearrange("p a b -> p (a b)"),
                    in_=scT[:, 0:ntk * 128], func=AF.Exp, scale=ESC),
                    reads=big_all(C) + [C.b_cpool], writes=[st.b_att])

                ps = C.ps_ab[:, sub, :]
                b_ps = C.b_ab[sub]

                def mm_ao(e, h=h, ntk=ntk, ps=ps):
                    last = None
                    for j in range(ntk):
                        last = e.matmul(ps[:, 0:65], st.att[:, j, :],
                                        st.v[:, j, h * 65:(h + 1) * 65],
                                        start=(j == 0), stop=(j == ntk - 1))
                    return last
                R.op("tensor", mm_ao, reads=[st.b_att, st.b_v], writes=[b_ps])
                R.op("vector", lambda e, sub=sub, ps=ps: e.reciprocal(
                    st.sums[:, sub:sub + 1], ps[:, 64:65]),
                    reads=[b_ps], writes=[st.b_sums])
                R.op("vector", lambda e, sub=sub, ps=ps: e.tensor_scalar(
                    out=st.aop[:, sub * 64:sub * 64 + 64], in0=ps[:, 0:64],
                    scalar1=st.sums[:, sub:sub + 1], scalar2=None, op0=ALU.mult),
                    reads=[b_ps, st.b_sums], writes=[st.b_aop])
            R.op("tensor", lambda e, qt=qt: e.transpose(
                C.ps_tr[:, qt % 2, 0:128], st.aop[:, :], C.ident[:]),
                reads=[st.b_aop, C.b_ident], writes=[C.b_ptr[qt % 2]])
            R.op("vector", lambda e, hp=hp, qt=qt: e.tensor_copy(
                aoT[:, hp, qt * 128:(qt + 1) * 128], C.ps_tr[:, qt % 2, 0:128]),
                reads=[C.b_ptr[qt % 2]], writes=[b_aoT])

    dmp("d_ao", st.xlnt[:, :, :], st.b_xlnt)
    R.dma("gpsimd", st.br[0:1, 0, :], bo[None, :], st.c_br,
          reads=[st.b_br], writes=[st.b_br])
    for nh in range(2):
        loadW(WoT, nh % 2, nh * 512)
        for t in range(NT):
            ps = C.ps_ab[:, t % 2, :]
            b_ps = C.b_ab[t % 2]

            def mm_o(e, t=t, nh=nh, ps=ps):
                for cc in range(8):
                    e.matmul(ps, aoT[:, cc, t * 128:(t + 1) * 128],
                             st.w[:, cc, :], start=(cc == 0), stop=False)
                return e.matmul(ps, C.ones1[0:1, 0:128],
                                st.br[0:1, 0, nh * 512:(nh + 1) * 512],
                                start=False, stop=True)
            R.op("tensor", mm_o, reads=[st.b_w[0], b_aoT, C.b_ones1, st.b_br],
                 writes=[b_ps])
            xadd_fn(t, nh, ps, b_ps)



def l1ffn_pre_state(R: Rec, name):
    st = NS()
    st.cent = R.sbuf(name + "_cent", [128, 8, 8], F32)
    st.b_cent = Buf(name + "_cent")
    st.xf = R.sbuf(name + "_xf", [128, 1024], F32)
    st.b_xf = Buf(name + "_xf")
    st.iot = R.sbuf(name + "_iot", [128, 8], F32)
    st.b_iot = Buf(name + "_iot")
    st.ftT = R.sbuf(name + "_ftT", [128, 8, 128], F32)
    st.b_ftT = Buf(name + "_ftT")
    st.sc = R.sbuf(name + "_sc", [128, 5 * 8], F32)
    st.b_sc = Buf(name + "_sc")
    st.fhat = R.sbuf(name + "_fhat", [128, 1024], BF16)
    st.b_fhat = Buf(name + "_fhat")
    st.fhT = R.sbuf(name + "_fhT", [128, 8, 128], BF16)
    st.b_fhT = Buf(name + "_fhT")
    st.c_f = R.ctr(name + "_cf", 16)
    st.base = R.sbuf(name + "_base", [128, 2, 1024], F32)
    st.b_base = [Buf(name + f"_base{i}") for i in range(2)]
    st.c_ld = R.ctr(name + "_cld", 16)
    st.ln = ln_state(R, name + "_ln")
    return st


def l1ffn_main_state(R: Rec, name):
    st = NS()
    st.hrT = R.sbuf(name + "_hrT", [128, 32, 1024], BF16)
    st.b_hrT = Buf(name + "_hrT")
    st.he = R.sbuf(name + "_he", [128, 8, 1024], BF16)
    st.b_he = Buf(name + "_he")
    st.fsl = R.sbuf(name + "_fsl", [128, 8, 1024], BF16)
    st.b_fsl = Buf(name + "_fsl")
    st.c_fsl = R.ctr(name + "_cfsl", 16)
    st.w1 = R.sbuf(name + "_w1", [128, 3, 8, 128], BF16)
    st.b_w1 = [Buf(name + f"_w1{i}") for i in range(3)]
    st.c_w1 = [R.ctr(name + f"_cw1{i}", 16) for i in range(3)]
    st.w2 = R.sbuf(name + "_w2", [128, 8, 512], BF16)
    st.b_w2 = [Buf(name + f"_w2{i}") for i in range(8)]
    st.c_w2 = [R.ctr(name + f"_cw2{i}", 16) for i in range(8)]
    st.stage = R.sbuf(name + "_stage", [128, 8, 512], F32)
    st.b_stage = [Buf(name + f"_stage{i}") for i in range(8)]
    st.c_acc = R.ctr(name + "_cacc", 16)
    st.gb = R.sbuf(name + "_gb", [128, 2, 8, 8], F32)
    st.b_gb = Buf(name + "_gb")
    st.b1 = R.sbuf(name + "_b1", [128, 32, 8], F32)
    st.b_b1 = Buf(name + "_b1")
    st.b2r = R.sbuf(name + "_b2r", [1, 8, 1024], BF16)
    st.b_b2r = Buf(name + "_b2r")
    st.c_ld = R.ctr(name + "_cld2", 16)
    return st


def emit_l1ffn_pre_setup(R: Rec, C, st, centT):
    R.dma("gpsimd", st.cent[:, :, :],
          centT.rearrange("(c k) e -> k c e", k=128), st.c_ld,
          reads=[st.b_cent], writes=[st.b_cent])
    R.op("gpsimd", lambda e: e.iota(st.iot[:, :], pattern=[[1, 8]], base=0,
                                    channel_multiplier=0,
                                    allow_small_or_imprecise_dtypes=True),
         reads=[st.b_iot], writes=[st.b_iot])
    R.op("vector", lambda e: e.memset(st.sc[:, 32:40], 0.0),
         reads=[st.b_sc], writes=[st.b_sc])


def emit_l1ffn_main_setup(R: Rec, C, st, gbT, b1T, b2):
    R.dma("gpsimd", st.gb[:, :, :, :],
          gbT.rearrange("(c k) g e -> k g c e", k=128), st.c_ld, writes=[st.b_gb])
    R.dma("gpsimd", st.b1[:, :, :],
          b1T.rearrange("(t k) e -> k t e", k=128), st.c_ld, writes=[st.b_b1])
    R.dma("gpsimd", st.b2r[0:1, :, :], b2[None, :, :], st.c_ld, writes=[st.b_b2r])


def emit_l1ffn_pre(R: Rec, C, st, x_ap_fn, b_x, out_hbm, b_out, c_out, toff,
                   fhat_d, b_fd, aE, b_aE):
    lnst = st.ln
    R.op("gpsimd", lambda e: e.affine_select(
        out=C.idf[:, :], in_=C.onesf[:, :], pattern=[[-1, 128]],
        compare_op=ALU.is_equal, fill=0.0, base=0, channel_multiplier=1),
        reads=[C.b_onesf, C.b_idf], writes=[C.b_idf])
    for i in range(16):
        ti = toff + i
        xi = x_ap_fn(i)
        R.op("scalar", lambda e, xi=xi: e.activation(
            out=lnst.sq[:], in_=xi, func=AF.Square, accum_out=lnst.st[:, 1:2]),
            reads=[b_x], writes=[lnst.b_sq, lnst.b_st])
        R.op("vector", lambda e, xi=xi: e.reduce_sum(
            lnst.st[:, 0:1], xi, axis=AX.X), reads=[b_x], writes=[lnst.b_st])
        R.op("vector", lambda e: e.tensor_scalar_mul(
            lnst.st[:, 2:3], lnst.st[:, 0:1], 1.0 / 1024),
            reads=[lnst.b_st], writes=[lnst.b_st])
        R.op("vector", lambda e: e.tensor_mul(
            lnst.st[:, 3:4], lnst.st[:, 2:3], lnst.st[:, 2:3]),
            reads=[lnst.b_st], writes=[lnst.b_st])
        R.op("vector", lambda e: e.scalar_tensor_tensor(
            out=lnst.st[:, 3:4], in0=lnst.st[:, 1:2], scalar=1.0 / 1024,
            in1=lnst.st[:, 3:4], op0=ALU.mult, op1=ALU.subtract),
            reads=[lnst.b_st], writes=[lnst.b_st])
        R.op("scalar", lambda e: e.activation(
            out=lnst.st[:, 4:5], in_=lnst.st[:, 3:4], func=AF.Sqrt, bias=1e-5),
            reads=[lnst.b_st, C.b_cpool], writes=[lnst.b_st])
        R.op("vector", lambda e: e.reciprocal(lnst.st[:, 5:6], lnst.st[:, 4:5]),
             reads=[lnst.b_st], writes=[lnst.b_st])
        xo = lnst.xln[:, 0, :]
        R.op("vector", lambda e, xi=xi: e.tensor_scalar(
            out=st.xf[:, :], in0=xi, scalar1=lnst.st[:, 2:3], scalar2=lnst.st[:, 5:6],
            op0=ALU.subtract, op1=ALU.mult),
            reads=[b_x, lnst.b_st], writes=[st.b_xf])
        bslot = i % 2
        R.op("vector", lambda e, xi=xi, bslot=bslot: e.tensor_add(
            st.base[:, bslot, :], xi, st.xf[:, :]),
            reads=[b_x, st.b_xf], writes=[st.b_base[bslot]])
        R.dma("sync", out_hbm[ti * 128:(ti + 1) * 128, :], st.base[:, bslot, :],
              c_out, reads=[st.b_base[bslot]], writes=[b_out[ti]])
        R.op("scalar", lambda e: e.activation(
            out=lnst.sq[:], in_=st.xf[:, :], func=AF.Square, accum_out=lnst.st[:, 1:2]),
            reads=[st.b_xf], writes=[lnst.b_sq, lnst.b_st])
        R.op("vector", lambda e: e.reduce_sum(
            lnst.st[:, 0:1], st.xf[:, :], axis=AX.X), reads=[st.b_xf], writes=[lnst.b_st])
        R.op("vector", lambda e: e.tensor_scalar_mul(
            lnst.st[:, 2:3], lnst.st[:, 0:1], 1.0 / 1024),
            reads=[lnst.b_st], writes=[lnst.b_st])
        R.op("vector", lambda e: e.tensor_mul(
            lnst.st[:, 3:4], lnst.st[:, 2:3], lnst.st[:, 2:3]),
            reads=[lnst.b_st], writes=[lnst.b_st])
        R.op("vector", lambda e: e.scalar_tensor_tensor(
            out=lnst.st[:, 3:4], in0=lnst.st[:, 1:2], scalar=1.0 / 1024,
            in1=lnst.st[:, 3:4], op0=ALU.mult, op1=ALU.subtract),
            reads=[lnst.b_st], writes=[lnst.b_st])
        R.op("scalar", lambda e: e.activation(
            out=lnst.st[:, 4:5], in_=lnst.st[:, 3:4], func=AF.Sqrt, bias=1e-5),
            reads=[lnst.b_st, C.b_cpool], writes=[lnst.b_st])
        R.op("vector", lambda e: e.reciprocal(lnst.st[:, 5:6], lnst.st[:, 4:5]),
             reads=[lnst.b_st], writes=[lnst.b_st])
        R.op("vector", lambda e: e.tensor_scalar(
            out=st.fhat[:, :], in0=st.xf[:, :], scalar1=lnst.st[:, 2:3],
            scalar2=lnst.st[:, 5:6], op0=ALU.subtract, op1=ALU.mult),
            reads=[st.b_xf, lnst.b_st], writes=[st.b_fhat])
        for c in range(8):
            pt = C.ps_tr[:, c % 2, 0:128]
            R.op("tensor", lambda e, c=c, pt=pt: e.transpose(
                pt, st.fhat[:, c * 128:(c + 1) * 128], C.ident[:]),
                reads=[st.b_fhat, C.b_ident], writes=[C.b_ptr[c % 2]])
            R.op("vector", lambda e, c=c, pt=pt: e.tensor_copy(
                st.fhT[:, c, :], pt),
                reads=[C.b_ptr[c % 2]], writes=[st.b_fhT])
        R.dma("sync", fhat_d[:, :, ti * 128:(ti + 1) * 128], st.fhT[:, :, :],
              st.c_f, reads=[st.b_fhT], writes=[b_fd])
        for c in range(8):
            pt = C.ps_big[:, c % 2, 0:128]
            R.op("tensor", lambda e, c=c, pt=pt: e.transpose(
                pt, st.xf[:, c * 128:(c + 1) * 128], C.idf[:, :]),
                reads=[st.b_xf, C.b_idf], writes=[C.b_big[c % 2]])
            R.op("vector", lambda e, c=c, pt=pt: e.tensor_copy(
                st.ftT[:, c, :], pt), reads=[C.b_big[c % 2]], writes=[st.b_ftT])
        def mm_s(e):
            last = None
            for c in range(8):
                last = e.matmul(C.ps_ab[:, 0, 0:8], st.ftT[:, c, :],
                                st.cent[:, c, :], start=(c == 0), stop=(c == 7))
            return last
        R.op("tensor", mm_s, reads=[st.b_ftT, st.b_cent], writes=[C.b_ab[0]])
        R.op("vector", lambda e: e.tensor_copy(st.sc[:, 0:8], C.ps_ab[:, 0, 0:8]),
             reads=[C.b_ab[0]], writes=[st.b_sc])
        R.op("vector", lambda e: e.reduce_max(
            st.sc[:, 8:9], st.sc[:, 0:8], axis=AX.X), reads=[st.b_sc], writes=[st.b_sc])
        R.op("vector", lambda e: e.tensor_scalar(
            out=st.sc[:, 16:24], in0=st.sc[:, 0:8], scalar1=st.sc[:, 8:9],
            scalar2=None, op0=ALU.is_equal), reads=[st.b_sc], writes=[st.b_sc])
        R.op("vector", lambda e: e.tensor_tensor_scan(
            out=st.sc[:, 24:32], data0=st.sc[:, 16:24], data1=st.sc[:, 32:40],
            initial=0.0, op0=ALU.add, op1=ALU.add),
            reads=[st.b_sc], writes=[st.b_sc])
        R.op("vector", lambda e: e.tensor_scalar(
            out=st.sc[:, 24:32], in0=st.sc[:, 24:32], scalar1=1.0, scalar2=None,
            op0=ALU.is_equal), reads=[st.b_sc], writes=[st.b_sc])
        R.op("vector", lambda e: e.tensor_mul(
            st.sc[:, 16:24], st.sc[:, 16:24], st.sc[:, 24:32]),
            reads=[st.b_sc], writes=[st.b_sc])
        R.op("scalar", lambda e: e.activation(
            out=st.sc[:, 9:10], in_=st.sc[:, 8:9], func=AF.Sigmoid),
            reads=[st.b_sc], writes=[st.b_sc])
        R.op("vector", lambda e, ti=ti: e.tensor_scalar(
            out=aE[:, ti, :], in0=st.sc[:, 16:24], scalar1=st.sc[:, 9:10],
            scalar2=None, op0=ALU.mult), reads=[st.b_sc, b_aE], writes=[b_aE])


def emit_l1ffn_main(R: Rec, C, st, W1B, W2B, b_wbf, out_hbm, b_out, fhat_d, b_fd,
                    aE, b_aE):
    ps_y = [C.ps_big[:, 0, :], C.ps_big[:, 1, :], C.ps_big[:, 2, :], C.ps_big[:, 3, :],
            C.ps_ab[:, 0, :], C.ps_ab[:, 1, :],
            C.ps_tr[:, 0, :].bitcast(F32), C.ps_tr[:, 1, :].bitcast(F32)]
    b_y = [C.b_big[0], C.b_big[1], C.b_big[2], C.b_big[3],
           C.b_ab[0], C.b_ab[1], C.b_ptr[0], C.b_ptr[1]]

    BW1, BW2 = b_wbf

    def load_w1(e, nt):
        slot = nt % 3
        R.dma("sync", st.w1[:, slot, :, :],
              W1B[e][nt * 128:(nt + 1) * 128, :].rearrange("k (c n) -> k c n", c=8),
              st.c_w1[slot], reads=[BW1[e][nt // 8]], writes=[st.b_w1[slot]])

    def load_w2(e, nt, dh):
        slot = nt % 8
        R.dma("scalar", st.w2[:, slot, :],
              W2B[e][nt * 128:(nt + 1) * 128, dh * 512:(dh + 1) * 512],
              st.c_w2[slot], reads=[BW2[e][nt // 8]], writes=[st.b_w2[slot]])

    def emit_fslhe(e, tg):
        R.dma("gpsimd", st.fsl[:, :, :], fhat_d[:, :, tg * 1024:(tg + 1) * 1024],
              st.c_fsl, reads=[b_fd, st.b_fsl], writes=[st.b_fsl])
        for c in range(8):
            R.op("vector", lambda en, c=c, e=e: en.tensor_scalar(
                out=st.he[:, c, :], in0=st.fsl[:, c, :],
                scalar1=st.gb[:, 0, c, e:e + 1], scalar2=st.gb[:, 1, c, e:e + 1],
                op0=ALU.mult, op1=ALU.add),
                reads=[st.b_fsl, st.b_gb], writes=[st.b_he])

    for k in range(3):
        load_w1(0, k)
    for k in range(8):
        load_w2(0, k, 0)
    emit_fslhe(0, 0)
    for e in range(8):
        for tg in range(4):
            for nt in range(32):
                slot = nt % 3
                for tc in range(2):
                    pi = (nt * 2 + tc) % 4
                    ps = ps_y[pi]
                    b_ps = b_y[pi]

                    def mm1(en, slot=slot, tc=tc, ps=ps):
                        last = None
                        for cc in range(8):
                            last = en.matmul(ps, st.w1[:, slot, cc, :],
                                             st.he[:, cc, tc * 512:(tc + 1) * 512],
                                             start=(cc == 0), stop=(cc == 7))
                        return last
                    R.op("tensor", mm1, reads=[st.b_w1[slot], st.b_he], writes=[b_ps])
                    R.op("vector", lambda en, nt=nt, tc=tc, e=e, ps=ps: en.tensor_scalar(
                        out=st.hrT[:, nt, tc * 512:(tc + 1) * 512], in0=ps,
                        scalar1=st.b1[:, nt, e:e + 1], scalar2=0.0,
                        op0=ALU.add, op1=ALU.max),
                        reads=[b_ps, st.b_b1], writes=[st.b_hrT])
                if nt + 3 < 32:
                    load_w1(e, nt + 3)
            for dh in range(2):
                for nt in range(32):
                    slot = nt % 8
                    for tt in range(8):
                        def mm2(en, nt=nt, slot=slot, tt=tt, e=e, dh=dh):
                            ins = en.matmul(ps_y[tt], st.hrT[:, nt, tt * 128:(tt + 1) * 128],
                                            st.w2[:, slot, :],
                                            start=(nt == 0), stop=False)
                            if nt == 31:
                                ins = en.matmul(ps_y[tt],
                                                C.ones1[0:1, 0:128],
                                                st.b2r[0:1, e, dh * 512:(dh + 1) * 512],
                                                start=False, stop=True)
                            return ins
                        R.op("tensor", mm2,
                             reads=[st.b_hrT, st.b_w2[slot], C.b_ones1, st.b_b2r],
                             writes=[b_y[tt]])
                    if nt + 8 < 32:
                        load_w2(e, nt + 8, dh)
                    elif dh == 0 and nt >= 24:
                        load_w2(e, nt - 24, 1)
                ne, ntg = (e, tg + 1) if tg < 3 else (e + 1, 0)
                if dh == 0:
                    if ne < 8:
                        emit_fslhe(ne, ntg)
                if dh == 1:
                    if ne < 8:
                        for k in range(3):
                            load_w1(ne, k)
                        for k in range(8):
                            load_w2(ne, k, 0)
                for tt in range(8):
                    ti = tg * 8 + tt
                    R.op("scalar", lambda en, tt=tt, ti=ti, e=e:
                         en.activation(
                             out=st.stage[:, tt, :],
                             in_=ps_y[tt], func=AF.Copy,
                             scale=aE[:, ti, e:e + 1]),
                         reads=[b_y[tt], b_aE], writes=[st.b_stage[tt]])
                    R.dma("gpsimd",
                          out_hbm[ti * 128:(ti + 1) * 128, dh * 512:(dh + 1) * 512],
                          st.stage[:, tt, :],
                          st.c_acc, accum_op=ALU.add,
                          reads=[st.b_stage[tt]], writes=[b_out[ti]])



def l0attn_state(R: Rec, name):
    st = NS()
    st.q = R.sbuf(name + "_q", [128, 8, 2048], BF16)
    st.k = R.sbuf(name + "_k", [128, 8, 2048], BF16)
    st.v = R.sbuf(name + "_v", [128, 8, 2048], BF16)
    st.gstg = R.sbuf(name + "_gstg", [128, 2, 512], BF16)
    st.b_q, st.b_k = Buf(name + "q"), Buf(name + "k")
    st.b_v, st.b_g = Buf(name + "v"), Buf(name + "gg")
    st.c_g = R.ctr(name + "_cg", 16)
    st.rowsA = R.sbuf(name + "_rowsA", [96, 2048], F32)
    st.rowsB = R.sbuf(name + "_rowsB", [96, 2048], F32)
    st.b_rA, st.b_rB = Buf(name + "rA"), Buf(name + "rB")
    st.cols = R.sbuf(name + "_cols", [128, 2, 80], F32)
    st.b_cols = Buf(name + "cols")
    st.fscr = R.sbuf(name + "_fscr", [128, 1024], F32)
    st.b_fscr = Buf(name + "fscr")
    st.gA = R.sbuf(name + "_gA", [128, 2, 8, 64], F32)
    st.betA = R.sbuf(name + "_betA", [128, 2, 8, 64], F32)
    st.egA = R.sbuf(name + "_egA", [128, 8, 64], BF16)
    st.b_gA = [Buf(name + f"gA{i}") for i in range(2)]
    st.b_betA = [Buf(name + f"bA{i}") for i in range(2)]
    st.b_egA = Buf(name + "eA")
    st.c_bc = R.ctr(name + "_cbc", 16)
    st.LTf = st.fscr[:, 0:512].rearrange("p (a b) -> p a b", b=64)
    st.b_LTf = st.b_fscr
    st.LT = R.sbuf(name + "_LT", [128, 3, 8, 64], BF16)
    st.b_LT = Buf(name + "LT")
    st.mask = R.sbuf(name + "_mask", [128, 3, 64], BF16)
    st.b_mask = Buf(name + "mask")
    st.P = R.sbuf(name + "_P", [128, 2, 8, 64], BF16)
    st.b_P = Buf(name + "P")
    st.pool = R.sbuf(name + "_pool", [128, 8, 576], BF16)
    st.b_pool = Buf(name + "pool")
    st.w = st.pool[:, :, 0:512]
    st.b_w = st.b_pool
    st.U = st.pool[:, :, 0:128]
    st.b_U = st.b_pool
    st.B = st.pool[:, :, 128:256]
    st.b_B = st.b_pool
    st.attnT = st.pool[:, :, 256:320]
    st.b_attnT = st.b_pool
    st.kb = st.pool[:, :, 320:384]
    st.b_kb = st.b_pool
    st.kd = st.pool[:, :, 384:448]
    st.b_kd = st.b_pool
    st.kcdT = st.pool[:, :, 448:512]
    st.b_kcdT = st.b_pool
    st.vn = st.pool[:, :, 512:576]
    st.b_vn = st.b_pool
    st.S32 = R.sbuf(name + "_S32", [128, 8, 64], F32)
    st.b_S32 = Buf(name + "S32")
    st.Sbf = R.sbuf(name + "_Sbf", [128, 8, 64], BF16)
    st.b_Sbf = Buf(name + "Sbf")
    st.c_w = R.ctr(name + "_cw", 16)
    st.q2 = R.sbuf(name + "_q2", [128, 1024], BF16)
    st.b_q2 = Buf(name + "q2")
    st.nrm = None
    st.b_nrm = None
    st.BO = R.sbuf(name + "_BO", [128, 8, 16], BF16)
    st.b_BO = Buf(name + "BO")
    st.EX = R.sbuf(name + "_EX", [16, 8, 128], BF16)
    st.b_EX = Buf(name + "EX")
    st.nrm = st.rowsA[0:16, :].bitcast(BF16)[:, 0:2048]
    st.b_nrm = st.b_rA
    st.smalls = R.sbuf(name + "_smalls", [128, 4], F32)
    st.b_smalls = Buf(name + "smalls")
    st.c_sm = R.ctr(name + "_csm", 16)
    st.ln = ln_state(R, name + "_ln", sq=st.q2[:, :], b_sq=st.b_q2,
                     xln=st.v[:, 0, 0:1024].rearrange("p (a b) -> p a b", a=1),
                     b_xlnb=st.b_v)
    return st


def emit_l0attn_setup(R: Rec, C, st):
    scrf = C.idf[:, 0:32]
    for c in range(8):
        R.op("gpsimd", lambda e, c=c: e.affine_select(
            out=scrf[:, 0:16], in_=C.onesf[:, 0:16], pattern=[[-64, 16]],
            compare_op=ALU.is_ge, fill=0.0, base=128 * c, channel_multiplier=1),
            reads=[C.b_onesf, C.b_idf], writes=[C.b_idf])
        R.op("gpsimd", lambda e, c=c: e.affine_select(
            out=scrf[:, 16:32], in_=scrf[:, 0:16], pattern=[[64, 16]],
            compare_op=ALU.is_ge, fill=0.0, base=63 - 128 * c, channel_multiplier=-1),
            reads=[C.b_idf], writes=[C.b_idf])
        R.op("vector", lambda e, c=c: e.tensor_copy(st.BO[:, c, :], scrf[:, 16:32]),
             reads=[C.b_idf], writes=[st.b_BO])
    s16 = C.idf[0:16, :]
    for c in range(8):
        R.op("gpsimd", lambda e, c=c: e.affine_select(
            out=s16[:, :], in_=C.onesf[0:16, :], pattern=[[1, 128]],
            compare_op=ALU.is_ge, fill=0.0, base=128 * c, channel_multiplier=-64),
            reads=[C.b_onesf, C.b_idf], writes=[C.b_idf])
        R.op("gpsimd", lambda e, c=c: e.affine_select(
            out=s16[:, :], in_=s16[:, :], pattern=[[-1, 128]],
            compare_op=ALU.is_ge, fill=0.0, base=63 - 128 * c, channel_multiplier=64),
            reads=[C.b_idf], writes=[C.b_idf])
        R.op("vector", lambda e, c=c: e.tensor_copy(st.EX[:, c, :], s16[:, :]),
             reads=[C.b_idf], writes=[st.b_EX])
    for mi, (cm_sign, b0) in enumerate([(1, 0), (1, 1), (-1, 1)]):
        for half in range(2):
            out = C.idf[half * 64:(half + 1) * 64, 0:64]
            R.op("gpsimd", lambda e, cm_sign=cm_sign, b0=b0, half=half, out=out:
                 e.affine_select(
                     out=out, in_=C.onesf[half * 64:(half + 1) * 64, 0:64],
                     pattern=[[cm_sign, 64]], compare_op=ALU.is_ge, fill=0.0,
                     base=-b0, channel_multiplier=-cm_sign),
                 reads=[C.b_onesf, C.b_idf], writes=[C.b_idf])
        R.op("vector", lambda e, mi=mi: e.tensor_copy(st.mask[:, mi, :], C.idf[:, 0:64]),
             reads=[C.b_idf], writes=[st.b_mask])


def emit_l0attn(R: Rec, C, st, x_ap_fn, b_xsrc, xadd_fn, o_fm, b_o, xlnt, b_xlnt,
                Wq, Wk, Wv, Wg, Wo, Wb, Wa, dt_bias, A_log, norm_w,
                g_dram, beta_dram, b_gd, gate_d, b_gt, dump=None, n_chunks=32,
                hook=None):
    nc = R.nc
    gR = st.rowsA[0:16, :]
    betaR = st.rowsA[32:48, :]
    eglgR = st.rowsA[64:80, :]
    sbgR = st.rowsB[0:16, :]
    eglR = st.rowsB[32:48, :]
    arawR = st.rowsB[64:80, :]
    b_rA, b_rB = st.b_rA, st.b_rB

    R.op("gpsimd", lambda e: e.affine_select(
        out=C.idf[:, :], in_=C.onesf[:, :], pattern=[[-1, 128]],
        compare_op=ALU.is_equal, fill=0.0, base=0, channel_multiplier=1),
        reads=[C.b_onesf, C.b_idf], writes=[C.b_idf])

    R.dma("sync", st.smalls[0:16, 0:1], dt_bias[:, None], st.c_sm,
          reads=[st.b_smalls], writes=[st.b_smalls], allow_slow_non_contiguous=True)
    R.dma("sync", st.smalls[0:16, 1:2], A_log[:, None], st.c_sm,
          reads=[st.b_smalls], writes=[st.b_smalls], allow_slow_non_contiguous=True)
    R.dma("sync", st.smalls[0:64, 2:3], norm_w[:, None], st.c_sm,
          reads=[st.b_smalls], writes=[st.b_smalls], allow_slow_non_contiguous=True)
    R.dma("sync", st.smalls[64:128, 2:3], norm_w[:, None], st.c_sm,
          reads=[st.b_smalls], writes=[st.b_smalls], allow_slow_non_contiguous=True)
    R.op("scalar", lambda e: e.activation(
        out=st.smalls[0:16, 3:4], in_=st.smalls[0:16, 1:2], func=AF.Exp),
        reads=[st.b_smalls], writes=[st.b_smalls])
    R.op("vector", lambda e: e.tensor_scalar_mul(
        st.smalls[0:16, 3:4], st.smalls[0:16, 3:4], -1.0),
        reads=[st.b_smalls], writes=[st.b_smalls])

    emit_ln_xlnt(R, C, st.ln, x_ap_fn, b_xsrc, xlnt, b_xlnt, 16)

    def loadW(W, col0, ncol=512):
        R.dma("gpsimd", st.w[:, :, 0:ncol],
              W[:, col0:col0 + ncol].rearrange("(c k) n -> k c n", k=128),
              st.c_w, reads=[st.b_w], writes=[st.b_w])

    for (W, dst, b_dst) in [(Wq, st.q, st.b_q), (Wk, st.k, st.b_k),
                            (Wv, st.v, st.b_v), (Wg, None, st.b_g)]:
        for hg in range(2):
            loadW(W, hg * 512)
            for tc in range(4):
                for c in range(4):
                    cg = hg * 4 + c
                    ps = C.ps_ab[:, c % 2, :]
                    b_ps = C.b_ab[c % 2]

                    def mm_p(e, c=c, tc=tc, ps=ps):
                        last = None
                        for cc in range(8):
                            last = e.matmul(ps, st.w[:, cc, c * 128:(c + 1) * 128],
                                            xlnt[:, cc, tc * 512:(tc + 1) * 512],
                                            start=(cc == 0), stop=(cc == 7))
                        return last
                    R.op("tensor", mm_p, reads=[st.b_w, b_xlnt], writes=[b_ps])
                    if dst is None:
                        gs = (tc + cg) % 2
                        R.op("vector", lambda e, gs=gs, ps=ps: e.tensor_copy(
                            st.gstg[:, gs, :], ps),
                            reads=[b_ps], writes=[st.b_g])
                        R.dma("scalar", gate_d[:, cg, tc * 512:(tc + 1) * 512],
                              st.gstg[:, gs, :], st.c_g,
                              reads=[st.b_g], writes=[b_gt])
                    else:
                        R.op("vector", lambda e, tc=tc, cg=cg, dst=dst, ps=ps:
                             e.tensor_copy(dst[:, cg, tc * 512:(tc + 1) * 512], ps),
                             reads=[b_ps], writes=[b_dst])

    for (W, outrow, b_or) in [(Wb, st.rowsB[64:80, :], b_rB),
                              (Wa, st.rowsA[64:80, :], b_rA)]:
        R.dma("gpsimd", st.w[:, :, 0:16],
              W.rearrange("(c k) n -> k c n", k=128), st.c_w,
              reads=[st.b_w], writes=[st.b_w])
        for tc in range(4):
            ps = C.ps_ab[0:16, 0, :]

            def mm_ba(e, tc=tc, ps=ps):
                last = None
                for cc in range(8):
                    last = e.matmul(ps, st.w[:, cc, 0:16],
                                    xlnt[:, cc, tc * 512:(tc + 1) * 512],
                                    start=(cc == 0), stop=(cc == 7))
                return last
            R.op("tensor", mm_ba, reads=[st.b_w, b_xlnt], writes=[C.b_ab[0]])
            R.op("vector", lambda e, tc=tc, outrow=outrow, ps=ps: e.tensor_copy(
                outrow[:, tc * 512:(tc + 1) * 512], ps),
                reads=[C.b_ab[0]], writes=[b_or])

    for iqk, (dst, b_dst) in enumerate([(st.q, st.b_q), (st.k, st.b_k)]):
        nfi = st.nrm[:, :]
        for tc in range(4):
            for cg in range(8):
                R.op("scalar", lambda e, tc=tc, cg=cg, dst=dst: e.activation(
                    out=st.q2[:, 0:512], in_=dst[:, cg, tc * 512:(tc + 1) * 512],
                    func=AF.Square), reads=[b_dst], writes=[st.b_q2])
                R.op("tensor", lambda e, cg=cg, tc=tc: e.matmul(
                    C.ps_big[0:16, tc, :], st.BO[:, cg, :], st.q2[:, 0:512],
                    start=(cg == 0), stop=(cg == 7)),
                    reads=[st.b_BO, st.b_q2], writes=[C.b_big[tc]])
            R.op("vector", lambda e, tc=tc: e.tensor_copy(
                st.nrm[:, tc * 512:(tc + 1) * 512], C.ps_big[0:16, tc, :]),
                reads=[C.b_big[tc]], writes=[st.b_nrm])
        R.op("vector", lambda e, nfi=nfi: e.tensor_scalar_add(nfi, nfi, 1e-6),
             reads=[st.b_nrm], writes=[st.b_nrm])
        R.op("scalar", lambda e, nfi=nfi: e.activation(out=nfi, in_=nfi, func=AF.Ln),
             reads=[st.b_nrm, C.b_cpool], writes=[st.b_nrm])
        R.op("scalar", lambda e, nfi=nfi: e.activation(
            out=nfi, in_=nfi, func=AF.Exp, scale=C.const(-0.5)),
            reads=[st.b_nrm, C.b_cpool], writes=[st.b_nrm])
        if iqk == 0:
            R.op("vector", lambda e, nfi=nfi: e.tensor_scalar_mul(nfi, nfi, 0.125),
                 reads=[st.b_nrm], writes=[st.b_nrm])
        for cg in range(8):
            for tc in range(4):
                ps = C.ps_ab[:, iqk, :]
                R.op("tensor", lambda e, cg=cg, tc=tc, ps=ps: e.matmul(
                    ps, st.EX[:, cg, :], st.nrm[:, tc * 512:(tc + 1) * 512],
                    start=True, stop=True),
                    reads=[st.b_EX, st.b_nrm], writes=[C.b_ab[iqk]])
                R.op("vector", lambda e, cg=cg, tc=tc, dst=dst, ps=ps: e.tensor_tensor(
                    out=dst[:, cg, tc * 512:(tc + 1) * 512],
                    in0=dst[:, cg, tc * 512:(tc + 1) * 512], in1=ps, op=ALU.mult),
                    reads=[C.b_ab[iqk], b_dst], writes=[b_dst])

    R.op("scalar", lambda e: e.activation(out=betaR, in_=st.rowsB[64:80, :],
                                          func=AF.Sigmoid),
         reads=[b_rB], writes=[b_rA])
    R.op("vector", lambda e: e.tensor_scalar(
        out=st.rowsA[64:80, :], in0=st.rowsA[64:80, :],
        scalar1=st.smalls[0:16, 0:1], scalar2=None, op0=ALU.add),
        reads=[b_rA, st.b_smalls], writes=[b_rA])
    R.op("scalar", lambda e: e.activation(out=arawR, in_=st.rowsA[64:80, :], func=AF.Abs),
         reads=[b_rA], writes=[b_rB])
    R.op("scalar", lambda e: e.activation(out=arawR, in_=arawR, func=AF.Exp, scale=-1.0),
         reads=[b_rB], writes=[b_rB])
    R.op("scalar", lambda e: e.activation(out=arawR, in_=arawR, func=AF.Ln, bias=1.0),
         reads=[b_rB], writes=[b_rB])
    R.op("vector", lambda e: e.tensor_scalar_max(
        st.rowsA[64:80, :], st.rowsA[64:80, :], 0.0),
        reads=[b_rA], writes=[b_rA])
    R.op("vector", lambda e: e.tensor_add(arawR, arawR, st.rowsA[64:80, :]),
         reads=[b_rA, b_rB], writes=[b_rB])
    R.op("vector", lambda e: e.tensor_scalar(
        out=arawR, in0=arawR, scalar1=st.smalls[0:16, 3:4], scalar2=None,
        op0=ALU.mult), reads=[b_rB, st.b_smalls], writes=[b_rB])
    R.op("vector", lambda e: e.tensor_tensor_scan(
        out=eglgR, data0=arawR, data1=arawR, initial=0.0,
        op0=ALU.add, op1=ALU.bypass),
        reads=[b_rB], writes=[b_rA])
    R.op("vector", lambda e: e.tensor_copy(gR, eglgR), reads=[b_rA], writes=[b_rA])
    R.op("vector", lambda e: e.memset(sbgR[:, 0:1], 0.0),
         reads=[b_rB], writes=[b_rB])
    R.op("vector", lambda e: e.tensor_copy(sbgR[:, 1:32], eglgR[:, 63:2047:64]),
         reads=[b_rA, b_rB], writes=[b_rB])
    R.op("vector", lambda e: e.tensor_tensor(
        out=gR.rearrange("p (a b) -> p a b", b=64),
        in0=gR.rearrange("p (a b) -> p a b", b=64),
        in1=sbgR[:, 0:32].unsqueeze(2).broadcast_to([16, 32, 64]),
        op=ALU.subtract), reads=[b_rA, b_rB], writes=[b_rA])
    R.op("vector", lambda e: e.tensor_tensor(
        out=eglgR.rearrange("p (a b) -> p a b", b=64),
        in0=gR[:, 63:2048:64].unsqueeze(2).broadcast_to([16, 32, 64]),
        in1=gR.rearrange("p (a b) -> p a b", b=64), op=ALU.subtract),
        reads=[b_rA], writes=[b_rA])
    R.op("scalar", lambda e: e.activation(out=eglgR, in_=eglgR, func=AF.Exp),
         reads=[b_rA], writes=[b_rA])
    R.op("scalar", lambda e: e.activation(out=eglR, in_=gR, func=AF.Exp),
         reads=[b_rA], writes=[b_rB])
    R.op("vector", lambda e: e.tensor_mul(sbgR, eglR, betaR),
         reads=[b_rA, b_rB], writes=[b_rB])
    R.op("vector", lambda e: e.tensor_copy(
        eglR.rearrange("p (a b) -> p a b", b=64),
        gR[:, 63:2048:64].unsqueeze(2).broadcast_to([16, 32, 64])),
        reads=[b_rA], writes=[b_rB])
    R.op("scalar", lambda e: e.activation(out=eglR, in_=eglR, func=AF.Exp),
         reads=[b_rB], writes=[b_rB])
    R.dma("sync", g_dram[:, :], gR, st.c_sm, reads=[b_rA], writes=[b_gd])
    R.dma("sync", beta_dram[:, :], betaR, st.c_sm, reads=[b_rA], writes=[b_gd])

    if dump is not None:
        dl = [("d_q", st.q[:, :, :], st.b_q), ("d_k", st.k[:, :, :], st.b_k),
              ("d_v", st.v[:, :, :], st.b_v)]
        for r0 in (0, 32, 64):
            dl.append((("d_rA", r0), st.rowsA[r0:r0 + 16, :], st.b_rA))
            dl.append((("d_rB", r0), st.rowsB[r0:r0 + 16, :], st.b_rB))
        for key, ap, b in dl:
            if isinstance(key, tuple):
                if key[0] in dump:
                    R.dma("sync", dump[key[0]][key[1]:key[1] + 16, :], ap, dump["c"],
                          reads=[b], writes=[dump["b"]])
            elif key in dump:
                R.dma("sync", dump[key], ap, dump["c"], reads=[b], writes=[dump["b"]])
    R.op("vector", lambda e: e.memset(st.S32[:, :, :], 0.0),
         reads=[st.b_S32], writes=[st.b_S32])
    R.op("vector", lambda e: e.memset(st.Sbf[:, :, :], 0.0),
         reads=[st.b_Sbf], writes=[st.b_Sbf])

    varrows = [(st.rowsA, 0, b_rA), (st.rowsB, 0, b_rB), (st.rowsA, 32, b_rA),
               (st.rowsB, 32, b_rB), (st.rowsA, 64, b_rA)]
    VAR_G, VAR_SBG, VAR_BETA, VAR_EGL, VAR_EGLG = 0, 1, 2, 3, 4

    def colsel(var, par2, cs):
        return st.cols[par2 * 64:(par2 + 1) * 64, cs, var * 16 + par2:var * 16 + 16:2]

    def emit_bc(ci):
        cb = ci * 64
        s = ci % 2
        for half in range(2):
            R.dma("scalar", st.gA[half * 64:(half + 1) * 64, s, :, :],
                  g_dram[half::2, cb:cb + 64].unsqueeze(0)
                  .partition_broadcast(64).squeeze(1),
                  st.c_bc, reads=[b_gd, st.b_gA[s]], writes=[st.b_gA[s]])
            R.dma("scalar", st.betA[half * 64:(half + 1) * 64, s, :, :],
                  beta_dram[half::2, cb:cb + 64].unsqueeze(0)
                  .partition_broadcast(64).squeeze(1),
                  st.c_bc, reads=[b_gd, st.b_betA[s]], writes=[st.b_betA[s]])
    st.emit_bc = emit_bc

    def phase_A(ci):
        cb = ci * 64
        cs = ci % 2
        lvl = globals().get("L0A_LVL", 99)
        if ci + 1 < n_chunks:
            emit_bc(ci + 1)
        R.op("scalar", lambda e: e.activation(
            out=st.egA[:, :, :], in_=st.gA[:, cs, :, :], func=AF.Exp),
            reads=[st.b_gA[cs]], writes=[st.b_egA])
        if lvl < 2: return
        vbank = {0: 1, 32: 2, 64: 3}
        vcol = [0, 16, 0, 16, 0]
        for vi, (rt, r0, b_rt) in enumerate(varrows):
            R.op("tensor", lambda e, vi=vi, rt=rt, r0=r0, cb=cb:
                 e.transpose(
                     C.ps_big[0:64, vbank[r0], vcol[vi]:vcol[vi] + 16],
                     rt[r0:r0 + 16, cb:cb + 64],
                     C.idf[r0:r0 + 16, r0:r0 + 16],
                     tile_position=(r0, 0)),
                 reads=[b_rt, C.b_idf], writes=[C.b_big[vbank[r0]]])
        for half in range(2):
            hs = slice(half * 64, (half + 1) * 64)
            R.op("vector", lambda e, cs=cs, hs=hs: e.tensor_copy(
                st.cols[hs, cs, 0:32], C.ps_big[0:64, 1, 0:32]),
                reads=[C.b_big[1]], writes=[st.b_cols])
            R.op("vector", lambda e, cs=cs, hs=hs: e.tensor_copy(
                st.cols[hs, cs, 32:64], C.ps_big[0:64, 2, 0:32]),
                reads=[C.b_big[2]], writes=[st.b_cols])
            R.op("vector", lambda e, cs=cs, hs=hs: e.tensor_copy(
                st.cols[hs, cs, 64:80], C.ps_big[0:64, 3, 0:16]),
                reads=[C.b_big[3]], writes=[st.b_cols])
        R.op("vector", lambda e, cb=cb, cs=cs: e.tensor_tensor(
            out=st.kb[:, :, :], in0=st.k[:, :, cb:cb + 64],
            in1=st.betA[:, cs, :, :],
            op=ALU.mult), reads=[st.b_k, st.b_betA[cs]], writes=[st.b_kb])
        if lvl < 3: return
        for par2 in range(2):
            sel = colsel(0, par2, cs)
            R.op("vector", lambda e, par2=par2, sel=sel, cs=cs: e.tensor_tensor(
                out=st.LTf[par2 * 64:(par2 + 1) * 64, :, :],
                in0=st.gA[par2 * 64:(par2 + 1) * 64, cs, :, :],
                in1=sel.unsqueeze(2).broadcast_to([64, 8, 64]),
                op=ALU.subtract),
                reads=[st.b_gA[cs], st.b_cols], writes=[st.b_LTf])
        R.op("vector", lambda e, cs=cs: e.tensor_scalar_min(
            st.betA[:, cs, :, :], st.LTf[:, :, :], 0.0),
            reads=[st.b_LTf, st.b_kb], writes=[st.b_betA[cs]])
        R.op("scalar", lambda e, cs=cs: e.activation(
            out=st.LT[:, 0, :, :], in_=st.betA[:, cs, :, :], func=AF.Exp),
            reads=[st.b_betA[cs]], writes=[st.b_LT])
        R.op("vector", lambda e: e.tensor_tensor(
            out=st.LT[:, 1, :, :], in0=st.LT[:, 0, :, :],
            in1=st.mask[:, 1, :].unsqueeze(1).broadcast_to([128, 8, 64]),
            op=ALU.mult), reads=[st.b_LT, st.b_mask], writes=[st.b_LT])
        R.op("vector", lambda e: e.tensor_tensor(
            out=st.LT[:, 0, :, :], in0=st.LT[:, 0, :, :],
            in1=st.mask[:, 0, :].unsqueeze(1).broadcast_to([128, 8, 64]),
            op=ALU.mult), reads=[st.b_LT, st.b_mask], writes=[st.b_LT])
        R.op("vector", lambda e, cs=cs: e.tensor_scalar_max(
            st.betA[:, cs, :, :], st.LTf[:, :, :], 0.0),
            reads=[st.b_LTf], writes=[st.b_betA[cs]])
        R.op("scalar", lambda e, cs=cs: e.activation(
            out=st.LT[:, 2, :, :], in_=st.betA[:, cs, :, :], func=AF.Exp, scale=-1.0),
            reads=[st.b_betA[cs]], writes=[st.b_LT])
        R.op("vector", lambda e: e.tensor_tensor(
            out=st.LT[:, 2, :, :], in0=st.LT[:, 2, :, :],
            in1=st.mask[:, 2, :].unsqueeze(1).broadcast_to([128, 8, 64]),
            op=ALU.mult), reads=[st.b_LT, st.b_mask], writes=[st.b_LT])
        if lvl < 4: return
        for kind, (lhs, b_lhs, rhs, b_rhs, bank) in enumerate([
                (st.kb, st.b_kb, None, st.b_k, 0),
                (None, st.b_k, st.kb, st.b_kb, 1),
                (None, st.b_k, None, st.b_q, 2)]):
            def mm_3(e, kind=kind, cb=cb, bank=bank):
                last = None
                for sp in range(8):
                    for par in range(2):
                        sl = slice(par * 64, (par + 1) * 64)
                        if kind == 0:
                            lt = st.kb[sl, sp, :]
                            rt = st.k[sl, sp, cb:cb + 64]
                        elif kind == 1:
                            lt = st.k[sl, sp, cb:cb + 64]
                            rt = st.kb[sl, sp, :]
                        else:
                            lt = st.k[sl, sp, cb:cb + 64]
                            rt = st.q[sl, sp, cb:cb + 64]
                        last = e.matmul(
                            C.ps_big[sl, bank, sp * 64:(sp + 1) * 64], lt, rt,
                            start=True, stop=True, tile_position=(par * 64, par * 64))
                return last
            R.op("tensor", mm_3, reads=[st.b_kb, st.b_k, st.b_q],
                 writes=[C.b_big[bank]])
        R.op("vector", lambda e, cb=cb: e.tensor_tensor(
            out=st.q[:, :, cb:cb + 64], in0=st.q[:, :, cb:cb + 64],
            in1=st.egA[:, :, :], op=ALU.mult),
            reads=[st.b_q, st.b_egA], writes=[st.b_q])
        R.op("vector", lambda e: e.tensor_tensor(
            out=st.P[:, 0, :, :],
            in0=C.ps_big[:, 0, :].rearrange("p (a b) -> p a b", b=64),
            in1=st.LT[:, 2, :, :], op=ALU.mult),
            reads=[C.b_big[0], st.b_LT], writes=[st.b_P])
        R.op("vector", lambda e: e.tensor_tensor(
            out=st.P[:, 1, :, :],
            in0=C.ps_big[:, 1, :].rearrange("p (a b) -> p a b", b=64),
            in1=st.LT[:, 1, :, :], op=ALU.mult),
            reads=[C.b_big[1], st.b_LT], writes=[st.b_P])
        R.op("vector", lambda e: e.tensor_tensor(
            out=st.attnT[:, :, :],
            in0=C.ps_big[:, 2, :].rearrange("p (a b) -> p a b", b=64),
            in1=st.LT[:, 0, :, :], op=ALU.mult),
            reads=[C.b_big[2], st.b_LT], writes=[st.b_attnT])
        if lvl < 5: return
        def mm_ktr(e, cb=cb):
            last = None
            for sp in range(8):
                for par in range(2):
                    sl = slice(par * 64, (par + 1) * 64)
                    last = e.transpose(
                        C.ps_tr[0:64, par, sp * 64:(sp + 1) * 64],
                        st.k[sl, sp, cb:cb + 64],
                        C.ident[sl, par * 64:(par + 1) * 64],
                        tile_position=(par * 64, 0))
            return last
        R.op("tensor", mm_ktr, reads=[st.b_k, C.b_ident],
             writes=[C.b_ptr[0], C.b_ptr[1]])
        for par2 in range(2):
            sl = slice(par2 * 64, (par2 + 1) * 64)
            src = C.ps_tr[0:64, par2, 0:512]
            if lvl == 5:
                R.op("vector", lambda e, sl=sl, src=src: e.tensor_copy(
                    st.kd[sl, :, :], src.rearrange("p (a b) -> p a b", b=64)),
                    reads=[C.b_ptr[0]], writes=[st.b_kd])
                continue
            R.op("vector", lambda e, par2=par2, sl=sl, cs=cs, src=src: e.tensor_tensor(
                out=st.kd[sl, :, :],
                in0=src.rearrange("p (a b) -> p a b", b=64),
                in1=colsel(4, par2, cs).unsqueeze(2).broadcast_to([64, 8, 64]),
                op=ALU.mult), reads=[C.b_ptr[par2], st.b_cols], writes=[st.b_kd])
            R.op("vector", lambda e, par2=par2, sl=sl, cs=cs, src=src: e.tensor_tensor(
                out=st.B[sl, :, 64:128],
                in0=src.rearrange("p (a b) -> p a b", b=64),
                in1=colsel(1, par2, cs).unsqueeze(2).broadcast_to([64, 8, 64]),
                op=ALU.mult), reads=[C.b_ptr[par2], st.b_cols], writes=[st.b_B])
        if lvl < 5.7: return
        def mm_vtr(e, cb=cb):
            last = None
            for sp in range(8):
                for par in range(2):
                    sl = slice(par * 64, (par + 1) * 64)
                    last = e.transpose(
                        C.ps_tr[0:64, par, 512 + sp * 64:512 + (sp + 1) * 64],
                        st.v[sl, sp, cb:cb + 64],
                        C.ident[sl, par * 64:(par + 1) * 64],
                        tile_position=(par * 64, 0))
            return last
        R.op("tensor", mm_vtr, reads=[st.b_v, C.b_ident],
             writes=[C.b_ptr[0], C.b_ptr[1]])
        for par2 in range(2):
            sl = slice(par2 * 64, (par2 + 1) * 64)
            src = C.ps_tr[0:64, par2, 512:1024]
            R.op("vector", lambda e, par2=par2, sl=sl, cs=cs, src=src: e.tensor_tensor(
                out=st.B[sl, :, 0:64],
                in0=src.rearrange("p (a b) -> p a b", b=64),
                in1=colsel(2, par2, cs).unsqueeze(2).broadcast_to([64, 8, 64]),
                op=ALU.mult), reads=[C.b_ptr[par2], st.b_cols], writes=[st.b_B])
        if lvl < 7: return
        Ups = C.ps_big[:, 0:2, :].rearrange("p a b -> p (a b)")

        def emit_Umm(rhs_t, b_rhs_t):
            def mm(e, rhs_t=rhs_t):
                last = None
                for sp in range(8):
                    for par in range(2):
                        sl = slice(par * 64, (par + 1) * 64)
                        last = e.matmul(
                            Ups[sl, sp * 128:(sp + 1) * 128],
                            st.P[sl, 1, sp, :], rhs_t[sl, sp, :],
                            start=True, stop=True,
                            tile_position=(par * 64, par * 64))
                return last
            R.op("tensor", mm, reads=[st.b_P, b_rhs_t],
                 writes=[C.b_big[0], C.b_big[1]])
            R.op("vector", lambda e, rhs_t=rhs_t: e.tensor_tensor(
                out=st.U[:, :, :],
                in0=Ups.rearrange("p (a b) -> p a b", b=128),
                in1=rhs_t, op=ALU.add),
                reads=[C.b_big[0], C.b_big[1], b_rhs_t], writes=[st.b_U])
        emit_Umm(st.B, st.b_B)
        for it in range(5):
            def mm_sq(e):
                last = None
                for sp in range(8):
                    for par in range(2):
                        sl = slice(par * 64, (par + 1) * 64)
                        e.matmul(C.ps_big[sl, 2, sp * 64:(sp + 1) * 64],
                                 st.P[sl, 1, sp, :], st.P[sl, 0, sp, :],
                                 start=True, stop=True,
                                 tile_position=(par * 64, par * 64))
                        last = e.matmul(C.ps_big[sl, 3, sp * 64:(sp + 1) * 64],
                                        st.P[sl, 0, sp, :], st.P[sl, 1, sp, :],
                                        start=True, stop=True,
                                        tile_position=(par * 64, par * 64))
                return last
            R.op("tensor", mm_sq, reads=[st.b_P], writes=[C.b_big[2], C.b_big[3]])
            R.op("vector", lambda e: e.tensor_copy(
                st.P[:, 0, :, :],
                C.ps_big[:, 2, :].rearrange("p (a b) -> p a b", b=64)),
                reads=[C.b_big[2]], writes=[st.b_P])
            R.op("vector", lambda e: e.tensor_copy(
                st.P[:, 1, :, :],
                C.ps_big[:, 3, :].rearrange("p (a b) -> p a b", b=64)),
                reads=[C.b_big[3]], writes=[st.b_P])
            emit_Umm(st.U, st.b_U)
        if lvl < 8: return
        def mm_kcdtr(e):
            last = None
            for sp in range(8):
                for par in range(2):
                    sl = slice(par * 64, (par + 1) * 64)
                    last = e.transpose(
                        C.ps_tr[0:64, par, sp * 64:(sp + 1) * 64],
                        st.U[sl, sp, 64:128],
                        C.ident[sl, par * 64:(par + 1) * 64],
                        tile_position=(par * 64, 0))
            return last
        R.op("tensor", mm_kcdtr, reads=[st.b_U, C.b_ident],
             writes=[C.b_ptr[0], C.b_ptr[1]])
        for par2 in range(2):
            sl = slice(par2 * 64, (par2 + 1) * 64)
            R.op("vector", lambda e, par2=par2, sl=sl: e.tensor_copy(
                st.kcdT[sl, :, :],
                C.ps_tr[0:64, par2, 0:512].rearrange(
                    "p (a b) -> p a b", b=64)),
                reads=[C.b_ptr[par2]], writes=[st.b_kcdT])

    def phase_B(ci):
        cb = ci * 64
        cs = ci % 2
        if globals().get("L0A_LVL", 99) < 9: return
        def mm_1(e):
            last = None
            for sp in range(8):
                for par in range(2):
                    sl = slice(par * 64, (par + 1) * 64)
                    last = e.matmul(C.ps_ab[sl, 0, sp * 64:(sp + 1) * 64],
                                    st.kcdT[sl, sp, :], st.Sbf[sl, sp, :],
                                    start=True, stop=True,
                                    tile_position=(par * 64, par * 64))
            return last
        R.op("tensor", mm_1, reads=[st.b_kcdT, st.b_Sbf], writes=[C.b_ab[0]])
        R.op("vector", lambda e: e.tensor_tensor(
            out=st.vn[:, :, :], in0=st.U[:, :, 0:64],
            in1=C.ps_ab[:, 0, :].rearrange("p (a b) -> p a b", b=64),
            op=ALU.subtract), reads=[st.b_U, C.b_ab[0]], writes=[st.b_vn])
        def mm_o(e):
            last = None
            for sp in range(8):
                for par in range(2):
                    sl = slice(par * 64, (par + 1) * 64)
                    e.matmul(C.ps_ab[sl, 1, sp * 64:(sp + 1) * 64],
                             st.Sbf[sl, sp, :], st.q[sl, sp, cb:cb + 64],
                             start=True, stop=False,
                             tile_position=(par * 64, par * 64))
                    last = e.matmul(C.ps_ab[sl, 1, sp * 64:(sp + 1) * 64],
                                    st.vn[sl, sp, :], st.attnT[sl, sp, :],
                                    start=False, stop=True,
                                    tile_position=(par * 64, par * 64))
            return last
        R.op("tensor", mm_o, reads=[st.b_Sbf, st.b_q, st.b_vn, st.b_attnT],
             writes=[C.b_ab[1]])
        R.op("vector", lambda e, cb=cb: e.tensor_copy(
            o_fm[:, :, cb:cb + 64],
            C.ps_ab[:, 1, :].rearrange("p (a b) -> p a b", b=64)),
            reads=[C.b_ab[1]], writes=[b_o])
        def mm_S(e):
            last = None
            for sp in range(8):
                for par in range(2):
                    sl = slice(par * 64, (par + 1) * 64)
                    last = e.matmul(C.ps_ab[sl, 0, sp * 64:(sp + 1) * 64],
                                    st.kd[sl, sp, :], st.vn[sl, sp, :],
                                    start=True, stop=True,
                                    tile_position=(par * 64, par * 64))
            return last
        R.op("tensor", mm_S, reads=[st.b_kd, st.b_vn], writes=[C.b_ab[0]])
        for par2 in range(2):
            sl = slice(par2 * 64, (par2 + 1) * 64)
            R.op("vector", lambda e, par2=par2, sl=sl, cs=cs: e.tensor_tensor(
                out=st.S32[sl, :, :], in0=st.S32[sl, :, :],
                in1=colsel(3, par2, cs).unsqueeze(2).broadcast_to([64, 8, 64]),
                op=ALU.mult), reads=[st.b_S32, st.b_cols], writes=[st.b_S32])
        R.op("vector", lambda e: e.tensor_tensor(
            out=st.S32[:, :, :], in0=st.S32[:, :, :],
            in1=C.ps_ab[:, 0, :].rearrange("p (a b) -> p a b", b=64),
            op=ALU.add), reads=[st.b_S32, C.b_ab[0]], writes=[st.b_S32])
        R.op("vector", lambda e: e.tensor_copy(st.Sbf[:, :, :], st.S32[:, :, :]),
             reads=[st.b_S32], writes=[st.b_Sbf])

    emit_bc(0)
    for ci in range(n_chunks):
        if hook is not None:
            hook()
        phase_A(ci)
        if dump is not None and ci == 0:
            for key, ap, b in [("d_LT", st.LT[:, :, :, :], st.b_LT),
                               ("d_P", st.P[:, :, :, :], st.b_P),
                               ("d_B", st.B[:, :, :], st.b_pool),
                               ("d_U", st.U[:, :, :], st.b_pool),
                               ("d_attnT", st.attnT[:, :, :], st.b_pool),
                               ("d_kcdT", st.kcdT[:, :, :], st.b_pool),
                               ("d_kd", st.kd[:, :, :], st.b_pool)]:
                if key in dump:
                    R.dma("sync", dump[key], ap, dump["c"], reads=[b], writes=[dump["b"]])
        phase_B(ci)
        if dump is not None and ci == 0:
            for key, ap, b in [("d_S", st.S32[:, :, :], st.b_S32),
                               ("d_vn", st.vn[:, :, :], st.b_pool)]:
                if key in dump:
                    R.dma("sync", dump[key], ap, dump["c"], reads=[b], writes=[dump["b"]])
    if dump is not None and "d_o" in dump:
        R.dma("sync", dump["d_o"], o_fm[:, :, :], dump["c"], reads=[b_o], writes=[dump["b"]])

    C.const(1e-5)
    for tc in range(4):
        for cg in range(8):
            R.op("scalar", lambda e, tc=tc, cg=cg: e.activation(
                out=st.q2[:, 0:512], in_=o_fm[:, cg, tc * 512:(tc + 1) * 512],
                func=AF.Square), reads=[b_o], writes=[st.b_q2])
            R.op("tensor", lambda e, cg=cg, tc=tc: e.matmul(
                C.ps_big[0:16, tc, :], st.BO[:, cg, :], st.q2[:, 0:512],
                start=(cg == 0), stop=(cg == 7)),
                reads=[st.b_BO, st.b_q2], writes=[C.b_big[tc]])
        R.op("vector", lambda e, tc=tc: e.tensor_copy(
            st.nrm[:, tc * 512:(tc + 1) * 512], C.ps_big[0:16, tc, :]),
            reads=[C.b_big[tc]], writes=[st.b_nrm])
    R.op("vector", lambda e: e.tensor_scalar(
        out=st.nrm[:, :], in0=st.nrm[:, :], scalar1=1.0 / 64, scalar2=1e-5,
        op0=ALU.mult, op1=ALU.add), reads=[st.b_nrm], writes=[st.b_nrm])
    R.op("scalar", lambda e: e.activation(out=st.nrm[:, :], in_=st.nrm[:, :], func=AF.Ln),
         reads=[st.b_nrm, C.b_cpool], writes=[st.b_nrm])
    R.op("scalar", lambda e: e.activation(
        out=st.nrm[:, :], in_=st.nrm[:, :], func=AF.Exp, scale=C.const(-0.5)),
        reads=[st.b_nrm, C.b_cpool], writes=[st.b_nrm])
    for cg in range(8):
        for tc in range(4):
            ps = C.ps_ab[:, tc % 2, :]
            R.op("tensor", lambda e, cg=cg, tc=tc, ps=ps: e.matmul(
                ps, st.EX[:, cg, :], st.nrm[:, tc * 512:(tc + 1) * 512],
                start=True, stop=True),
                reads=[st.b_EX, st.b_nrm], writes=[C.b_ab[tc % 2]])
            gs = (tc + cg) % 2
            R.dma("scalar", st.gstg[:, gs, :],
                  gate_d[:, cg, tc * 512:(tc + 1) * 512], st.c_g,
                  reads=[b_gt, st.b_g], writes=[st.b_g])
            R.op("scalar", lambda e, gs=gs: e.activation(
                out=st.q2[:, 0:512], in_=st.gstg[:, gs, :],
                func=AF.Silu), reads=[st.b_g], writes=[st.b_q2])
            R.op("vector", lambda e, cg=cg, tc=tc, ps=ps: e.tensor_tensor(
                out=o_fm[:, cg, tc * 512:(tc + 1) * 512],
                in0=o_fm[:, cg, tc * 512:(tc + 1) * 512], in1=ps, op=ALU.mult),
                reads=[C.b_ab[tc % 2], b_o], writes=[b_o])
            R.op("vector", lambda e, cg=cg, tc=tc: e.scalar_tensor_tensor(
                out=o_fm[:, cg, tc * 512:(tc + 1) * 512],
                in0=o_fm[:, cg, tc * 512:(tc + 1) * 512],
                scalar=st.smalls[:, 2:3], in1=st.q2[:, 0:512],
                op0=ALU.mult, op1=ALU.mult),
                reads=[st.b_q2, b_o, st.b_smalls], writes=[b_o])

    for nh in range(2):
        loadW(Wo, nh * 512)
        for t in range(16):
            ps = C.ps_ab[:, t % 2, :]
            b_ps = C.b_ab[t % 2]

            def mm_wo(e, t=t, ps=ps):
                last = None
                for cc in range(8):
                    last = e.matmul(ps, o_fm[:, cc, t * 128:(t + 1) * 128],
                                    st.w[:, cc, :], start=(cc == 0), stop=(cc == 7))
                return last
            R.op("tensor", mm_wo, reads=[st.b_w, b_o], writes=[b_ps])
            xadd_fn(t, nh, ps, b_ps)



def build_l0ffn_program():
    nc = bass.Bass("TRN2", target_bir_lowering=False, dynamic_dma_scratch_size=16384)
    x_in = nc.dram_tensor("x_in", [4096, 1024], F32, kind="ExternalInput").ap()
    wg_i = nc.dram_tensor("wgT", [1024, 3072], I32, kind="ExternalInput").ap()
    wu_i = nc.dram_tensor("wuT", [1024, 3072], I32, kind="ExternalInput").ap()
    wd_i = nc.dram_tensor("wdT", [3072, 1024], I32, kind="ExternalInput").ap()
    x_out = nc.dram_tensor("x_out", [4096, 1024], F32, kind="ExternalOutput").ap()
    wg_b = nc.dram_tensor("wg_bf", [1024, 3072], BF16, kind="Internal").ap()
    wu_b = nc.dram_tensor("wu_bf", [1024, 3072], BF16, kind="Internal").ap()
    wd_b = nc.dram_tensor("wd_bf", [3072, 1024], BF16, kind="Internal").ap()

    R = Rec(nc)
    C = emit_common(R)
    C.const(1e-5)
    b_w = Buf("wbf")
    cvt = R.sbuf("cvt", [128, 2, 3072], BF16)
    b_cvt = [Buf(f"cvt{i}") for i in range(2)]
    c_cl = [R.ctr(f"cvt_cl{i}", 16) for i in range(2)]
    c_cs = [R.ctr(f"cvt_cs{i}", 16) for i in range(2)]
    emit_int_to_bf16(R, cvt, b_cvt, c_cl, c_cs, wg_i, wg_b, 1024, 3072, b_w)
    emit_int_to_bf16(R, cvt, b_cvt, c_cl, c_cs, wu_i, wu_b, 1024, 3072, b_w)
    emit_int_to_bf16(R, cvt, b_cvt, c_cl, c_cs, wd_i, wd_b, 3072, 1024, b_w)

    x_sb = R.sbuf("x_sb", [128, 16, 1024], F32)
    b_x = Buf("x")
    c_x = R.ctr("c_x", 16)
    c_xo = R.ctr("c_xo", 16)
    st = l0ffn_state(R, "f")
    b_xout = Buf("xout")
    for p in range(2):
        R.dma("sync", x_sb[:, :, :],
              x_in[p * 2048:(p + 1) * 2048, :].rearrange("(a p) d -> p a d", p=128),
              c_x, reads=[b_x], writes=[b_x])
        emit_l0ffn(R, C, st, x_sb, b_x, wg_b, wu_b, wd_b, b_w)
        R.dma("sync", x_out[p * 2048:(p + 1) * 2048, :].rearrange("(a p) d -> p a d", p=128),
              x_sb[:, :, :], c_xo, reads=[b_x], writes=[b_xout])
        R.full_barrier()
    R.drain("sync", [c_xo])
    R.emit()
    return nc, R


def build_l1attn_program():
    nc = bass.Bass("TRN2", target_bir_lowering=False, dynamic_dma_scratch_size=16384)
    x_in = nc.dram_tensor("x_in", [4096, 1024], F32, kind="ExternalInput").ap()
    wq = nc.dram_tensor("WqT", [1024, 1024], F32, kind="ExternalInput").ap()
    wk = nc.dram_tensor("WkT", [1024, 1024], F32, kind="ExternalInput").ap()
    wv = nc.dram_tensor("WvT", [1024, 1024], F32, kind="ExternalInput").ap()
    wo = nc.dram_tensor("WoT", [1024, 1024], F32, kind="ExternalInput").ap()
    bq = nc.dram_tensor("bq", [1024], F32, kind="ExternalInput").ap()
    bk = nc.dram_tensor("bk", [1024], F32, kind="ExternalInput").ap()
    bv = nc.dram_tensor("bv", [1024], F32, kind="ExternalInput").ap()
    bo = nc.dram_tensor("bo", [1024], F32, kind="ExternalInput").ap()
    x_out = nc.dram_tensor("x_out", [4096, 1024], F32, kind="ExternalOutput").ap()

    R = Rec(nc)
    C = emit_common(R)
    C.const(1e-5)
    x_sb = R.sbuf("x_sb", [128, 16, 1024], F32)
    b_x = Buf("x")
    c_x = R.ctr("c_x", 16)
    c_xo = R.ctr("c_xo", 16)
    st = l1attn_state(R, C, "a")
    emit_l1attn_setup(R, C, st, "a")
    b_xout = Buf("xout")
    def x_ap_fn(i):
        return x_sb[:, i, :]

    def xadd_fn(t, nh, ps, b_ps):
        R.op("vector", lambda e, t=t, nh=nh, ps=ps: e.scalar_tensor_tensor(
            out=x_sb[:, t, nh * 512:(nh + 1) * 512], in0=ps, scalar=0.5,
            in1=x_sb[:, t, nh * 512:(nh + 1) * 512], op0=ALU.mult, op1=ALU.add),
            reads=[b_ps, b_x], writes=[b_x])
    for p in range(2):
        R.dma("sync", x_sb[:, :, :],
              x_in[p * 2048:(p + 1) * 2048, :].rearrange("(a p) d -> p a d", p=128),
              c_x, reads=[b_x], writes=[b_x])
        emit_l1attn(R, C, st, x_ap_fn, b_x, xadd_fn, wq, wk, wv, wo, bq, bk, bv, bo)
        R.dma("sync", x_out[p * 2048:(p + 1) * 2048, :].rearrange("(a p) d -> p a d", p=128),
              x_sb[:, :, :], c_xo, reads=[b_x], writes=[b_xout])
        R.full_barrier()
    R.drain("sync", [c_xo])
    R.emit()
    return nc, R


def kernel_l0ffn_only(x, wg, wu, wd):
    wgT = np.ascontiguousarray(wg.T)
    wuT = np.ascontiguousarray(wu.T)
    wdT = np.ascontiguousarray(wd.T)
    nc, R = build_l0ffn_program()
    in_maps = []
    for c in range(8):
        xs = np.ascontiguousarray(x[c * 2:(c + 1) * 2].reshape(4096, 1024))
        in_maps.append({"x_in": xs, "wgT": wgT, "wuT": wuT, "wdT": wdT})
    res = run_bass_kernel_spmd(nc, in_maps, core_ids=list(range(8)))
    out = np.stack([np.asarray(r["x_out"]).reshape(2, 2048, 1024) for r in res.results])
    return out.reshape(16, 2048, 1024)


def kernel_l1attn_only(x, Wq, bq, Wk, bk, Wv, bv, Wo, bo):
    m = {"WqT": np.ascontiguousarray(Wq.T), "WkT": np.ascontiguousarray(Wk.T),
         "WvT": np.ascontiguousarray(Wv.T), "WoT": np.ascontiguousarray(Wo.T),
         "bq": bq, "bk": bk, "bv": bv, "bo": bo}
    nc, R = build_l1attn_program()
    in_maps = []
    for c in range(8):
        xs = np.ascontiguousarray(x[c * 2:(c + 1) * 2].reshape(4096, 1024))
        in_maps.append(dict(m, x_in=xs))
    res = run_bass_kernel_spmd(nc, in_maps, core_ids=list(range(8)))
    out = np.stack([np.asarray(r["x_out"]).reshape(2, 2048, 1024) for r in res.results])
    return out.reshape(16, 2048, 1024)


def build_l1ffn_program():
    nc = bass.Bass("TRN2", target_bir_lowering=False, dynamic_dma_scratch_size=16384)
    x_in = nc.dram_tensor("x_in", [4096, 1024], F32, kind="ExternalInput").ap()
    w1 = [nc.dram_tensor(f"W1T{e}", [4096, 1024], F32, kind="ExternalInput").ap()
          for e in range(8)]
    w2 = [nc.dram_tensor(f"W2T{e}", [4096, 1024], F32, kind="ExternalInput").ap()
          for e in range(8)]
    gbT = nc.dram_tensor("gbT", [1024, 2, 8], F32, kind="ExternalInput").ap()
    b1T = nc.dram_tensor("b1T", [4096, 8], F32, kind="ExternalInput").ap()
    b2 = nc.dram_tensor("b2", [8, 1024], F32, kind="ExternalInput").ap()
    centT = nc.dram_tensor("centT", [1024, 8], F32, kind="ExternalInput").ap()
    out = nc.dram_tensor("out", [4096, 1024], F32, kind="ExternalOutput").ap()
    fhat_d = nc.dram_tensor("fhat_d", [128, 8, 4096], BF16, kind="Internal").ap()
    w1b = [nc.dram_tensor(f"w1b{e}", [4096, 1024], BF16, kind="Internal").ap()
           for e in range(8)]
    w2b = [nc.dram_tensor(f"w2b{e}", [4096, 1024], BF16, kind="Internal").ap()
           for e in range(8)]

    R = Rec(nc)
    C = emit_common(R)
    C.const(1e-5)
    aE = R.sbuf("aE", [128, 32, 8], F32)
    b_aE = Buf("aE")
    b_fd = Buf("fd")
    BW1 = [[Buf(f"cv1_{e}_{q}") for q in range(4)] for e in range(8)]
    BW2 = [[Buf(f"cv2_{e}_{q}") for q in range(4)] for e in range(8)]
    c_conv = R.ctr("c_conv", 16)
    for e in range(8):
        for q in range(4):
            sl = slice(q * 1024, (q + 1) * 1024)
            R.dma("gpsimd", w1b[e][sl, :], w1[e][sl, :], c_conv, writes=[BW1[e][q]])
            R.dma("gpsimd", w2b[e][sl, :], w2[e][sl, :], c_conv, writes=[BW2[e][q]])
    stp = l1ffn_pre_state(R, "p")
    stm = l1ffn_main_state(R, "m")
    emit_l1ffn_pre_setup(R, C, stp, centT)
    emit_l1ffn_main_setup(R, C, stm, gbT, b1T, b2)
    xt = R.sbuf("xt", [128, 2, 1024], F32)
    b_xt = Buf("xt")
    c_x = R.ctr("c_x", 16)
    b_out = [Buf(f"out{t}") for t in range(32)]
    c_out = R.ctr("c_out", 16)

    def make_xfn(p):
        def xfn(i):
            s2 = i % 2
            R.dma("sync", xt[:, s2, :],
                  x_in[(p * 16 + i) * 128:(p * 16 + i + 1) * 128, :], c_x,
                  reads=[b_xt], writes=[b_xt])
            return xt[:, s2, :]
        return xfn
    for p in range(2):
        emit_l1ffn_pre(R, C, stp, make_xfn(p), b_xt, out, b_out, c_out, p * 16,
                       fhat_d, b_fd, aE, b_aE)
    R.full_barrier()
    emit_l1ffn_main(R, C, stm, w1b, w2b, (BW1, BW2), out, b_out, fhat_d, b_fd,
                    aE, b_aE)
    R.drain("sync", [stm.c_acc, c_out])
    R.emit()
    return nc, R


def kernel_l1ffn_only(x, centroids, gamma, beta, W1, b1, W2, b2):
    m = {"gbT": np.ascontiguousarray(np.stack([gamma.T, beta.T], axis=1)),
         "b1T": np.ascontiguousarray(b1.T), "b2": b2,
         "centT": np.ascontiguousarray(centroids.T)}
    for e in range(8):
        m[f"W1T{e}"] = np.ascontiguousarray(
            W1[e].reshape(32, 128, 8, 128).transpose(0, 3, 2, 1).reshape(4096, 1024))
        m[f"W2T{e}"] = np.ascontiguousarray(W2[e].T)
    nc, R = build_l1ffn_program()
    in_maps = []
    for c in range(8):
        xs = np.ascontiguousarray(x[c * 2:(c + 1) * 2].reshape(4096, 1024))
        in_maps.append(dict(m, x_in=xs))
    res = run_bass_kernel_spmd(nc, in_maps, core_ids=list(range(8)))
    o = np.stack([np.asarray(r["out"]).reshape(2, 2048, 1024) for r in res.results])
    return o.reshape(16, 2048, 1024)



def build_l0attn_program(n_chunks=32):
    nc = bass.Bass("TRN2", target_bir_lowering=False, dynamic_dma_scratch_size=16384)
    x_in = nc.dram_tensor("x_in", [4096, 1024], F32, kind="ExternalInput").ap()
    Wq = nc.dram_tensor("Wq", [1024, 1024], F32, kind="ExternalInput").ap()
    Wk = nc.dram_tensor("Wk", [1024, 1024], F32, kind="ExternalInput").ap()
    Wv = nc.dram_tensor("Wv", [1024, 1024], F32, kind="ExternalInput").ap()
    Wg = nc.dram_tensor("Wg", [1024, 1024], F32, kind="ExternalInput").ap()
    Wo = nc.dram_tensor("Wo", [1024, 1024], F32, kind="ExternalInput").ap()
    Wb = nc.dram_tensor("Wb", [1024, 16], F32, kind="ExternalInput").ap()
    Wa = nc.dram_tensor("Wa", [1024, 16], F32, kind="ExternalInput").ap()
    dtb = nc.dram_tensor("dtb", [16], F32, kind="ExternalInput").ap()
    alog = nc.dram_tensor("alog", [16], F32, kind="ExternalInput").ap()
    nw = nc.dram_tensor("nw", [64], F32, kind="ExternalInput").ap()
    x_out = nc.dram_tensor("x_out", [4096, 1024], F32, kind="ExternalOutput").ap()
    g_d = nc.dram_tensor("g_d", [16, 2048], F32, kind="Internal").ap()
    be_d = nc.dram_tensor("be_d", [16, 2048], F32, kind="Internal").ap()
    gt_d = nc.dram_tensor("gt_d", [128, 8, 2048], BF16, kind="Internal").ap()

    R = Rec(nc)
    C = emit_common(R)
    C.const(1e-5)
    st = l0attn_state(R, "d")
    emit_l0attn_setup(R, C, st)
    xlnt = R.sbuf("d_xlnt", [128, 8, 2048], BF16)
    b_xlnt = Buf("d_xlnt")
    xt = st.fscr.rearrange("p (a b) -> p a b", a=1)
    b_xt = st.b_fscr
    xo = st.S32
    b_xo = st.b_S32
    c_xi = R.ctr("d_cxi", 16)
    c_xo = R.ctr("d_cxo", 16)
    b_gd = Buf("d_gd")
    b_xout = Buf("d_xout")

    for p in range(2):
        def x_ap_fn(i, p=p):
            R.dma("sync", xt[:, 0, :], x_in[(p * 16 + i) * 128:(p * 16 + i + 1) * 128, :],
                  c_xi, reads=[b_xt], writes=[b_xt])
            return xt[:, 0, :]

        def xadd_fn(t, nh, ps, b_ps, p=p):
            sl = t % 2
            rowsl = slice((p * 16 + t) * 128, (p * 16 + t + 1) * 128)
            colsl = slice(nh * 512, (nh + 1) * 512)
            R.dma("sync", xt[:, 0, colsl], x_in[rowsl, colsl],
                  c_xi, reads=[b_xt], writes=[b_xt])
            R.op("vector", lambda e, colsl=colsl, ps=ps: e.tensor_add(
                xo[:, :, :], ps.rearrange("p (a b) -> p a b", b=64),
                xt[:, 0, colsl].rearrange("p (a b) -> p a b", b=64)),
                reads=[b_ps, b_xt], writes=[b_xo])
            R.dma("sync",
                  x_out[rowsl, colsl].rearrange("p (a b) -> p a b", b=64),
                  xo[:, :, :], c_xo, reads=[b_xo], writes=[b_xout])
        emit_l0attn(R, C, st, x_ap_fn, b_xt, xadd_fn, xlnt, b_xlnt, xlnt, b_xlnt,
                    Wq, Wk, Wv, Wg, Wo, Wb, Wa, dtb, alog, nw, g_d, be_d, b_gd,
                    gt_d, Buf("d_gt"), n_chunks=n_chunks)
        R.full_barrier()
    R.drain("sync", [c_xo])
    R.emit()
    return nc, R


def kernel_l0attn_only(x, Wq, Wk, Wv, Wg, Wo, Wb, Wa, dt_bias, A_log, norm_w):
    m = {"Wq": Wq, "Wk": Wk, "Wv": Wv, "Wg": Wg, "Wo": Wo, "Wb": Wb, "Wa": Wa,
         "dtb": dt_bias, "alog": A_log, "nw": norm_w}
    nc, R = build_l0attn_program()
    in_maps = []
    for c in range(8):
        xs = np.ascontiguousarray(x[c * 2:(c + 1) * 2].reshape(4096, 1024))
        in_maps.append(dict(m, x_in=xs))
    res = run_bass_kernel_spmd(nc, in_maps, core_ids=list(range(8)))
    o = np.stack([np.asarray(r["x_out"]).reshape(2, 2048, 1024) for r in res.results])
    return o.reshape(16, 2048, 1024)





def build_fused_program(arena_kb=200):
    nc = bass.Bass("TRN2", target_bir_lowering=False, dynamic_dma_scratch_size=16384)
    DT = nc.dram_tensor
    x_in = DT("x_in", [4096, 1024], F32, kind="ExternalInput").ap()
    Wq = DT("Wq", [1024, 1024], F32, kind="ExternalInput").ap()
    Wk = DT("Wk", [1024, 1024], F32, kind="ExternalInput").ap()
    Wv = DT("Wv", [1024, 1024], F32, kind="ExternalInput").ap()
    Wg = DT("Wg", [1024, 1024], F32, kind="ExternalInput").ap()
    Wo = DT("Wo", [1024, 1024], F32, kind="ExternalInput").ap()
    Wb = DT("Wb", [1024, 16], F32, kind="ExternalInput").ap()
    Wa = DT("Wa", [1024, 16], F32, kind="ExternalInput").ap()
    dtb = DT("dtb", [16], F32, kind="ExternalInput").ap()
    alog = DT("alog", [16], F32, kind="ExternalInput").ap()
    nw = DT("nw", [64], F32, kind="ExternalInput").ap()
    wg_i = DT("wgT", [1024, 3072], I32, kind="ExternalInput").ap()
    wu_i = DT("wuT", [1024, 3072], I32, kind="ExternalInput").ap()
    wd_i = DT("wdT", [3072, 1024], I32, kind="ExternalInput").ap()
    AWq = DT("AWqT", [1024, 1024], F32, kind="ExternalInput").ap()
    AWk = DT("AWkT", [1024, 1024], F32, kind="ExternalInput").ap()
    AWv = DT("AWvT", [1024, 1024], F32, kind="ExternalInput").ap()
    AWo = DT("AWoT", [1024, 1024], F32, kind="ExternalInput").ap()
    bq = DT("bq", [1024], F32, kind="ExternalInput").ap()
    bk = DT("bk", [1024], F32, kind="ExternalInput").ap()
    bv = DT("bv", [1024], F32, kind="ExternalInput").ap()
    bo = DT("bo", [1024], F32, kind="ExternalInput").ap()
    w1 = [DT(f"W1T{e}", [4096, 1024], F32, kind="ExternalInput").ap() for e in range(8)]
    w2 = [DT(f"W2T{e}", [4096, 1024], F32, kind="ExternalInput").ap() for e in range(8)]
    gbT = DT("gbT", [1024, 2, 8], F32, kind="ExternalInput").ap()
    b1T = DT("b1T", [4096, 8], F32, kind="ExternalInput").ap()
    b2 = DT("b2", [8, 1024], F32, kind="ExternalInput").ap()
    centT = DT("centT", [1024, 8], F32, kind="ExternalInput").ap()
    out = DT("out", [4096, 1024], F32, kind="ExternalOutput").ap()
    x_h = DT("x_h", [4096, 1024], F32, kind="Internal").ap()
    wg_b = DT("wg_bf", [1024, 3072], BF16, kind="Internal").ap()
    wu_b = DT("wu_bf", [1024, 3072], BF16, kind="Internal").ap()
    wd_b = DT("wd_bf", [3072, 1024], BF16, kind="Internal").ap()
    g_d = DT("g_d", [16, 2048], F32, kind="Internal").ap()
    be_d = DT("be_d", [16, 2048], F32, kind="Internal").ap()
    fhat_d = DT("fhat_d", [128, 8, 4096], BF16, kind="Internal").ap()
    gt_d = DT("gt_d", [128, 8, 2048], BF16, kind="Internal").ap()
    w1b = [DT(f"w1b{e}", [4096, 1024], BF16, kind="Internal").ap() for e in range(8)]
    w2b = [DT(f"w2b{e}", [4096, 1024], BF16, kind="Internal").ap() for e in range(8)]

    R = Rec(nc)
    R.use_arena(arena_kb * 1024)
    C = emit_common(R)
    C.const(1e-5)
    C.const(1.0 / 127.0)
    aE = R.sbuf("aE", [128, 32, 8], F32)
    b_aE = Buf("aE")
    b_fd, b_gd, b_w, b_xh = Buf("fd"), Buf("gd"), Buf("wbf"), Buf("xh")
    b_out = [Buf(f"out{t}") for t in range(32)]
    c_out = R.ctr("c_out", 16)
    c_xh = R.ctr("c_xh", 16)
    BW1 = [[Buf(f"cv1_{e}_{q}") for q in range(4)] for e in range(8)]
    BW2 = [[Buf(f"cv2_{e}_{q}") for q in range(4)] for e in range(8)]
    c_conv = R.ctr("c_conv", 16)
    conv_jobs = [(wg_b, wg_i, b_w), (wu_b, wu_i, b_w), (wd_b, wd_i, b_w)]
    for e in range(8):
        for q in range(4):
            sl = slice(q * 1024, (q + 1) * 1024)
            conv_jobs.append((w1b[e][sl, :], w1[e][sl, :], BW1[e][q]))
            conv_jobs.append((w2b[e][sl, :], w2[e][sl, :], BW2[e][q]))
    conv_it = iter(conv_jobs)

    def conv_hook():
        try:
            dst, src, bw = next(conv_it)
        except StopIteration:
            return
        R.dma("gpsimd", dst, src, c_conv, writes=[bw])
    R.set_region_base()

    for p in range(2):
        R.open_region()
        st0 = l0attn_state(R, f"a{p}")
        emit_l0attn_setup(R, C, st0)
        xlnt0 = R.sbuf(f"a{p}_xlnt", [128, 8, 2048], BF16)
        b_xlnt0 = Buf(f"a{p}_xlnt")
        xt0 = st0.fscr.rearrange("p (a b) -> p a b", a=1)
        b_xt0 = st0.b_fscr
        xo0 = st0.S32
        b_xo0 = st0.b_S32

        def x_ap0(i, p=p):
            R.dma("sync", xt0[:, 0, :],
                  x_in[(p * 16 + i) * 128:(p * 16 + i + 1) * 128, :],
                  c_xh, reads=[b_xt0], writes=[b_xt0])
            return xt0[:, 0, :]

        def xadd0(t, nh, ps, b_ps, p=p):
            rowsl = slice((p * 16 + t) * 128, (p * 16 + t + 1) * 128)
            colsl = slice(nh * 512, (nh + 1) * 512)
            R.dma("sync", xt0[:, 0, colsl], x_in[rowsl, colsl],
                  c_xh, reads=[b_xt0], writes=[b_xt0])
            R.op("vector", lambda e, colsl=colsl, ps=ps: e.tensor_add(
                xo0[:, :, :], ps.rearrange("p (a b) -> p a b", b=64),
                xt0[:, 0, colsl].rearrange("p (a b) -> p a b", b=64)),
                reads=[b_ps, b_xt0], writes=[b_xo0])
            R.dma("sync", x_h[rowsl, colsl].rearrange("p (a b) -> p a b", b=64),
                  xo0[:, :, :], c_xh, reads=[b_xo0], writes=[b_xh])
        emit_l0attn(R, C, st0, x_ap0, b_xt0, xadd0, xlnt0, b_xlnt0, xlnt0, b_xlnt0,
                    Wq, Wk, Wv, Wg, Wo, Wb, Wa, dtb, alog, nw, g_d, be_d, b_gd,
                    gt_d, Buf(f"gt{p}"), hook=conv_hook)
        R.full_barrier()

        R.open_region()
        stf = l0ffn_state(R, f"f{p}")
        x_sb = R.sbuf(f"f{p}_xsb", [128, 16, 1024], F32)
        b_x = Buf(f"f{p}_x")
        R.dma("sync", x_sb[:, :, :],
              x_h[p * 2048:(p + 1) * 2048, :].rearrange("(a p) d -> p a d", p=128),
              c_xh, reads=[b_xh, b_x], writes=[b_x])
        emit_l0ffn(R, C, stf, x_sb, b_x, wg_b, wu_b, wd_b, b_w)
        R.dma("sync", x_h[p * 2048:(p + 1) * 2048, :].rearrange("(a p) d -> p a d", p=128),
              x_sb[:, :, :], c_xh, reads=[b_x], writes=[b_xh])
        R.full_barrier()

        R.open_region()
        sta = l1attn_state(R, C, f"t{p}")
        emit_l1attn_setup(R, C, sta, f"t{p}")
        xa_st = R.sbuf(f"t{p}_xst", [128, 2, 1024], F32)
        b_xa = Buf(f"t{p}_x")

        def x_ap1(i, p=p):
            R.dma("sync", xa_st[:, 0, :],
                  x_h[(p * 16 + i) * 128:(p * 16 + i + 1) * 128, :], c_xh,
                  reads=[b_xh, b_xa], writes=[b_xa])
            return xa_st[:, 0, :]

        def xadd1(t, nh, ps, b_ps, p=p):
            rowsl = slice((p * 16 + t) * 128, (p * 16 + t + 1) * 128)
            colsl = slice(nh * 512, (nh + 1) * 512)
            R.dma("sync", xa_st[:, 0, colsl], x_h[rowsl, colsl], c_xh,
                  reads=[b_xh, b_xa], writes=[b_xa])
            R.op("vector", lambda e, colsl=colsl, nh=nh, ps=ps: e.scalar_tensor_tensor(
                out=xa_st[:, 1, nh * 512:(nh + 1) * 512], in0=ps, scalar=0.5,
                in1=xa_st[:, 0, colsl], op0=ALU.mult, op1=ALU.add),
                reads=[b_ps, b_xa], writes=[b_xa])
            R.dma("sync", x_h[rowsl, colsl], xa_st[:, 1, colsl], c_xh,
                  reads=[b_xa], writes=[b_xh])
        emit_l1attn(R, C, sta, x_ap1, b_xa, xadd1, AWq, AWk, AWv, AWo, bq, bk, bv, bo)
        R.full_barrier()

        R.open_region()
        stp = l1ffn_pre_state(R, f"q{p}")
        emit_l1ffn_pre_setup(R, C, stp, centT)
        xtq = R.sbuf(f"q{p}_xt", [128, 2, 1024], F32)
        b_xtq = Buf(f"q{p}_xt")

        def x_apq(i, p=p):
            s2 = i % 2
            R.dma("sync", xtq[:, s2, :],
                  x_h[(p * 16 + i) * 128:(p * 16 + i + 1) * 128, :], c_xh,
                  reads=[b_xh, b_xtq], writes=[b_xtq])
            return xtq[:, s2, :]
        emit_l1ffn_pre(R, C, stp, x_apq, b_xtq, out, b_out, c_out, p * 16,
                       fhat_d, b_fd, aE, b_aE)
        R.full_barrier()

    for dst, src, bw in conv_it:
        R.dma("gpsimd", dst, src, c_conv, writes=[bw])
    R.open_region()
    stm = l1ffn_main_state(R, "m")
    emit_l1ffn_main_setup(R, C, stm, gbT, b1T, b2)
    emit_l1ffn_main(R, C, stm, w1b, w2b, (BW1, BW2), out, b_out, fhat_d, b_fd,
                    aE, b_aE)
    R.drain("sync", [stm.c_acc, c_out])
    R.drain("gpsimd", [stm.c_acc])
    R.emit()
    return nc, R


def kernel_fused(inputs):
    ii = inputs
    x = np.asarray(ii["L0_attn_x"])
    ca = np.ascontiguousarray
    m = {"Wq": ca(np.asarray(ii["L0_attn_Wq"])), "Wk": ca(np.asarray(ii["L0_attn_Wk"])),
         "Wv": ca(np.asarray(ii["L0_attn_Wv"])), "Wg": ca(np.asarray(ii["L0_attn_Wg"])),
         "Wo": ca(np.asarray(ii["L0_attn_Wo"])), "Wb": ca(np.asarray(ii["L0_attn_Wb"])),
         "Wa": ca(np.asarray(ii["L0_attn_Wa"])),
         "dtb": ca(np.asarray(ii["L0_attn_dt_bias"])),
         "alog": ca(np.asarray(ii["L0_attn_A_log"])),
         "nw": ca(np.asarray(ii["L0_attn_norm_w"])),
         "wgT": ca(np.asarray(ii["L0_ffn_w_gate"]).T),
         "wuT": ca(np.asarray(ii["L0_ffn_w_up"]).T),
         "wdT": ca(np.asarray(ii["L0_ffn_w_down"]).T),
         "AWqT": ca(np.asarray(ii["L1_attn_Wq"]).T),
         "AWkT": ca(np.asarray(ii["L1_attn_Wk"]).T),
         "AWvT": ca(np.asarray(ii["L1_attn_Wv"]).T),
         "AWoT": ca(np.asarray(ii["L1_attn_Wo"]).T),
         "bq": ca(np.asarray(ii["L1_attn_bq"])), "bk": ca(np.asarray(ii["L1_attn_bk"])),
         "bv": ca(np.asarray(ii["L1_attn_bv"])), "bo": ca(np.asarray(ii["L1_attn_bo"])),
         "gbT": ca(np.stack([np.asarray(ii["L1_ffn_gamma"]).T,
                             np.asarray(ii["L1_ffn_beta"]).T], axis=1)),
         "b1T": ca(np.asarray(ii["L1_ffn_b1"]).T),
         "b2": ca(np.asarray(ii["L1_ffn_b2"])),
         "centT": ca(np.asarray(ii["L1_ffn_centroids"]).T)}
    W1 = np.asarray(ii["L1_ffn_W1"])
    W2 = np.asarray(ii["L1_ffn_W2"])
    for e in range(8):
        m[f"W1T{e}"] = ca(W1[e].reshape(32, 128, 8, 128)
                          .transpose(0, 3, 2, 1).reshape(4096, 1024))
        m[f"W2T{e}"] = ca(W2[e].T)
    nc, R = build_fused_program()
    in_maps = []
    for c in range(8):
        in_maps.append(dict(m, x_in=ca(x[c * 2:(c + 1) * 2].reshape(4096, 1024))))
    res = run_bass_kernel_spmd(nc, in_maps, core_ids=list(range(8)))
    o = np.stack([np.asarray(r["out"]).reshape(2, 2048, 1024) for r in res.results])
    return o.reshape(16, 2048, 1024)


def kernel(**inputs):
    return kernel_fused(inputs)


# version 08 (final)
# speedup vs baseline: 1.1726x; 1.0628x over previous
import contextlib
import math
import numpy as np
import ml_dtypes
import concourse.bass as bass
import concourse.mybir as mybir
from concourse.bass_utils import run_bass_kernel_spmd

F32 = mybir.dt.float32
BF16 = mybir.dt.bfloat16
I32 = mybir.dt.int32
AF = mybir.ActivationFunctionType
ALU = mybir.AluOpType
AX = mybir.AxisListType

ENGINES = ("sync", "scalar", "vector", "tensor", "gpsimd")


class Ctr:
    __slots__ = ("sem", "n", "step", "name")

    def __init__(self, sem, step, name):
        self.sem, self.n, self.step, self.name = sem, 0, step, name


class Buf:
    __slots__ = ("name", "wr", "rd")

    def __init__(self, name):
        self.name = name
        self.wr = {}
        self.rd = {}


class Op:
    __slots__ = ("engine", "fn", "waits", "incs")

    def __init__(self, engine, fn, waits, incs):
        self.engine, self.fn, self.waits, self.incs = engine, fn, waits, incs


class Rec:

    def __init__(self, nc):
        self.nc = nc
        self.ops = []
        self.stack = contextlib.ExitStack()
        self._ctrs = {}
        self.ec = {e: self.ctr("c_" + e, 1) for e in ENGINES}
        self.floor = {e: {} for e in ENGINES}
        self.n_sbuf_bytes = 0

    def ctr(self, name, step=1):
        if name in self._ctrs:
            return self._ctrs[name]
        sem = self.stack.enter_context(self.nc.semaphore(name))
        c = Ctr(sem, step, name)
        self._ctrs[name] = c
        return c

    def sbuf(self, name, shape, dtype):
        itemsz = {F32: 4, BF16: 2, I32: 4}[dtype]
        nb = int(np.prod(shape[1:])) * itemsz
        if getattr(self, "arena", None) is not None:
            off = (self.acur + 3) & ~3
            self.acur = off + nb
            assert self.acur <= self.arena_bytes, \
                f"arena overflow at {name}: {self.acur} > {self.arena_bytes}"
            self.apeak = max(getattr(self, "apeak", 0), self.acur)
            view = self.arena[0:shape[0], off:off + nb].bitcast(dtype)
            if len(shape) > 2:
                dims = " ".join(f"d{i}" for i in range(1, len(shape)))
                view = view.rearrange(
                    f"p ({dims}) -> p {dims}",
                    **{f"d{i}": shape[i] for i in range(1, len(shape) - 1)})
            return view
        t = self.stack.enter_context(self.nc.sbuf_tensor(name, shape, dtype))
        self.n_sbuf_bytes += nb
        return t

    def use_arena(self, nbytes):
        self.arena = self.stack.enter_context(
            self.nc.sbuf_tensor("arena", [128, nbytes], mybir.dt.int8))
        self.arena_bytes = nbytes
        self.acur = 0
        self.apeak = 0

    def open_region(self):
        if getattr(self, "arena", None) is not None:
            self.acur = self.region_base

    def set_region_base(self):
        self.region_base = (self.acur + 3) & ~3

    def psum(self, name, shape, dtype=F32):
        return self.stack.enter_context(self.nc.psum_tensor(name, shape, dtype))

    def op(self, engine, fn, reads=(), writes=(), ctr=None, inc=None):
        myctr = ctr if ctr is not None else self.ec[engine]
        waits = {}

        def need(tokmap):
            for c, v in tokmap.items():
                if waits.get(c, 0) < v:
                    waits[c] = v

        for b in reads:
            need(b.wr)
        for b in writes:
            need(b.wr)
            need(b.rd)
        for c in list(waits):
            if c.step == 16 and waits[c] < c.n:
                waits[c] = c.n
        fl = self.floor[engine]
        wl = []
        for c, v in waits.items():
            if fl.get(c, 0) < v:
                wl.append((c, v))
                fl[c] = v
        inc_amt = myctr.step if inc is None else inc
        myctr.n += inc_amt
        tok = (myctr, myctr.n)
        for b in writes:
            b.wr[myctr] = myctr.n
            b.rd = {}
        for b in reads:
            b.rd[myctr] = myctr.n
        self.ops.append(Op(engine, fn, wl, [(myctr, inc_amt)]))
        return tok

    def dma(self, queue, out, in_, ctr, reads=(), writes=(), **kw):
        def fn(eng):
            return eng.dma_start(out=out, in_=in_, **kw)
        return self.op(queue, fn, reads=reads, writes=writes, ctr=ctr, inc=16)

    def drain(self, engine, ctrs):
        self.ops.append(Op(engine, None, [(c, c.n) for c in ctrs if c.n], []))

    def full_barrier(self):
        bar = self.ctr("c_bar", 1)
        for e in ENGINES:
            self.ops.append(Op(e, lambda eng: eng.nop(), [], [(bar, 1)]))
            bar.n += 1
        dma_waits = [(c, c.n) for c in self._ctrs.values() if c.step == 16 and c.n]
        for e in ENGINES:
            wl = [(bar, bar.n)] + dma_waits
            fl = self.floor[e]
            keep = [(c, v) for c, v in wl if fl.get(c, 0) < v]
            for c, v in keep:
                fl[c] = v
            self.ops.append(Op(e, None, keep, []))

    def emit(self):
        nc = self.nc
        per = {e: [] for e in ENGINES}
        for o in self.ops:
            per[o.engine].append(o)
        with nc.Block() as block:
            for ename in ENGINES:
                items = per[ename]
                if not items:
                    continue

                def make(items):
                    def emit_fn(eng):
                        for it in items:
                            for (c, v) in it.waits:
                                eng.wait_ge(c.sem, v)
                            if it.fn is not None:
                                instr = it.fn(eng)
                                for (c, k) in it.incs:
                                    instr.then_inc(c.sem, k)
                    return emit_fn

                getattr(block, ename)(make(items))


class NS:
    pass



def emit_common(R: Rec):
    C = NS()
    C.ident = R.sbuf("c_ident", [128, 128], BF16)
    C.b_ident = Buf("ident")
    C.onesf = R.sbuf("c_ones", [128, 128], F32)
    C.b_onesf = Buf("onesf")
    idf = R.sbuf("c_identf", [128, 128], F32)
    b_idf = Buf("identf")
    C.idf, C.b_idf = idf, b_idf
    R.op("vector", lambda e: e.memset(C.onesf[:], 1.0), writes=[C.b_onesf])
    R.op("gpsimd", lambda e: e.affine_select(
        out=idf[:, :], in_=C.onesf[:, :], pattern=[[-1, 128]],
        compare_op=ALU.is_equal, fill=0.0, base=0, channel_multiplier=1),
        reads=[C.b_onesf], writes=[b_idf])
    R.op("vector", lambda e: e.tensor_copy(C.ident[:], idf[:]),
         reads=[b_idf], writes=[C.b_ident])
    C.ones1 = R.sbuf("c_ones1", [1, 512], BF16)
    C.b_ones1 = Buf("ones1")
    R.op("vector", lambda e: e.memset(C.ones1[:], 1.0), writes=[C.b_ones1])
    C.ps_tr = R.psum("c_ptr", [128, 2, 1024], BF16)
    C.b_ptr = [Buf(f"c_ptr{i}") for i in range(2)]
    C.ps_big = R.psum("c_pbig", [128, 4, 512], F32)
    C.b_big = [Buf(f"c_pbig{i}") for i in range(4)]
    C.ps_ab = R.psum("c_pab", [128, 2, 512], F32)
    C.b_ab = [Buf(f"c_pab{i}") for i in range(2)]
    C.cpool = R.sbuf("c_pool", [128, 8], F32)
    C.b_cpool = Buf("cpool")
    C._consts = {}

    def const(val):
        val = float(val)
        if val not in C._consts:
            idx = len(C._consts)
            assert idx < 8
            ap = C.cpool[:, idx:idx + 1]
            R.op("vector", lambda e, ap=ap, val=val: e.memset(ap, val),
                 writes=[C.b_cpool])
            R.nc.const_aps.register(F32, val, ap)
            C._consts[val] = ap
        return val
    C.const = const
    return C


def big_all(C):
    return [C.b_big[0], C.b_big[1], C.b_big[2], C.b_big[3]]



def ln_state(R: Rec, name, sq=None, b_sq=None, xln=None, b_xlnb=None):
    st = NS()
    if sq is None:
        st.sq = R.sbuf(name + "_sq", [128, 1024], BF16)
        st.b_sq = Buf(name + "_sq")
    else:
        st.sq, st.b_sq = sq, b_sq
    st.st = R.sbuf(name + "_st", [128, 6], F32)
    st.b_st = Buf(name + "_st")
    if xln is None:
        st.xln = R.sbuf(name + "_xln", [128, 1, 1024], BF16)
        st.b_xln = [Buf(name + "_xln0")]
    else:
        st.xln = xln
        st.b_xln = [b_xlnb]
    st.b_xln.append(st.b_xln[0])
    return st


def emit_ln_xlnt(R: Rec, C, st, x_ap_fn, b_x, xlnt_t, b_xlnt, n_tiles=16,
                 dump_xln=None):
    for i in range(n_tiles):
        xi = x_ap_fn(i)
        R.op("scalar", lambda e, xi=xi: e.activation(
            out=st.sq[:], in_=xi, func=AF.Square, accum_out=st.st[:, 1:2]),
            reads=[b_x], writes=[st.b_sq, st.b_st])
        R.op("vector", lambda e, xi=xi: e.reduce_sum(
            st.st[:, 0:1], xi, axis=AX.X), reads=[b_x], writes=[st.b_st])
        R.op("vector", lambda e: e.tensor_scalar_mul(st.st[:, 2:3], st.st[:, 0:1], 1.0 / 1024),
             reads=[st.b_st], writes=[st.b_st])
        R.op("vector", lambda e: e.tensor_mul(st.st[:, 3:4], st.st[:, 2:3], st.st[:, 2:3]),
             reads=[st.b_st], writes=[st.b_st])
        R.op("vector", lambda e: e.scalar_tensor_tensor(
            out=st.st[:, 3:4], in0=st.st[:, 1:2], scalar=1.0 / 1024, in1=st.st[:, 3:4],
            op0=ALU.mult, op1=ALU.subtract), reads=[st.b_st], writes=[st.b_st])
        R.op("scalar", lambda e: e.activation(
            out=st.st[:, 4:5], in_=st.st[:, 3:4], func=AF.Sqrt, bias=1e-5),
            reads=[st.b_st, C.b_cpool], writes=[st.b_st])
        R.op("vector", lambda e: e.reciprocal(st.st[:, 5:6], st.st[:, 4:5]),
             reads=[st.b_st], writes=[st.b_st])
        xo = st.xln[:, 0, :]
        R.op("vector", lambda e, xi=xi, xo=xo: e.tensor_scalar(
            out=xo, in0=xi, scalar1=st.st[:, 2:3], scalar2=st.st[:, 5:6],
            op0=ALU.subtract, op1=ALU.mult),
            reads=[b_x, st.b_st], writes=[st.b_xln[i % 2]])
        if dump_xln is not None:
            R.dma("sync", dump_xln[0][dump_xln[1], :, i, :], xo, dump_xln[2],
                  reads=[st.b_xln[i % 2]], writes=[dump_xln[3]])
        for c in range(8):
            pt = C.ps_tr[:, c % 2, 0:128]
            xoc = xo.rearrange("p (c f) -> p c f", f=128)[:, c, :]
            R.op("tensor", lambda e, xoc=xoc, pt=pt: e.transpose(pt, xoc, C.ident[:]),
                 reads=[st.b_xln[i % 2], C.b_ident], writes=[C.b_ptr[c % 2]])
            R.op("vector", lambda e, c=c, i=i, pt=pt: e.tensor_copy(
                xlnt_t[:, c, i * 128:(i + 1) * 128], pt),
                reads=[C.b_ptr[c % 2]], writes=[b_xlnt])



def emit_int_to_bf16(R: Rec, tile, b_tile, c_ld, c_st, src_ap, dst_ap,
                     rows, cols, b_dst):
    assert rows % 128 == 0
    for r in range(rows // 128):
        s = r % 2
        R.dma("gpsimd", tile[:, s, 0:cols], src_ap[r * 128:(r + 1) * 128, :],
              c_ld[s], writes=[b_tile[s]])
        R.dma("gpsimd", dst_ap[r * 128:(r + 1) * 128, :], tile[:, s, 0:cols],
              c_st[s], reads=[b_tile[s]], writes=[b_dst])



def l0ffn_state(R: Rec, name):
    st = NS()
    st.xlnt = R.sbuf(name + "_xlnt", [128, 8, 2048], BF16)
    st.b_xlnt = Buf(name + "_xlnt")
    st.wg_s = R.sbuf(name + "_wg", [128, 2, 8, 128], BF16)
    st.wu_s = R.sbuf(name + "_wu", [128, 2, 8, 128], BF16)
    st.wd_s = R.sbuf(name + "_wd", [128, 2, 1024], BF16)
    st.b_wg = [Buf(name + f"_wg{i}") for i in range(2)]
    st.b_wu = [Buf(name + f"_wu{i}") for i in range(2)]
    st.b_wd = [Buf(name + f"_wd{i}") for i in range(2)]
    st.c_gu = [R.ctr(name + f"_cgu{i}", 16) for i in range(2)]
    st.c_wd = [R.ctr(name + f"_cwd{i}", 16) for i in range(2)]
    st.h_s = R.sbuf(name + "_h", [128, 24, 1024], BF16)
    st.b_h = Buf(name + "_h")
    st.sg = R.sbuf(name + "_sg", [128, 512], F32)
    st.b_sg = Buf(name + "_sg")
    st.ln = ln_state(R, name + "_ln")
    return st


def emit_l0ffn(R: Rec, C, st, x_t, b_x, wgT_bf, wuT_bf, wdT_bf, b_w):
    S = C.const(1.0 / 127.0)
    NT, NHB = 16, 24
    emit_ln_xlnt(R, C, st.ln, lambda i: x_t[:, i, :], b_x, st.xlnt, st.b_xlnt, NT)
    ps_g, ps_u = C.ps_ab[:, 0, :], C.ps_ab[:, 1, :]
    b_pg, b_pu = C.b_ab[0], C.b_ab[1]
    ps_y = [C.ps_big[:, 0:2, :].rearrange("p a b -> p (a b)"),
            C.ps_big[:, 2:4, :].rearrange("p a b -> p (a b)")]
    b_py = [[C.b_big[0], C.b_big[1]], [C.b_big[2], C.b_big[3]]]

    def load_gu(hb, slot):
        R.dma("scalar", st.wg_s[:, slot, :, :],
              wgT_bf[:, hb * 128:(hb + 1) * 128].rearrange("(c k) h -> k c h", k=128),
              st.c_gu[slot], reads=[b_w], writes=[st.b_wg[slot]])
        R.dma("scalar", st.wu_s[:, slot, :, :],
              wuT_bf[:, hb * 128:(hb + 1) * 128].rearrange("(c k) h -> k c h", k=128),
              st.c_gu[slot], reads=[b_w], writes=[st.b_wu[slot]])

    def load_wd(hb, slot):
        R.dma("sync", st.wd_s[:, slot, :], wdT_bf[hb * 128:(hb + 1) * 128, :],
              st.c_wd[slot], reads=[b_w], writes=[st.b_wd[slot]])

    for half in range(2):
        hbase = half * 1024
        load_gu(0, 0)
        for hb in range(NHB):
            slot = hb % 2
            if hb + 1 < NHB:
                load_gu(hb + 1, (hb + 1) % 2)
            for th in range(2):
                rhs = st.xlnt[:, :, hbase + th * 512: hbase + (th + 1) * 512]

                def mm_gu(e, slot=slot, rhs=rhs):
                    for cch in range(8):
                        e.matmul(ps_g, st.wg_s[:, slot, cch, :],
                                 rhs[:, cch, :], start=(cch == 0), stop=(cch == 7))
                    for cch in range(8):
                        last = e.matmul(ps_u, st.wu_s[:, slot, cch, :],
                                        rhs[:, cch, :], start=(cch == 0), stop=(cch == 7))
                    return last
                R.op("tensor", mm_gu, reads=[st.b_wg[slot], st.b_wu[slot], st.b_xlnt],
                     writes=[b_pg, b_pu])
                R.op("scalar", lambda e: e.activation(
                    out=st.sg[:, :], in_=ps_g, func=AF.Silu, scale=S),
                    reads=[b_pg, C.b_cpool], writes=[st.b_sg])
                R.op("vector", lambda e, th=th, hb=hb: e.scalar_tensor_tensor(
                    out=st.h_s[:, hb, th * 512:(th + 1) * 512], in0=ps_u,
                    scalar=S, in1=st.sg[:, :], op0=ALU.mult, op1=ALU.mult),
                    reads=[b_pu, st.b_sg], writes=[st.b_h])
        for o in range(4):
            load_wd(0, 0)
            for hb in range(NHB):
                slot = hb % 2
                if hb + 1 < NHB:
                    load_wd(hb + 1, (hb + 1) % 2)
                for tt in range(2):
                    col = o * 256 + tt * 128

                    def mm_d(e, hb=hb, slot=slot, tt=tt, col=col):
                        e.matmul(ps_y[tt][:, 0:512], st.h_s[:, hb, col:col + 128],
                                 st.wd_s[:, slot, 0:512],
                                 start=(hb == 0), stop=(hb == NHB - 1))
                        return e.matmul(ps_y[tt][:, 512:1024],
                                        st.h_s[:, hb, col:col + 128],
                                        st.wd_s[:, slot, 512:1024],
                                        start=(hb == 0), stop=(hb == NHB - 1))
                    R.op("tensor", mm_d, reads=[st.b_h, st.b_wd[slot]],
                         writes=b_py[tt])
            for tt in range(2):
                ti = half * 8 + o * 2 + tt
                R.op("vector", lambda e, tt=tt, ti=ti: e.scalar_tensor_tensor(
                    out=x_t[:, ti, :], in0=ps_y[tt], scalar=S, in1=x_t[:, ti, :],
                    op0=ALU.mult, op1=ALU.add),
                    reads=b_py[tt] + [b_x], writes=[b_x])



def l1attn_state(R: Rec, C, name):
    st = NS()
    st.xlnt = R.sbuf(name + "_xlnt", [128, 8, 2048], BF16)
    st.b_xlnt = Buf(name + "_xlnt")
    st.v = R.sbuf(name + "_v", [128, 16, 16 * 65], BF16)
    st.b_v = Buf(name + "_v")
    st.q = R.sbuf(name + "_q", [128, 8, 2048], BF16)
    st.k = R.sbuf(name + "_k", [128, 8, 2048], BF16)
    st.b_q = Buf(name + "_q")
    st.b_k = Buf(name + "_k")
    st.w = R.sbuf(name + "_w", [128, 8, 512], BF16)
    st.b_w = [Buf(name + "_w0")]
    st.c_w = [R.ctr(name + "_cw0", 16)]
    st.att = st.w[:, :, :].rearrange("p a b -> p (a b)")[:, 0:2048].rearrange(
        "p (a b) -> p a b", a=16)
    st.b_att = st.b_w[0]
    vflat = st.v[:, :, :].rearrange("p a b -> p (a b)")
    st.q2 = vflat[:, 9216:9728]
    st.b_q2 = st.b_v
    st.nrm = vflat[0:16, 0:4096].rearrange("p (a b) -> p a b", a=2)
    st.b_nrm = st.b_v
    st.br = R.sbuf(name + "_br", [1, 2, 1024], BF16)
    st.b_br = Buf(name + "_br")
    st.c_br = R.ctr(name + "_cbr", 16)
    st.sums = R.sbuf(name + "_sums", [128, 2], F32)
    st.b_sums = Buf(name + "_sums")
    st.aop = R.sbuf(name + "_aop", [128, 128], BF16)
    st.b_aop = Buf(name + "_aop")
    st.maskT = R.sbuf(name + "_maskT", [128, 128], F32)
    st.b_maskT = Buf(name + "_maskT")
    st.BO = R.sbuf(name + "_BO", [128, 8, 16], BF16)
    st.b_BO = Buf(name + "_BO")
    st.EX = vflat[0:16, 4096:5120].rearrange("p (a b) -> p a b", a=8)
    st.b_EX = st.b_v
    st.ln = ln_state(R, name + "_ln",
                     sq=vflat[:, 7168:8192], b_sq=st.b_v,
                     xln=vflat[:, 6144:7168].rearrange("p (a b) -> p a b", a=1),
                     b_xlnb=st.b_v)
    return st


def emit_l1attn_setup(R: Rec, C, st, name):
    scrf = C.idf[:, 0:32]
    b_scrf = C.b_idf
    zer, b_zer = C.idf, C.b_idf
    R.op("vector", lambda e: e.memset(zer[:], 0.0), reads=[b_zer], writes=[b_zer])
    R.op("gpsimd", lambda e: e.affine_select(
        out=st.maskT[:, :], in_=zer[:, :], pattern=[[1, 128]],
        compare_op=ALU.is_ge, fill=-10000.0, base=0, channel_multiplier=-1),
        reads=[b_zer], writes=[st.b_maskT])
    scrf16 = zer[0:16, :]
    b_s16 = b_zer
    ones16 = C.onesf[0:16, :]
    b_o16 = C.b_onesf
    for c in range(8):
        R.op("gpsimd", lambda e, c=c: e.affine_select(
            out=scrf[:, 0:16], in_=C.onesf[:, 0:16], pattern=[[-64, 16]],
            compare_op=ALU.is_ge, fill=0.0, base=128 * c, channel_multiplier=1),
            reads=[C.b_onesf], writes=[b_scrf])
        R.op("gpsimd", lambda e, c=c: e.affine_select(
            out=scrf[:, 16:32], in_=scrf[:, 0:16], pattern=[[64, 16]],
            compare_op=ALU.is_ge, fill=0.0, base=63 - 128 * c, channel_multiplier=-1),
            reads=[b_scrf], writes=[b_scrf])
        R.op("vector", lambda e, c=c: e.tensor_copy(st.BO[:, c, :], scrf[:, 16:32]),
             reads=[b_scrf], writes=[st.b_BO])



def emit_l1attn(R: Rec, C, st, x_ap_fn, b_x, xadd_fn, WqT, WkT, WvT, WoT,
                bq, bk, bv, bo, dump=None):
    def dmp(key, sb_ap, b):
        if dump is not None:
            R.dma("sync", dump[key][dump["p"]], sb_ap, dump["c"],
                  reads=[b], writes=[dump["b"]])
    NT = 16
    ESC = C.const(1.0 / (math.sqrt(128.0) * 1.5))
    emit_ln_xlnt(R, C, st.ln, x_ap_fn, b_x, st.xlnt, st.b_xlnt, NT,
                 dump_xln=(dump or {}).get("xlnhook"))

    dmp("d_xlnt", st.xlnt[:, :, :], st.b_xlnt)
    R.dma("gpsimd", st.br[0:1, 0, :], bq[None, :], st.c_br,
          reads=[st.b_br], writes=[st.b_br])
    R.dma("gpsimd", st.br[0:1, 1, :], bk[None, :], st.c_br,
          reads=[st.b_br], writes=[st.b_br])

    def loadW(W, slot, col0):
        R.dma("gpsimd", st.w[:, :, :],
              W[:, col0:col0 + 512].rearrange("(c k) n -> k c n", k=128),
              st.c_w[0], writes=[st.b_w[0]])

    for iqk, (W, brow, dst, b_dst) in enumerate(
            [(WqT, 0, st.q, st.b_q), (WkT, 1, st.k, st.b_k)]):
        for hg in range(2):
            loadW(W, iqk, hg * 512)
            for tc in range(4):
                for c in range(4):
                    cg = hg * 4 + c
                    ps = C.ps_ab[:, iqk, :]
                    b_ps = C.b_ab[iqk]

                    def mm_qk(e, c=c, tc=tc, iqk=iqk, brow=brow, cg=cg, ps=ps):
                        for cc in range(8):
                            e.matmul(ps, st.w[:, cc, c * 128:(c + 1) * 128],
                                     st.xlnt[:, cc, tc * 512:(tc + 1) * 512],
                                     start=(cc == 0), stop=False)
                        return e.matmul(ps, st.br[0:1, brow, cg * 128:(cg + 1) * 128],
                                        C.ones1[0:1, 0:512], start=False, stop=True)
                    R.op("tensor", mm_qk, reads=[st.b_w[0], st.b_xlnt, st.b_br,
                                                 C.b_ones1], writes=[b_ps])
                    R.op("vector", lambda e, tc=tc, cg=cg, dst=dst, ps=ps: e.tensor_copy(
                        dst[:, cg, tc * 512:(tc + 1) * 512], ps),
                        reads=[b_ps], writes=[b_dst])
                    R.op("scalar", lambda e, tc=tc, cg=cg, dst=dst: e.activation(
                        out=st.q2[:, :], in_=dst[:, cg, tc * 512:(tc + 1) * 512],
                        func=AF.Square),
                        reads=[b_dst], writes=[st.b_q2])
                    R.op("tensor", lambda e, cg=cg, tc=tc: e.matmul(
                        C.ps_big[0:16, tc, :], st.BO[:, cg, :], st.q2[:, :],
                        start=(cg == 0), stop=(cg == 7)),
                        reads=[st.b_BO, st.b_q2], writes=[C.b_big[tc]])
        for tc in range(4):
            R.op("vector", lambda e, tc=tc, iqk=iqk: e.tensor_copy(
                st.nrm[:, iqk, tc * 512:(tc + 1) * 512], C.ps_big[0:16, tc, :]),
                reads=[C.b_big[tc]], writes=[st.b_nrm])
    nf = st.nrm[:, :, :].rearrange("p a b -> p (a b)")
    R.op("vector", lambda e: e.tensor_scalar_max(nf, nf, 1e-16),
         reads=[st.b_nrm], writes=[st.b_nrm])
    R.op("scalar", lambda e: e.activation(out=nf, in_=nf, func=AF.Ln),
         reads=[st.b_nrm, C.b_cpool], writes=[st.b_nrm])
    R.op("scalar", lambda e: e.activation(out=nf, in_=nf, func=AF.Exp, scale=C.const(-0.5)),
         reads=[st.b_nrm, C.b_cpool], writes=[st.b_nrm])
    scrf16 = C.idf[0:16, :]
    for c in range(8):
        R.op("gpsimd", lambda e, c=c: e.affine_select(
            out=scrf16[:, :], in_=C.onesf[0:16, :], pattern=[[1, 128]],
            compare_op=ALU.is_ge, fill=0.0, base=128 * c, channel_multiplier=-64),
            reads=[C.b_onesf, C.b_idf], writes=[C.b_idf])
        R.op("gpsimd", lambda e, c=c: e.affine_select(
            out=scrf16[:, :], in_=scrf16[:, :], pattern=[[-1, 128]],
            compare_op=ALU.is_ge, fill=0.0, base=63 - 128 * c, channel_multiplier=64),
            reads=[C.b_idf], writes=[C.b_idf])
        R.op("vector", lambda e, c=c: e.tensor_copy(st.EX[:, c, :], scrf16[:, :]),
             reads=[C.b_idf], writes=[st.b_EX])
    for iqk, (dst, b_dst) in enumerate([(st.q, st.b_q), (st.k, st.b_k)]):
        for cg in range(8):
            for tc in range(4):
                ps = C.ps_ab[:, iqk, :]
                b_ps = C.b_ab[iqk]
                R.op("tensor", lambda e, cg=cg, tc=tc, iqk=iqk, ps=ps: e.matmul(
                    ps, st.EX[:, cg, :], st.nrm[:, iqk, tc * 512:(tc + 1) * 512],
                    start=True, stop=True),
                    reads=[st.b_EX, st.b_nrm], writes=[b_ps])
                R.op("vector", lambda e, cg=cg, tc=tc, dst=dst, ps=ps: e.tensor_tensor(
                    out=dst[:, cg, tc * 512:(tc + 1) * 512],
                    in0=dst[:, cg, tc * 512:(tc + 1) * 512], in1=ps, op=ALU.mult),
                    reads=[b_ps, b_dst], writes=[b_dst])

    dmp("d_q", st.q[:, :, :], st.b_q)
    dmp("d_k", st.k[:, :, :], st.b_k)
    R.dma("gpsimd", st.br[0:1, 0, :], bv[None, :], st.c_br,
          reads=[st.b_br], writes=[st.b_br])
    R.op("vector", lambda e: e.memset(
        st.v[:, :, :].rearrange("p a (h w) -> p a h w", w=65)[:, :, :, 64:65], 1.0),
        reads=[st.b_v], writes=[st.b_v])
    for nh in range(2):
        loadW(WvT, 0, nh * 512)
        for t in range(NT):
            ps = C.ps_ab[:, t % 2, :]
            b_ps = C.b_ab[t % 2]

            def mm_v(e, t=t, nh=nh, ps=ps):
                for cc in range(8):
                    e.matmul(ps, st.xlnt[:, cc, t * 128:(t + 1) * 128],
                             st.w[:, cc, :], start=(cc == 0), stop=False)
                return e.matmul(ps, C.ones1[0:1, 0:128],
                                st.br[0:1, 0, nh * 512:(nh + 1) * 512],
                                start=False, stop=True)
            R.op("tensor", mm_v, reads=[st.b_w[0], st.b_xlnt, C.b_ones1, st.b_br],
                 writes=[b_ps])
            vout = st.v[:, t, :].rearrange("p (h w) -> p h w", w=65)[
                :, nh * 8:(nh + 1) * 8, 0:64]
            R.op("vector", lambda e, vout=vout, ps=ps: e.tensor_copy(
                vout, ps.rearrange("p (h w) -> p h w", w=64)),
                reads=[b_ps], writes=[st.b_v])

    dmp("d_v", st.v[:, :, :], st.b_v)
    aoT, b_aoT = st.xlnt, st.b_xlnt
    scT = C.ps_big[:, :, :].rearrange("p a b -> p (a b)")
    for hp in range(8):
        for qt in range(NT):
            ntk = qt + 1
            for sub in range(2):
                h = 2 * hp + sub
                qs = st.q[sub * 64:sub * 64 + 64, hp, :]
                ks = st.k[sub * 64:sub * 64 + 64, hp, :]

                def mm_sc(e, qs=qs, ks=ks, qt=qt, ntk=ntk):
                    last = None
                    for j in range(ntk):
                        last = e.matmul(scT[:, j * 128:(j + 1) * 128],
                                        ks[:, j * 128:(j + 1) * 128],
                                        qs[:, qt * 128:(qt + 1) * 128],
                                        start=True, stop=True)
                    return last
                R.op("tensor", mm_sc, reads=[st.b_q, st.b_k], writes=big_all(C))
                R.op("vector", lambda e, qt=qt: e.tensor_tensor(
                    out=scT[:, qt * 128:(qt + 1) * 128],
                    in0=scT[:, qt * 128:(qt + 1) * 128], in1=st.maskT[:, :],
                    op=ALU.add),
                    reads=[st.b_maskT] + big_all(C), writes=big_all(C))
                R.op("scalar", lambda e, ntk=ntk: e.activation(
                    out=st.att[:, 0:ntk, :].rearrange("p a b -> p (a b)"),
                    in_=scT[:, 0:ntk * 128], func=AF.Exp, scale=ESC),
                    reads=big_all(C) + [C.b_cpool], writes=[st.b_att])

                ps = C.ps_ab[:, sub, :]
                b_ps = C.b_ab[sub]

                def mm_ao(e, h=h, ntk=ntk, ps=ps):
                    last = None
                    for j in range(ntk):
                        last = e.matmul(ps[:, 0:65], st.att[:, j, :],
                                        st.v[:, j, h * 65:(h + 1) * 65],
                                        start=(j == 0), stop=(j == ntk - 1))
                    return last
                R.op("tensor", mm_ao, reads=[st.b_att, st.b_v], writes=[b_ps])
                R.op("vector", lambda e, sub=sub, ps=ps: e.reciprocal(
                    st.sums[:, sub:sub + 1], ps[:, 64:65]),
                    reads=[b_ps], writes=[st.b_sums])
                R.op("vector", lambda e, sub=sub, ps=ps: e.tensor_scalar(
                    out=st.aop[:, sub * 64:sub * 64 + 64], in0=ps[:, 0:64],
                    scalar1=st.sums[:, sub:sub + 1], scalar2=None, op0=ALU.mult),
                    reads=[b_ps, st.b_sums], writes=[st.b_aop])
            R.op("tensor", lambda e, qt=qt: e.transpose(
                C.ps_tr[:, qt % 2, 0:128], st.aop[:, :], C.ident[:]),
                reads=[st.b_aop, C.b_ident], writes=[C.b_ptr[qt % 2]])
            R.op("vector", lambda e, hp=hp, qt=qt: e.tensor_copy(
                aoT[:, hp, qt * 128:(qt + 1) * 128], C.ps_tr[:, qt % 2, 0:128]),
                reads=[C.b_ptr[qt % 2]], writes=[b_aoT])

    dmp("d_ao", st.xlnt[:, :, :], st.b_xlnt)
    R.dma("gpsimd", st.br[0:1, 0, :], bo[None, :], st.c_br,
          reads=[st.b_br], writes=[st.b_br])
    for nh in range(2):
        loadW(WoT, nh % 2, nh * 512)
        for t in range(NT):
            ps = C.ps_ab[:, t % 2, :]
            b_ps = C.b_ab[t % 2]

            def mm_o(e, t=t, nh=nh, ps=ps):
                for cc in range(8):
                    e.matmul(ps, aoT[:, cc, t * 128:(t + 1) * 128],
                             st.w[:, cc, :], start=(cc == 0), stop=False)
                return e.matmul(ps, C.ones1[0:1, 0:128],
                                st.br[0:1, 0, nh * 512:(nh + 1) * 512],
                                start=False, stop=True)
            R.op("tensor", mm_o, reads=[st.b_w[0], b_aoT, C.b_ones1, st.b_br],
                 writes=[b_ps])
            xadd_fn(t, nh, ps, b_ps)



def l1ffn_pre_state(R: Rec, name):
    st = NS()
    st.cent = R.sbuf(name + "_cent", [128, 8, 8], F32)
    st.b_cent = Buf(name + "_cent")
    st.xf = R.sbuf(name + "_xf", [128, 1024], F32)
    st.b_xf = Buf(name + "_xf")
    st.iot = R.sbuf(name + "_iot", [128, 8], F32)
    st.b_iot = Buf(name + "_iot")
    st.ftT = R.sbuf(name + "_ftT", [128, 8, 128], F32)
    st.b_ftT = Buf(name + "_ftT")
    st.sc = R.sbuf(name + "_sc", [128, 5 * 8], F32)
    st.b_sc = Buf(name + "_sc")
    st.fhat = R.sbuf(name + "_fhat", [128, 1024], BF16)
    st.b_fhat = Buf(name + "_fhat")
    st.fhT = R.sbuf(name + "_fhT", [128, 8, 128], BF16)
    st.b_fhT = Buf(name + "_fhT")
    st.c_f = R.ctr(name + "_cf", 16)
    st.base = R.sbuf(name + "_base", [128, 2, 1024], F32)
    st.b_base = [Buf(name + f"_base{i}") for i in range(2)]
    st.c_ld = R.ctr(name + "_cld", 16)
    st.ln = ln_state(R, name + "_ln")
    return st


def l1ffn_main_state(R: Rec, name):
    st = NS()
    st.hrT = R.sbuf(name + "_hrT", [128, 32, 1024], BF16)
    st.b_hrT = Buf(name + "_hrT")
    st.he = R.sbuf(name + "_he", [128, 8, 1024], BF16)
    st.b_he = Buf(name + "_he")
    st.fsl = R.sbuf(name + "_fsl", [128, 8, 1024], BF16)
    st.b_fsl = Buf(name + "_fsl")
    st.c_fsl = R.ctr(name + "_cfsl", 16)
    st.w1 = R.sbuf(name + "_w1", [128, 3, 8, 128], BF16)
    st.b_w1 = [Buf(name + f"_w1{i}") for i in range(3)]
    st.c_w1 = [R.ctr(name + f"_cw1{i}", 16) for i in range(3)]
    st.w2 = R.sbuf(name + "_w2", [128, 8, 512], BF16)
    st.b_w2 = [Buf(name + f"_w2{i}") for i in range(8)]
    st.c_w2 = [R.ctr(name + f"_cw2{i}", 16) for i in range(8)]
    st.stage = R.sbuf(name + "_stage", [128, 8, 512], F32)
    st.b_stage = [Buf(name + f"_stage{i}") for i in range(8)]
    st.c_acc = R.ctr(name + "_cacc", 16)
    st.gb = R.sbuf(name + "_gb", [128, 2, 8, 8], F32)
    st.b_gb = Buf(name + "_gb")
    st.b1 = R.sbuf(name + "_b1", [128, 32, 8], F32)
    st.b_b1 = Buf(name + "_b1")
    st.b2r = R.sbuf(name + "_b2r", [1, 8, 1024], BF16)
    st.b_b2r = Buf(name + "_b2r")
    st.c_ld = R.ctr(name + "_cld2", 16)
    return st


def emit_l1ffn_pre_setup(R: Rec, C, st, centT):
    R.dma("gpsimd", st.cent[:, :, :],
          centT.rearrange("(c k) e -> k c e", k=128), st.c_ld,
          reads=[st.b_cent], writes=[st.b_cent])
    R.op("gpsimd", lambda e: e.iota(st.iot[:, :], pattern=[[1, 8]], base=0,
                                    channel_multiplier=0,
                                    allow_small_or_imprecise_dtypes=True),
         reads=[st.b_iot], writes=[st.b_iot])
    R.op("vector", lambda e: e.memset(st.sc[:, 32:40], 0.0),
         reads=[st.b_sc], writes=[st.b_sc])


def emit_l1ffn_main_setup(R: Rec, C, st, gbT, b1T, b2):
    R.dma("gpsimd", st.gb[:, :, :, :],
          gbT.rearrange("(c k) g e -> k g c e", k=128), st.c_ld, writes=[st.b_gb])
    R.dma("gpsimd", st.b1[:, :, :],
          b1T.rearrange("(t k) e -> k t e", k=128), st.c_ld, writes=[st.b_b1])
    R.dma("gpsimd", st.b2r[0:1, :, :], b2[None, :, :], st.c_ld, writes=[st.b_b2r])


def emit_l1ffn_pre(R: Rec, C, st, x_ap_fn, b_x, out_hbm, b_out, c_out, toff,
                   fhat_d, b_fd, aE, b_aE):
    lnst = st.ln
    R.op("gpsimd", lambda e: e.affine_select(
        out=C.idf[:, :], in_=C.onesf[:, :], pattern=[[-1, 128]],
        compare_op=ALU.is_equal, fill=0.0, base=0, channel_multiplier=1),
        reads=[C.b_onesf, C.b_idf], writes=[C.b_idf])
    for i in range(16):
        ti = toff + i
        xi = x_ap_fn(i)
        R.op("scalar", lambda e, xi=xi: e.activation(
            out=lnst.sq[:], in_=xi, func=AF.Square, accum_out=lnst.st[:, 1:2]),
            reads=[b_x], writes=[lnst.b_sq, lnst.b_st])
        R.op("vector", lambda e, xi=xi: e.reduce_sum(
            lnst.st[:, 0:1], xi, axis=AX.X), reads=[b_x], writes=[lnst.b_st])
        R.op("vector", lambda e: e.tensor_scalar_mul(
            lnst.st[:, 2:3], lnst.st[:, 0:1], 1.0 / 1024),
            reads=[lnst.b_st], writes=[lnst.b_st])
        R.op("vector", lambda e: e.tensor_mul(
            lnst.st[:, 3:4], lnst.st[:, 2:3], lnst.st[:, 2:3]),
            reads=[lnst.b_st], writes=[lnst.b_st])
        R.op("vector", lambda e: e.scalar_tensor_tensor(
            out=lnst.st[:, 3:4], in0=lnst.st[:, 1:2], scalar=1.0 / 1024,
            in1=lnst.st[:, 3:4], op0=ALU.mult, op1=ALU.subtract),
            reads=[lnst.b_st], writes=[lnst.b_st])
        R.op("scalar", lambda e: e.activation(
            out=lnst.st[:, 4:5], in_=lnst.st[:, 3:4], func=AF.Sqrt, bias=1e-5),
            reads=[lnst.b_st, C.b_cpool], writes=[lnst.b_st])
        R.op("vector", lambda e: e.reciprocal(lnst.st[:, 5:6], lnst.st[:, 4:5]),
             reads=[lnst.b_st], writes=[lnst.b_st])
        xo = lnst.xln[:, 0, :]
        R.op("vector", lambda e, xi=xi: e.tensor_scalar(
            out=st.xf[:, :], in0=xi, scalar1=lnst.st[:, 2:3], scalar2=lnst.st[:, 5:6],
            op0=ALU.subtract, op1=ALU.mult),
            reads=[b_x, lnst.b_st], writes=[st.b_xf])
        bslot = i % 2
        R.op("vector", lambda e, xi=xi, bslot=bslot: e.tensor_add(
            st.base[:, bslot, :], xi, st.xf[:, :]),
            reads=[b_x, st.b_xf], writes=[st.b_base[bslot]])
        R.dma("sync", out_hbm[ti * 128:(ti + 1) * 128, :], st.base[:, bslot, :],
              c_out, reads=[st.b_base[bslot]], writes=[b_out[ti]])
        R.op("scalar", lambda e: e.activation(
            out=lnst.sq[:], in_=st.xf[:, :], func=AF.Square, accum_out=lnst.st[:, 1:2]),
            reads=[st.b_xf], writes=[lnst.b_sq, lnst.b_st])
        R.op("vector", lambda e: e.reduce_sum(
            lnst.st[:, 0:1], st.xf[:, :], axis=AX.X), reads=[st.b_xf], writes=[lnst.b_st])
        R.op("vector", lambda e: e.tensor_scalar_mul(
            lnst.st[:, 2:3], lnst.st[:, 0:1], 1.0 / 1024),
            reads=[lnst.b_st], writes=[lnst.b_st])
        R.op("vector", lambda e: e.tensor_mul(
            lnst.st[:, 3:4], lnst.st[:, 2:3], lnst.st[:, 2:3]),
            reads=[lnst.b_st], writes=[lnst.b_st])
        R.op("vector", lambda e: e.scalar_tensor_tensor(
            out=lnst.st[:, 3:4], in0=lnst.st[:, 1:2], scalar=1.0 / 1024,
            in1=lnst.st[:, 3:4], op0=ALU.mult, op1=ALU.subtract),
            reads=[lnst.b_st], writes=[lnst.b_st])
        R.op("scalar", lambda e: e.activation(
            out=lnst.st[:, 4:5], in_=lnst.st[:, 3:4], func=AF.Sqrt, bias=1e-5),
            reads=[lnst.b_st, C.b_cpool], writes=[lnst.b_st])
        R.op("vector", lambda e: e.reciprocal(lnst.st[:, 5:6], lnst.st[:, 4:5]),
             reads=[lnst.b_st], writes=[lnst.b_st])
        R.op("vector", lambda e: e.tensor_scalar(
            out=st.fhat[:, :], in0=st.xf[:, :], scalar1=lnst.st[:, 2:3],
            scalar2=lnst.st[:, 5:6], op0=ALU.subtract, op1=ALU.mult),
            reads=[st.b_xf, lnst.b_st], writes=[st.b_fhat])
        for c in range(8):
            pt = C.ps_tr[:, c % 2, 0:128]
            R.op("tensor", lambda e, c=c, pt=pt: e.transpose(
                pt, st.fhat[:, c * 128:(c + 1) * 128], C.ident[:]),
                reads=[st.b_fhat, C.b_ident], writes=[C.b_ptr[c % 2]])
            R.op("vector", lambda e, c=c, pt=pt: e.tensor_copy(
                st.fhT[:, c, :], pt),
                reads=[C.b_ptr[c % 2]], writes=[st.b_fhT])
        R.dma("sync", fhat_d[:, :, ti * 128:(ti + 1) * 128], st.fhT[:, :, :],
              st.c_f, reads=[st.b_fhT], writes=[b_fd])
        for c in range(8):
            pt = C.ps_big[:, c % 2, 0:128]
            R.op("tensor", lambda e, c=c, pt=pt: e.transpose(
                pt, st.xf[:, c * 128:(c + 1) * 128], C.idf[:, :]),
                reads=[st.b_xf, C.b_idf], writes=[C.b_big[c % 2]])
            R.op("vector", lambda e, c=c, pt=pt: e.tensor_copy(
                st.ftT[:, c, :], pt), reads=[C.b_big[c % 2]], writes=[st.b_ftT])
        def mm_s(e):
            last = None
            for c in range(8):
                last = e.matmul(C.ps_ab[:, 0, 0:8], st.ftT[:, c, :],
                                st.cent[:, c, :], start=(c == 0), stop=(c == 7))
            return last
        R.op("tensor", mm_s, reads=[st.b_ftT, st.b_cent], writes=[C.b_ab[0]])
        R.op("vector", lambda e: e.tensor_copy(st.sc[:, 0:8], C.ps_ab[:, 0, 0:8]),
             reads=[C.b_ab[0]], writes=[st.b_sc])
        R.op("vector", lambda e: e.reduce_max(
            st.sc[:, 8:9], st.sc[:, 0:8], axis=AX.X), reads=[st.b_sc], writes=[st.b_sc])
        R.op("vector", lambda e: e.tensor_scalar(
            out=st.sc[:, 16:24], in0=st.sc[:, 0:8], scalar1=st.sc[:, 8:9],
            scalar2=None, op0=ALU.is_equal), reads=[st.b_sc], writes=[st.b_sc])
        R.op("vector", lambda e: e.tensor_tensor_scan(
            out=st.sc[:, 24:32], data0=st.sc[:, 16:24], data1=st.sc[:, 32:40],
            initial=0.0, op0=ALU.add, op1=ALU.add),
            reads=[st.b_sc], writes=[st.b_sc])
        R.op("vector", lambda e: e.tensor_scalar(
            out=st.sc[:, 24:32], in0=st.sc[:, 24:32], scalar1=1.0, scalar2=None,
            op0=ALU.is_equal), reads=[st.b_sc], writes=[st.b_sc])
        R.op("vector", lambda e: e.tensor_mul(
            st.sc[:, 16:24], st.sc[:, 16:24], st.sc[:, 24:32]),
            reads=[st.b_sc], writes=[st.b_sc])
        R.op("scalar", lambda e: e.activation(
            out=st.sc[:, 9:10], in_=st.sc[:, 8:9], func=AF.Sigmoid),
            reads=[st.b_sc], writes=[st.b_sc])
        R.op("vector", lambda e, ti=ti: e.tensor_scalar(
            out=aE[:, ti, :], in0=st.sc[:, 16:24], scalar1=st.sc[:, 9:10],
            scalar2=None, op0=ALU.mult), reads=[st.b_sc, b_aE], writes=[b_aE])


def emit_l1ffn_main(R: Rec, C, st, W1B, W2B, b_wbf, out_hbm, b_out, fhat_d, b_fd,
                    aE, b_aE):
    ps_y = [C.ps_big[:, 0, :], C.ps_big[:, 1, :], C.ps_big[:, 2, :], C.ps_big[:, 3, :],
            C.ps_ab[:, 0, :], C.ps_ab[:, 1, :],
            C.ps_tr[:, 0, :].bitcast(F32), C.ps_tr[:, 1, :].bitcast(F32)]
    b_y = [C.b_big[0], C.b_big[1], C.b_big[2], C.b_big[3],
           C.b_ab[0], C.b_ab[1], C.b_ptr[0], C.b_ptr[1]]

    BW1, BW2 = b_wbf

    def load_w1(e, nt):
        slot = nt % 3
        R.dma("sync", st.w1[:, slot, :, :],
              W1B[e][nt * 128:(nt + 1) * 128, :].rearrange("k (c n) -> k c n", c=8),
              st.c_w1[slot], reads=[BW1[e][nt // 8]], writes=[st.b_w1[slot]])

    def load_w2(e, nt, dh):
        slot = nt % 8
        R.dma("scalar", st.w2[:, slot, :],
              W2B[e][nt * 128:(nt + 1) * 128, dh * 512:(dh + 1) * 512],
              st.c_w2[slot], reads=[BW2[e][nt // 8]], writes=[st.b_w2[slot]])

    def emit_fslhe(e, tg):
        R.dma("gpsimd", st.fsl[:, :, :], fhat_d[:, :, tg * 1024:(tg + 1) * 1024],
              st.c_fsl, reads=[b_fd, st.b_fsl], writes=[st.b_fsl])
        for c in range(8):
            R.op("vector", lambda en, c=c, e=e: en.tensor_scalar(
                out=st.he[:, c, :], in0=st.fsl[:, c, :],
                scalar1=st.gb[:, 0, c, e:e + 1], scalar2=st.gb[:, 1, c, e:e + 1],
                op0=ALU.mult, op1=ALU.add),
                reads=[st.b_fsl, st.b_gb], writes=[st.b_he])

    for k in range(3):
        load_w1(0, k)
    for k in range(8):
        load_w2(0, k, 0)
    emit_fslhe(0, 0)
    for e in range(8):
        for tg in range(4):
            for nt in range(32):
                slot = nt % 3
                for tc in range(2):
                    pi = (nt * 2 + tc) % 4
                    ps = ps_y[pi]
                    b_ps = b_y[pi]

                    def mm1(en, slot=slot, tc=tc, ps=ps):
                        last = None
                        for cc in range(8):
                            last = en.matmul(ps, st.w1[:, slot, cc, :],
                                             st.he[:, cc, tc * 512:(tc + 1) * 512],
                                             start=(cc == 0), stop=(cc == 7))
                        return last
                    R.op("tensor", mm1, reads=[st.b_w1[slot], st.b_he], writes=[b_ps])
                    R.op("vector", lambda en, nt=nt, tc=tc, e=e, ps=ps: en.tensor_scalar(
                        out=st.hrT[:, nt, tc * 512:(tc + 1) * 512], in0=ps,
                        scalar1=st.b1[:, nt, e:e + 1], scalar2=0.0,
                        op0=ALU.add, op1=ALU.max),
                        reads=[b_ps, st.b_b1], writes=[st.b_hrT])
                if nt + 3 < 32:
                    load_w1(e, nt + 3)
            for dh in range(2):
                for nt in range(32):
                    slot = nt % 8
                    for tt in range(8):
                        def mm2(en, nt=nt, slot=slot, tt=tt, e=e, dh=dh):
                            ins = en.matmul(ps_y[tt], st.hrT[:, nt, tt * 128:(tt + 1) * 128],
                                            st.w2[:, slot, :],
                                            start=(nt == 0), stop=False)
                            if nt == 31:
                                ins = en.matmul(ps_y[tt],
                                                C.ones1[0:1, 0:128],
                                                st.b2r[0:1, e, dh * 512:(dh + 1) * 512],
                                                start=False, stop=True)
                            return ins
                        R.op("tensor", mm2,
                             reads=[st.b_hrT, st.b_w2[slot], C.b_ones1, st.b_b2r],
                             writes=[b_y[tt]])
                    if nt + 8 < 32:
                        load_w2(e, nt + 8, dh)
                    elif dh == 0 and nt >= 24:
                        load_w2(e, nt - 24, 1)
                ne, ntg = (e, tg + 1) if tg < 3 else (e + 1, 0)
                if dh == 0:
                    if ne < 8:
                        emit_fslhe(ne, ntg)
                if dh == 1:
                    if ne < 8:
                        for k in range(3):
                            load_w1(ne, k)
                        for k in range(8):
                            load_w2(ne, k, 0)
                for tt in range(8):
                    ti = tg * 8 + tt
                    R.op("scalar", lambda en, tt=tt, ti=ti, e=e:
                         en.activation(
                             out=st.stage[:, tt, :],
                             in_=ps_y[tt], func=AF.Copy,
                             scale=aE[:, ti, e:e + 1]),
                         reads=[b_y[tt], b_aE], writes=[st.b_stage[tt]])
                    R.dma("gpsimd",
                          out_hbm[ti * 128:(ti + 1) * 128, dh * 512:(dh + 1) * 512],
                          st.stage[:, tt, :],
                          st.c_acc, accum_op=ALU.add,
                          reads=[st.b_stage[tt]], writes=[b_out[ti]])



def l0attn_state(R: Rec, name):
    st = NS()
    st.q = R.sbuf(name + "_q", [128, 8, 2048], BF16)
    st.k = R.sbuf(name + "_k", [128, 8, 2048], BF16)
    st.v = R.sbuf(name + "_v", [128, 8, 2048], BF16)
    st.gstg = R.sbuf(name + "_gstg", [128, 2, 512], BF16)
    st.b_q, st.b_k = Buf(name + "q"), Buf(name + "k")
    st.b_v, st.b_g = Buf(name + "v"), Buf(name + "gg")
    st.c_g = R.ctr(name + "_cg", 16)
    st.rowsA = R.sbuf(name + "_rowsA", [96, 2048], F32)
    st.rowsB = R.sbuf(name + "_rowsB", [96, 2048], F32)
    st.b_rA, st.b_rB = Buf(name + "rA"), Buf(name + "rB")
    st.cols = R.sbuf(name + "_cols", [128, 2, 80], F32)
    st.b_cols = Buf(name + "cols")
    st.fscr = R.sbuf(name + "_fscr", [128, 1024], F32)
    st.b_fscr = Buf(name + "fscr")
    st.gA = R.sbuf(name + "_gA", [128, 2, 8, 64], F32)
    st.betA = R.sbuf(name + "_betA", [128, 2, 8, 64], F32)
    st.egA = R.sbuf(name + "_egA", [128, 8, 64], BF16)
    st.b_gA = [[Buf(name + f"gA{i}_{h}") for h in range(2)] for i in range(2)]
    st.b_betA = [[Buf(name + f"bA{i}_{h}") for h in range(2)] for i in range(2)]
    st.b_egA = Buf(name + "eA")
    st.c_bc = [R.ctr(name + f"_cbc{i}", 16) for i in range(2)]
    st.LTf = st.fscr[:, 0:512].rearrange("p (a b) -> p a b", b=64)
    st.b_LTf = st.b_fscr
    st.LT = R.sbuf(name + "_LT", [128, 3, 8, 64], BF16)
    st.b_LT = Buf(name + "LT")
    st.mask = R.sbuf(name + "_mask", [128, 3, 64], BF16)
    st.b_mask = Buf(name + "mask")
    st.P = R.sbuf(name + "_P", [128, 2, 8, 64], BF16)
    st.b_P = Buf(name + "P")
    st.pool = R.sbuf(name + "_pool", [128, 8, 576], BF16)
    st.b_pool = Buf(name + "pool")
    st.w = st.pool[:, :, 0:512]
    st.b_w = st.b_pool
    st.U = st.pool[:, :, 0:128]
    st.b_U = st.b_pool
    st.B = st.pool[:, :, 128:256]
    st.b_B = st.b_pool
    st.attnT = st.pool[:, :, 256:320]
    st.b_attnT = st.b_pool
    st.kb = st.pool[:, :, 320:384]
    st.b_kb = st.b_pool
    st.kd = st.pool[:, :, 384:448]
    st.b_kd = st.b_pool
    st.kcdT = st.pool[:, :, 448:512]
    st.b_kcdT = st.b_pool
    st.vn = st.pool[:, :, 512:576]
    st.b_vn = st.b_pool
    st.S32 = R.sbuf(name + "_S32", [128, 8, 64], F32)
    st.b_S32 = Buf(name + "S32")
    st.Sbf = R.sbuf(name + "_Sbf", [128, 8, 64], BF16)
    st.b_Sbf = Buf(name + "Sbf")
    st.c_w = R.ctr(name + "_cw", 16)
    st.q2 = R.sbuf(name + "_q2", [128, 1024], BF16)
    st.b_q2 = Buf(name + "q2")
    st.nrm = None
    st.b_nrm = None
    st.BO = R.sbuf(name + "_BO", [128, 8, 16], BF16)
    st.b_BO = Buf(name + "BO")
    st.EX = R.sbuf(name + "_EX", [16, 8, 128], BF16)
    st.b_EX = Buf(name + "EX")
    st.nrm = st.rowsA[0:16, :].bitcast(BF16)[:, 0:2048]
    st.b_nrm = st.b_rA
    st.smalls = R.sbuf(name + "_smalls", [128, 4], F32)
    st.b_smalls = Buf(name + "smalls")
    st.c_sm = R.ctr(name + "_csm", 16)
    st.ln = ln_state(R, name + "_ln", sq=st.q2[:, :], b_sq=st.b_q2,
                     xln=st.v[:, 0, 0:1024].rearrange("p (a b) -> p a b", a=1),
                     b_xlnb=st.b_v)
    return st


def emit_l0attn_setup(R: Rec, C, st):
    scrf = C.idf[:, 0:32]
    for c in range(8):
        R.op("gpsimd", lambda e, c=c: e.affine_select(
            out=scrf[:, 0:16], in_=C.onesf[:, 0:16], pattern=[[-64, 16]],
            compare_op=ALU.is_ge, fill=0.0, base=128 * c, channel_multiplier=1),
            reads=[C.b_onesf, C.b_idf], writes=[C.b_idf])
        R.op("gpsimd", lambda e, c=c: e.affine_select(
            out=scrf[:, 16:32], in_=scrf[:, 0:16], pattern=[[64, 16]],
            compare_op=ALU.is_ge, fill=0.0, base=63 - 128 * c, channel_multiplier=-1),
            reads=[C.b_idf], writes=[C.b_idf])
        R.op("vector", lambda e, c=c: e.tensor_copy(st.BO[:, c, :], scrf[:, 16:32]),
             reads=[C.b_idf], writes=[st.b_BO])
    s16 = C.idf[0:16, :]
    for c in range(8):
        R.op("gpsimd", lambda e, c=c: e.affine_select(
            out=s16[:, :], in_=C.onesf[0:16, :], pattern=[[1, 128]],
            compare_op=ALU.is_ge, fill=0.0, base=128 * c, channel_multiplier=-64),
            reads=[C.b_onesf, C.b_idf], writes=[C.b_idf])
        R.op("gpsimd", lambda e, c=c: e.affine_select(
            out=s16[:, :], in_=s16[:, :], pattern=[[-1, 128]],
            compare_op=ALU.is_ge, fill=0.0, base=63 - 128 * c, channel_multiplier=64),
            reads=[C.b_idf], writes=[C.b_idf])
        R.op("vector", lambda e, c=c: e.tensor_copy(st.EX[:, c, :], s16[:, :]),
             reads=[C.b_idf], writes=[st.b_EX])
    for mi, (cm_sign, b0) in enumerate([(1, 0), (1, 1), (-1, 1)]):
        for half in range(2):
            out = C.idf[half * 64:(half + 1) * 64, 0:64]
            R.op("gpsimd", lambda e, cm_sign=cm_sign, b0=b0, half=half, out=out:
                 e.affine_select(
                     out=out, in_=C.onesf[half * 64:(half + 1) * 64, 0:64],
                     pattern=[[cm_sign, 64]], compare_op=ALU.is_ge, fill=0.0,
                     base=-b0, channel_multiplier=-cm_sign),
                 reads=[C.b_onesf, C.b_idf], writes=[C.b_idf])
        R.op("vector", lambda e, mi=mi: e.tensor_copy(st.mask[:, mi, :], C.idf[:, 0:64]),
             reads=[C.b_idf], writes=[st.b_mask])


def emit_l0attn(R: Rec, C, st, x_ap_fn, b_xsrc, xadd_fn, o_fm, b_o, xlnt, b_xlnt,
                Wq, Wk, Wv, Wg, Wo, Wb, Wa, dt_bias, A_log, norm_w,
                g_dram, beta_dram, b_gd, gate_d, b_gt, dump=None, n_chunks=32,
                hook=None):
    nc = R.nc
    gR = st.rowsA[0:16, :]
    betaR = st.rowsA[32:48, :]
    eglgR = st.rowsA[64:80, :]
    sbgR = st.rowsB[0:16, :]
    eglR = st.rowsB[32:48, :]
    arawR = st.rowsB[64:80, :]
    b_rA, b_rB = st.b_rA, st.b_rB

    R.op("gpsimd", lambda e: e.affine_select(
        out=C.idf[:, :], in_=C.onesf[:, :], pattern=[[-1, 128]],
        compare_op=ALU.is_equal, fill=0.0, base=0, channel_multiplier=1),
        reads=[C.b_onesf, C.b_idf], writes=[C.b_idf])

    R.dma("sync", st.smalls[0:16, 0:1], dt_bias[:, None], st.c_sm,
          reads=[st.b_smalls], writes=[st.b_smalls], allow_slow_non_contiguous=True)
    R.dma("sync", st.smalls[0:16, 1:2], A_log[:, None], st.c_sm,
          reads=[st.b_smalls], writes=[st.b_smalls], allow_slow_non_contiguous=True)
    R.dma("sync", st.smalls[0:64, 2:3], norm_w[:, None], st.c_sm,
          reads=[st.b_smalls], writes=[st.b_smalls], allow_slow_non_contiguous=True)
    R.dma("sync", st.smalls[64:128, 2:3], norm_w[:, None], st.c_sm,
          reads=[st.b_smalls], writes=[st.b_smalls], allow_slow_non_contiguous=True)
    R.op("scalar", lambda e: e.activation(
        out=st.smalls[0:16, 3:4], in_=st.smalls[0:16, 1:2], func=AF.Exp),
        reads=[st.b_smalls], writes=[st.b_smalls])
    R.op("vector", lambda e: e.tensor_scalar_mul(
        st.smalls[0:16, 3:4], st.smalls[0:16, 3:4], -1.0),
        reads=[st.b_smalls], writes=[st.b_smalls])

    emit_ln_xlnt(R, C, st.ln, x_ap_fn, b_xsrc, xlnt, b_xlnt, 16)

    def loadW(W, col0, ncol=512):
        R.dma("gpsimd", st.w[:, :, 0:ncol],
              W[:, col0:col0 + ncol].rearrange("(c k) n -> k c n", k=128),
              st.c_w, reads=[st.b_w], writes=[st.b_w])

    for (W, dst, b_dst) in [(Wq, st.q, st.b_q), (Wk, st.k, st.b_k),
                            (Wv, st.v, st.b_v), (Wg, None, st.b_g)]:
        for hg in range(2):
            loadW(W, hg * 512)
            for tc in range(4):
                for c in range(4):
                    cg = hg * 4 + c
                    ps = C.ps_ab[:, c % 2, :]
                    b_ps = C.b_ab[c % 2]

                    def mm_p(e, c=c, tc=tc, ps=ps):
                        last = None
                        for cc in range(8):
                            last = e.matmul(ps, st.w[:, cc, c * 128:(c + 1) * 128],
                                            xlnt[:, cc, tc * 512:(tc + 1) * 512],
                                            start=(cc == 0), stop=(cc == 7))
                        return last
                    R.op("tensor", mm_p, reads=[st.b_w, b_xlnt], writes=[b_ps])
                    if dst is None:
                        gs = (tc + cg) % 2
                        R.op("vector", lambda e, gs=gs, ps=ps: e.tensor_copy(
                            st.gstg[:, gs, :], ps),
                            reads=[b_ps], writes=[st.b_g])
                        R.dma("scalar", gate_d[:, cg, tc * 512:(tc + 1) * 512],
                              st.gstg[:, gs, :], st.c_g,
                              reads=[st.b_g], writes=[b_gt])
                    else:
                        R.op("vector", lambda e, tc=tc, cg=cg, dst=dst, ps=ps:
                             e.tensor_copy(dst[:, cg, tc * 512:(tc + 1) * 512], ps),
                             reads=[b_ps], writes=[b_dst])

    for (W, outrow, b_or) in [(Wb, st.rowsB[64:80, :], b_rB),
                              (Wa, st.rowsA[64:80, :], b_rA)]:
        R.dma("gpsimd", st.w[:, :, 0:16],
              W.rearrange("(c k) n -> k c n", k=128), st.c_w,
              reads=[st.b_w], writes=[st.b_w])
        for tc in range(4):
            ps = C.ps_ab[0:16, 0, :]

            def mm_ba(e, tc=tc, ps=ps):
                last = None
                for cc in range(8):
                    last = e.matmul(ps, st.w[:, cc, 0:16],
                                    xlnt[:, cc, tc * 512:(tc + 1) * 512],
                                    start=(cc == 0), stop=(cc == 7))
                return last
            R.op("tensor", mm_ba, reads=[st.b_w, b_xlnt], writes=[C.b_ab[0]])
            R.op("vector", lambda e, tc=tc, outrow=outrow, ps=ps: e.tensor_copy(
                outrow[:, tc * 512:(tc + 1) * 512], ps),
                reads=[C.b_ab[0]], writes=[b_or])

    for iqk, (dst, b_dst) in enumerate([(st.q, st.b_q), (st.k, st.b_k)]):
        nfi = st.nrm[:, :]
        for tc in range(4):
            for cg in range(8):
                R.op("scalar", lambda e, tc=tc, cg=cg, dst=dst: e.activation(
                    out=st.q2[:, 0:512], in_=dst[:, cg, tc * 512:(tc + 1) * 512],
                    func=AF.Square), reads=[b_dst], writes=[st.b_q2])
                R.op("tensor", lambda e, cg=cg, tc=tc: e.matmul(
                    C.ps_big[0:16, tc, :], st.BO[:, cg, :], st.q2[:, 0:512],
                    start=(cg == 0), stop=(cg == 7)),
                    reads=[st.b_BO, st.b_q2], writes=[C.b_big[tc]])
            R.op("vector", lambda e, tc=tc: e.tensor_copy(
                st.nrm[:, tc * 512:(tc + 1) * 512], C.ps_big[0:16, tc, :]),
                reads=[C.b_big[tc]], writes=[st.b_nrm])
        R.op("vector", lambda e, nfi=nfi: e.tensor_scalar_add(nfi, nfi, 1e-6),
             reads=[st.b_nrm], writes=[st.b_nrm])
        R.op("scalar", lambda e, nfi=nfi: e.activation(out=nfi, in_=nfi, func=AF.Ln),
             reads=[st.b_nrm, C.b_cpool], writes=[st.b_nrm])
        R.op("scalar", lambda e, nfi=nfi: e.activation(
            out=nfi, in_=nfi, func=AF.Exp, scale=C.const(-0.5)),
            reads=[st.b_nrm, C.b_cpool], writes=[st.b_nrm])
        if iqk == 0:
            R.op("vector", lambda e, nfi=nfi: e.tensor_scalar_mul(nfi, nfi, 0.125),
                 reads=[st.b_nrm], writes=[st.b_nrm])
        for cg in range(8):
            for tc in range(4):
                ps = C.ps_ab[:, iqk, :]
                R.op("tensor", lambda e, cg=cg, tc=tc, ps=ps: e.matmul(
                    ps, st.EX[:, cg, :], st.nrm[:, tc * 512:(tc + 1) * 512],
                    start=True, stop=True),
                    reads=[st.b_EX, st.b_nrm], writes=[C.b_ab[iqk]])
                R.op("vector", lambda e, cg=cg, tc=tc, dst=dst, ps=ps: e.tensor_tensor(
                    out=dst[:, cg, tc * 512:(tc + 1) * 512],
                    in0=dst[:, cg, tc * 512:(tc + 1) * 512], in1=ps, op=ALU.mult),
                    reads=[C.b_ab[iqk], b_dst], writes=[b_dst])

    R.op("scalar", lambda e: e.activation(out=betaR, in_=st.rowsB[64:80, :],
                                          func=AF.Sigmoid),
         reads=[b_rB], writes=[b_rA])
    R.op("vector", lambda e: e.tensor_scalar(
        out=st.rowsA[64:80, :], in0=st.rowsA[64:80, :],
        scalar1=st.smalls[0:16, 0:1], scalar2=None, op0=ALU.add),
        reads=[b_rA, st.b_smalls], writes=[b_rA])
    R.op("scalar", lambda e: e.activation(out=arawR, in_=st.rowsA[64:80, :], func=AF.Abs),
         reads=[b_rA], writes=[b_rB])
    R.op("scalar", lambda e: e.activation(out=arawR, in_=arawR, func=AF.Exp, scale=-1.0),
         reads=[b_rB], writes=[b_rB])
    R.op("scalar", lambda e: e.activation(out=arawR, in_=arawR, func=AF.Ln, bias=1.0),
         reads=[b_rB], writes=[b_rB])
    R.op("vector", lambda e: e.tensor_scalar_max(
        st.rowsA[64:80, :], st.rowsA[64:80, :], 0.0),
        reads=[b_rA], writes=[b_rA])
    R.op("vector", lambda e: e.tensor_add(arawR, arawR, st.rowsA[64:80, :]),
         reads=[b_rA, b_rB], writes=[b_rB])
    R.op("vector", lambda e: e.tensor_scalar(
        out=arawR, in0=arawR, scalar1=st.smalls[0:16, 3:4], scalar2=None,
        op0=ALU.mult), reads=[b_rB, st.b_smalls], writes=[b_rB])
    R.op("vector", lambda e: e.tensor_tensor_scan(
        out=eglgR, data0=arawR, data1=arawR, initial=0.0,
        op0=ALU.add, op1=ALU.bypass),
        reads=[b_rB], writes=[b_rA])
    R.op("vector", lambda e: e.tensor_copy(gR, eglgR), reads=[b_rA], writes=[b_rA])
    R.op("vector", lambda e: e.memset(sbgR[:, 0:1], 0.0),
         reads=[b_rB], writes=[b_rB])
    R.op("vector", lambda e: e.tensor_copy(sbgR[:, 1:32], eglgR[:, 63:2047:64]),
         reads=[b_rA, b_rB], writes=[b_rB])
    R.op("vector", lambda e: e.tensor_tensor(
        out=gR.rearrange("p (a b) -> p a b", b=64),
        in0=gR.rearrange("p (a b) -> p a b", b=64),
        in1=sbgR[:, 0:32].unsqueeze(2).broadcast_to([16, 32, 64]),
        op=ALU.subtract), reads=[b_rA, b_rB], writes=[b_rA])
    R.op("vector", lambda e: e.tensor_tensor(
        out=eglgR.rearrange("p (a b) -> p a b", b=64),
        in0=gR[:, 63:2048:64].unsqueeze(2).broadcast_to([16, 32, 64]),
        in1=gR.rearrange("p (a b) -> p a b", b=64), op=ALU.subtract),
        reads=[b_rA], writes=[b_rA])
    R.op("scalar", lambda e: e.activation(out=eglgR, in_=eglgR, func=AF.Exp),
         reads=[b_rA], writes=[b_rA])
    R.op("scalar", lambda e: e.activation(out=eglR, in_=gR, func=AF.Exp),
         reads=[b_rA], writes=[b_rB])
    R.op("vector", lambda e: e.tensor_mul(sbgR, eglR, betaR),
         reads=[b_rA, b_rB], writes=[b_rB])
    R.op("vector", lambda e: e.tensor_copy(
        eglR.rearrange("p (a b) -> p a b", b=64),
        gR[:, 63:2048:64].unsqueeze(2).broadcast_to([16, 32, 64])),
        reads=[b_rA], writes=[b_rB])
    R.op("scalar", lambda e: e.activation(out=eglR, in_=eglR, func=AF.Exp),
         reads=[b_rB], writes=[b_rB])
    R.dma("sync", g_dram[:, :], gR, st.c_sm, reads=[b_rA], writes=[b_gd])
    R.dma("sync", beta_dram[:, :], betaR, st.c_sm, reads=[b_rA], writes=[b_gd])

    if dump is not None:
        dl = [("d_q", st.q[:, :, :], st.b_q), ("d_k", st.k[:, :, :], st.b_k),
              ("d_v", st.v[:, :, :], st.b_v)]
        for r0 in (0, 32, 64):
            dl.append((("d_rA", r0), st.rowsA[r0:r0 + 16, :], st.b_rA))
            dl.append((("d_rB", r0), st.rowsB[r0:r0 + 16, :], st.b_rB))
        for key, ap, b in dl:
            if isinstance(key, tuple):
                if key[0] in dump:
                    R.dma("sync", dump[key[0]][key[1]:key[1] + 16, :], ap, dump["c"],
                          reads=[b], writes=[dump["b"]])
            elif key in dump:
                R.dma("sync", dump[key], ap, dump["c"], reads=[b], writes=[dump["b"]])
    R.op("vector", lambda e: e.memset(st.S32[:, :, :], 0.0),
         reads=[st.b_S32], writes=[st.b_S32])
    R.op("vector", lambda e: e.memset(st.Sbf[:, :, :], 0.0),
         reads=[st.b_Sbf], writes=[st.b_Sbf])

    varrows = [(st.rowsA, 0, b_rA), (st.rowsB, 0, b_rB), (st.rowsA, 32, b_rA),
               (st.rowsB, 32, b_rB), (st.rowsA, 64, b_rA)]
    VAR_G, VAR_SBG, VAR_BETA, VAR_EGL, VAR_EGLG = 0, 1, 2, 3, 4

    def colsel(var, par2, cs):
        return st.cols[par2 * 64:(par2 + 1) * 64, cs, var * 16 + par2:var * 16 + 16:2]

    def emit_bc(ci):
        cb = ci * 64
        s = ci % 2
        for half in range(2):
            R.dma("scalar", st.gA[half * 64:(half + 1) * 64, s, :, :],
                  g_dram[half::2, cb:cb + 64].unsqueeze(0)
                  .partition_broadcast(64).squeeze(1),
                  st.c_bc[s], reads=[b_gd, st.b_gA[s][half]],
                  writes=[st.b_gA[s][half]])
            R.dma("scalar", st.betA[half * 64:(half + 1) * 64, s, :, :],
                  beta_dram[half::2, cb:cb + 64].unsqueeze(0)
                  .partition_broadcast(64).squeeze(1),
                  st.c_bc[s], reads=[b_gd, st.b_betA[s][half]],
                  writes=[st.b_betA[s][half]])
    st.emit_bc = emit_bc

    def phase_A(ci):
        cb = ci * 64
        cs = ci % 2
        lvl = globals().get("L0A_LVL", 99)
        if ci + 1 < n_chunks:
            emit_bc(ci + 1)
        R.op("scalar", lambda e: e.activation(
            out=st.egA[:, :, :], in_=st.gA[:, cs, :, :], func=AF.Exp),
            reads=[st.b_gA[cs][0], st.b_gA[cs][1]], writes=[st.b_egA])
        if lvl < 2: return
        vbank = {0: 1, 32: 2, 64: 3}
        vcol = [0, 16, 0, 16, 0]
        for vi, (rt, r0, b_rt) in enumerate(varrows):
            R.op("tensor", lambda e, vi=vi, rt=rt, r0=r0, cb=cb:
                 e.transpose(
                     C.ps_big[0:64, vbank[r0], vcol[vi]:vcol[vi] + 16],
                     rt[r0:r0 + 16, cb:cb + 64],
                     C.idf[r0:r0 + 16, r0:r0 + 16],
                     tile_position=(r0, 0)),
                 reads=[b_rt, C.b_idf], writes=[C.b_big[vbank[r0]]])
        for half in range(2):
            hs = slice(half * 64, (half + 1) * 64)
            R.op("vector", lambda e, cs=cs, hs=hs: e.tensor_copy(
                st.cols[hs, cs, 0:32], C.ps_big[0:64, 1, 0:32]),
                reads=[C.b_big[1]], writes=[st.b_cols])
            R.op("vector", lambda e, cs=cs, hs=hs: e.tensor_copy(
                st.cols[hs, cs, 32:64], C.ps_big[0:64, 2, 0:32]),
                reads=[C.b_big[2]], writes=[st.b_cols])
            R.op("vector", lambda e, cs=cs, hs=hs: e.tensor_copy(
                st.cols[hs, cs, 64:80], C.ps_big[0:64, 3, 0:16]),
                reads=[C.b_big[3]], writes=[st.b_cols])
        R.op("vector", lambda e, cb=cb, cs=cs: e.tensor_tensor(
            out=st.kb[:, :, :], in0=st.k[:, :, cb:cb + 64],
            in1=st.betA[:, cs, :, :],
            op=ALU.mult), reads=[st.b_k, st.b_betA[cs][0], st.b_betA[cs][1]],
            writes=[st.b_kb])
        if lvl < 3: return
        for par2 in range(2):
            sel = colsel(0, par2, cs)
            R.op("vector", lambda e, par2=par2, sel=sel, cs=cs: e.tensor_tensor(
                out=st.LTf[par2 * 64:(par2 + 1) * 64, :, :],
                in0=st.gA[par2 * 64:(par2 + 1) * 64, cs, :, :],
                in1=sel.unsqueeze(2).broadcast_to([64, 8, 64]),
                op=ALU.subtract),
                reads=[st.b_gA[cs][par2], st.b_cols], writes=[st.b_LTf])
        R.op("vector", lambda e, cs=cs: e.tensor_scalar_min(
            st.betA[:, cs, :, :], st.LTf[:, :, :], 0.0),
            reads=[st.b_LTf, st.b_kb],
            writes=[st.b_betA[cs][0], st.b_betA[cs][1]])
        R.op("scalar", lambda e, cs=cs: e.activation(
            out=st.LT[:, 0, :, :], in_=st.betA[:, cs, :, :], func=AF.Exp),
            reads=[st.b_betA[cs][0], st.b_betA[cs][1]], writes=[st.b_LT])
        R.op("vector", lambda e: e.tensor_tensor(
            out=st.LT[:, 1, :, :], in0=st.LT[:, 0, :, :],
            in1=st.mask[:, 1, :].unsqueeze(1).broadcast_to([128, 8, 64]),
            op=ALU.mult), reads=[st.b_LT, st.b_mask], writes=[st.b_LT])
        R.op("vector", lambda e: e.tensor_tensor(
            out=st.LT[:, 0, :, :], in0=st.LT[:, 0, :, :],
            in1=st.mask[:, 0, :].unsqueeze(1).broadcast_to([128, 8, 64]),
            op=ALU.mult), reads=[st.b_LT, st.b_mask], writes=[st.b_LT])
        R.op("vector", lambda e, cs=cs: e.tensor_scalar_max(
            st.betA[:, cs, :, :], st.LTf[:, :, :], 0.0),
            reads=[st.b_LTf],
            writes=[st.b_betA[cs][0], st.b_betA[cs][1]])
        R.op("scalar", lambda e, cs=cs: e.activation(
            out=st.LT[:, 2, :, :], in_=st.betA[:, cs, :, :], func=AF.Exp, scale=-1.0),
            reads=[st.b_betA[cs][0], st.b_betA[cs][1]], writes=[st.b_LT])
        R.op("vector", lambda e: e.tensor_tensor(
            out=st.LT[:, 2, :, :], in0=st.LT[:, 2, :, :],
            in1=st.mask[:, 2, :].unsqueeze(1).broadcast_to([128, 8, 64]),
            op=ALU.mult), reads=[st.b_LT, st.b_mask], writes=[st.b_LT])
        if lvl < 4: return
        for kind, (lhs, b_lhs, rhs, b_rhs, bank) in enumerate([
                (st.kb, st.b_kb, None, st.b_k, 0),
                (None, st.b_k, st.kb, st.b_kb, 1),
                (None, st.b_k, None, st.b_q, 2)]):
            def mm_3(e, kind=kind, cb=cb, bank=bank):
                last = None
                for sp in range(8):
                    for par in range(2):
                        sl = slice(par * 64, (par + 1) * 64)
                        if kind == 0:
                            lt = st.kb[sl, sp, :]
                            rt = st.k[sl, sp, cb:cb + 64]
                        elif kind == 1:
                            lt = st.k[sl, sp, cb:cb + 64]
                            rt = st.kb[sl, sp, :]
                        else:
                            lt = st.k[sl, sp, cb:cb + 64]
                            rt = st.q[sl, sp, cb:cb + 64]
                        last = e.matmul(
                            C.ps_big[sl, bank, sp * 64:(sp + 1) * 64], lt, rt,
                            start=True, stop=True, tile_position=(par * 64, par * 64))
                return last
            R.op("tensor", mm_3, reads=[st.b_kb, st.b_k, st.b_q],
                 writes=[C.b_big[bank]])
        R.op("vector", lambda e, cb=cb: e.tensor_tensor(
            out=st.q[:, :, cb:cb + 64], in0=st.q[:, :, cb:cb + 64],
            in1=st.egA[:, :, :], op=ALU.mult),
            reads=[st.b_q, st.b_egA], writes=[st.b_q])
        R.op("vector", lambda e: e.tensor_tensor(
            out=st.P[:, 0, :, :],
            in0=C.ps_big[:, 0, :].rearrange("p (a b) -> p a b", b=64),
            in1=st.LT[:, 2, :, :], op=ALU.mult),
            reads=[C.b_big[0], st.b_LT], writes=[st.b_P])
        R.op("vector", lambda e: e.tensor_tensor(
            out=st.P[:, 1, :, :],
            in0=C.ps_big[:, 1, :].rearrange("p (a b) -> p a b", b=64),
            in1=st.LT[:, 1, :, :], op=ALU.mult),
            reads=[C.b_big[1], st.b_LT], writes=[st.b_P])
        R.op("vector", lambda e: e.tensor_tensor(
            out=st.attnT[:, :, :],
            in0=C.ps_big[:, 2, :].rearrange("p (a b) -> p a b", b=64),
            in1=st.LT[:, 0, :, :], op=ALU.mult),
            reads=[C.b_big[2], st.b_LT], writes=[st.b_attnT])
        if lvl < 5: return
        def mm_ktr(e, cb=cb):
            last = None
            for sp in range(8):
                for par in range(2):
                    sl = slice(par * 64, (par + 1) * 64)
                    last = e.transpose(
                        C.ps_tr[0:64, par, sp * 64:(sp + 1) * 64],
                        st.k[sl, sp, cb:cb + 64],
                        C.ident[sl, par * 64:(par + 1) * 64],
                        tile_position=(par * 64, 0))
            return last
        R.op("tensor", mm_ktr, reads=[st.b_k, C.b_ident],
             writes=[C.b_ptr[0], C.b_ptr[1]])
        for par2 in range(2):
            sl = slice(par2 * 64, (par2 + 1) * 64)
            src = C.ps_tr[0:64, par2, 0:512]
            if lvl == 5:
                R.op("vector", lambda e, sl=sl, src=src: e.tensor_copy(
                    st.kd[sl, :, :], src.rearrange("p (a b) -> p a b", b=64)),
                    reads=[C.b_ptr[0]], writes=[st.b_kd])
                continue
            R.op("vector", lambda e, par2=par2, sl=sl, cs=cs, src=src: e.tensor_tensor(
                out=st.kd[sl, :, :],
                in0=src.rearrange("p (a b) -> p a b", b=64),
                in1=colsel(4, par2, cs).unsqueeze(2).broadcast_to([64, 8, 64]),
                op=ALU.mult), reads=[C.b_ptr[par2], st.b_cols], writes=[st.b_kd])
            R.op("vector", lambda e, par2=par2, sl=sl, cs=cs, src=src: e.tensor_tensor(
                out=st.B[sl, :, 64:128],
                in0=src.rearrange("p (a b) -> p a b", b=64),
                in1=colsel(1, par2, cs).unsqueeze(2).broadcast_to([64, 8, 64]),
                op=ALU.mult), reads=[C.b_ptr[par2], st.b_cols], writes=[st.b_B])
        if lvl < 5.7: return
        def mm_vtr(e, cb=cb):
            last = None
            for sp in range(8):
                for par in range(2):
                    sl = slice(par * 64, (par + 1) * 64)
                    last = e.transpose(
                        C.ps_tr[0:64, par, 512 + sp * 64:512 + (sp + 1) * 64],
                        st.v[sl, sp, cb:cb + 64],
                        C.ident[sl, par * 64:(par + 1) * 64],
                        tile_position=(par * 64, 0))
            return last
        R.op("tensor", mm_vtr, reads=[st.b_v, C.b_ident],
             writes=[C.b_ptr[0], C.b_ptr[1]])
        for par2 in range(2):
            sl = slice(par2 * 64, (par2 + 1) * 64)
            src = C.ps_tr[0:64, par2, 512:1024]
            R.op("vector", lambda e, par2=par2, sl=sl, cs=cs, src=src: e.tensor_tensor(
                out=st.B[sl, :, 0:64],
                in0=src.rearrange("p (a b) -> p a b", b=64),
                in1=colsel(2, par2, cs).unsqueeze(2).broadcast_to([64, 8, 64]),
                op=ALU.mult), reads=[C.b_ptr[par2], st.b_cols], writes=[st.b_B])
        if lvl < 7: return
        Ups = C.ps_big[:, 0:2, :].rearrange("p a b -> p (a b)")

        def emit_Umm(rhs_t, b_rhs_t):
            def mm(e, rhs_t=rhs_t):
                last = None
                for sp in range(8):
                    for par in range(2):
                        sl = slice(par * 64, (par + 1) * 64)
                        last = e.matmul(
                            Ups[sl, sp * 128:(sp + 1) * 128],
                            st.P[sl, 1, sp, :], rhs_t[sl, sp, :],
                            start=True, stop=True,
                            tile_position=(par * 64, par * 64))
                return last
            R.op("tensor", mm, reads=[st.b_P, b_rhs_t],
                 writes=[C.b_big[0], C.b_big[1]])
            R.op("vector", lambda e, rhs_t=rhs_t: e.tensor_tensor(
                out=st.U[:, :, :],
                in0=Ups.rearrange("p (a b) -> p a b", b=128),
                in1=rhs_t, op=ALU.add),
                reads=[C.b_big[0], C.b_big[1], b_rhs_t], writes=[st.b_U])
        emit_Umm(st.B, st.b_B)
        for it in range(5):
            def mm_sq(e):
                last = None
                for sp in range(8):
                    for par in range(2):
                        sl = slice(par * 64, (par + 1) * 64)
                        e.matmul(C.ps_big[sl, 2, sp * 64:(sp + 1) * 64],
                                 st.P[sl, 1, sp, :], st.P[sl, 0, sp, :],
                                 start=True, stop=True,
                                 tile_position=(par * 64, par * 64))
                        last = e.matmul(C.ps_big[sl, 3, sp * 64:(sp + 1) * 64],
                                        st.P[sl, 0, sp, :], st.P[sl, 1, sp, :],
                                        start=True, stop=True,
                                        tile_position=(par * 64, par * 64))
                return last
            R.op("tensor", mm_sq, reads=[st.b_P], writes=[C.b_big[2], C.b_big[3]])
            R.op("vector", lambda e: e.tensor_copy(
                st.P[:, 0, :, :],
                C.ps_big[:, 2, :].rearrange("p (a b) -> p a b", b=64)),
                reads=[C.b_big[2]], writes=[st.b_P])
            R.op("vector", lambda e: e.tensor_copy(
                st.P[:, 1, :, :],
                C.ps_big[:, 3, :].rearrange("p (a b) -> p a b", b=64)),
                reads=[C.b_big[3]], writes=[st.b_P])
            emit_Umm(st.U, st.b_U)
        if lvl < 8: return
        def mm_kcdtr(e):
            last = None
            for sp in range(8):
                for par in range(2):
                    sl = slice(par * 64, (par + 1) * 64)
                    last = e.transpose(
                        C.ps_tr[0:64, par, sp * 64:(sp + 1) * 64],
                        st.U[sl, sp, 64:128],
                        C.ident[sl, par * 64:(par + 1) * 64],
                        tile_position=(par * 64, 0))
            return last
        R.op("tensor", mm_kcdtr, reads=[st.b_U, C.b_ident],
             writes=[C.b_ptr[0], C.b_ptr[1]])
        for par2 in range(2):
            sl = slice(par2 * 64, (par2 + 1) * 64)
            R.op("vector", lambda e, par2=par2, sl=sl: e.tensor_copy(
                st.kcdT[sl, :, :],
                C.ps_tr[0:64, par2, 0:512].rearrange(
                    "p (a b) -> p a b", b=64)),
                reads=[C.b_ptr[par2]], writes=[st.b_kcdT])

    def phase_B(ci):
        cb = ci * 64
        cs = ci % 2
        if globals().get("L0A_LVL", 99) < 9: return
        def mm_1(e):
            last = None
            for sp in range(8):
                for par in range(2):
                    sl = slice(par * 64, (par + 1) * 64)
                    last = e.matmul(C.ps_ab[sl, 0, sp * 64:(sp + 1) * 64],
                                    st.kcdT[sl, sp, :], st.Sbf[sl, sp, :],
                                    start=True, stop=True,
                                    tile_position=(par * 64, par * 64))
            return last
        R.op("tensor", mm_1, reads=[st.b_kcdT, st.b_Sbf], writes=[C.b_ab[0]])
        R.op("vector", lambda e: e.tensor_tensor(
            out=st.vn[:, :, :], in0=st.U[:, :, 0:64],
            in1=C.ps_ab[:, 0, :].rearrange("p (a b) -> p a b", b=64),
            op=ALU.subtract), reads=[st.b_U, C.b_ab[0]], writes=[st.b_vn])
        def mm_o(e):
            last = None
            for sp in range(8):
                for par in range(2):
                    sl = slice(par * 64, (par + 1) * 64)
                    e.matmul(C.ps_ab[sl, 1, sp * 64:(sp + 1) * 64],
                             st.Sbf[sl, sp, :], st.q[sl, sp, cb:cb + 64],
                             start=True, stop=False,
                             tile_position=(par * 64, par * 64))
                    last = e.matmul(C.ps_ab[sl, 1, sp * 64:(sp + 1) * 64],
                                    st.vn[sl, sp, :], st.attnT[sl, sp, :],
                                    start=False, stop=True,
                                    tile_position=(par * 64, par * 64))
            return last
        R.op("tensor", mm_o, reads=[st.b_Sbf, st.b_q, st.b_vn, st.b_attnT],
             writes=[C.b_ab[1]])
        R.op("vector", lambda e, cb=cb: e.tensor_copy(
            o_fm[:, :, cb:cb + 64],
            C.ps_ab[:, 1, :].rearrange("p (a b) -> p a b", b=64)),
            reads=[C.b_ab[1]], writes=[b_o])
        def mm_S(e):
            last = None
            for sp in range(8):
                for par in range(2):
                    sl = slice(par * 64, (par + 1) * 64)
                    last = e.matmul(C.ps_ab[sl, 0, sp * 64:(sp + 1) * 64],
                                    st.kd[sl, sp, :], st.vn[sl, sp, :],
                                    start=True, stop=True,
                                    tile_position=(par * 64, par * 64))
            return last
        R.op("tensor", mm_S, reads=[st.b_kd, st.b_vn], writes=[C.b_ab[0]])
        for par2 in range(2):
            sl = slice(par2 * 64, (par2 + 1) * 64)
            R.op("vector", lambda e, par2=par2, sl=sl, cs=cs: e.tensor_tensor(
                out=st.S32[sl, :, :], in0=st.S32[sl, :, :],
                in1=colsel(3, par2, cs).unsqueeze(2).broadcast_to([64, 8, 64]),
                op=ALU.mult), reads=[st.b_S32, st.b_cols], writes=[st.b_S32])
        R.op("vector", lambda e: e.tensor_tensor(
            out=st.S32[:, :, :], in0=st.S32[:, :, :],
            in1=C.ps_ab[:, 0, :].rearrange("p (a b) -> p a b", b=64),
            op=ALU.add), reads=[st.b_S32, C.b_ab[0]], writes=[st.b_S32])
        R.op("vector", lambda e: e.tensor_copy(st.Sbf[:, :, :], st.S32[:, :, :]),
             reads=[st.b_S32], writes=[st.b_Sbf])

    emit_bc(0)
    for ci in range(n_chunks):
        if hook is not None:
            hook()
        phase_A(ci)
        if dump is not None and ci == 0:
            for key, ap, b in [("d_LT", st.LT[:, :, :, :], st.b_LT),
                               ("d_P", st.P[:, :, :, :], st.b_P),
                               ("d_B", st.B[:, :, :], st.b_pool),
                               ("d_U", st.U[:, :, :], st.b_pool),
                               ("d_attnT", st.attnT[:, :, :], st.b_pool),
                               ("d_kcdT", st.kcdT[:, :, :], st.b_pool),
                               ("d_kd", st.kd[:, :, :], st.b_pool)]:
                if key in dump:
                    R.dma("sync", dump[key], ap, dump["c"], reads=[b], writes=[dump["b"]])
        phase_B(ci)
        if dump is not None and ci == 0:
            for key, ap, b in [("d_S", st.S32[:, :, :], st.b_S32),
                               ("d_vn", st.vn[:, :, :], st.b_pool)]:
                if key in dump:
                    R.dma("sync", dump[key], ap, dump["c"], reads=[b], writes=[dump["b"]])
    if dump is not None and "d_o" in dump:
        R.dma("sync", dump["d_o"], o_fm[:, :, :], dump["c"], reads=[b_o], writes=[dump["b"]])

    C.const(1e-5)
    for tc in range(4):
        for cg in range(8):
            R.op("scalar", lambda e, tc=tc, cg=cg: e.activation(
                out=st.q2[:, 0:512], in_=o_fm[:, cg, tc * 512:(tc + 1) * 512],
                func=AF.Square), reads=[b_o], writes=[st.b_q2])
            R.op("tensor", lambda e, cg=cg, tc=tc: e.matmul(
                C.ps_big[0:16, tc, :], st.BO[:, cg, :], st.q2[:, 0:512],
                start=(cg == 0), stop=(cg == 7)),
                reads=[st.b_BO, st.b_q2], writes=[C.b_big[tc]])
        R.op("vector", lambda e, tc=tc: e.tensor_copy(
            st.nrm[:, tc * 512:(tc + 1) * 512], C.ps_big[0:16, tc, :]),
            reads=[C.b_big[tc]], writes=[st.b_nrm])
    R.op("vector", lambda e: e.tensor_scalar(
        out=st.nrm[:, :], in0=st.nrm[:, :], scalar1=1.0 / 64, scalar2=1e-5,
        op0=ALU.mult, op1=ALU.add), reads=[st.b_nrm], writes=[st.b_nrm])
    R.op("scalar", lambda e: e.activation(out=st.nrm[:, :], in_=st.nrm[:, :], func=AF.Ln),
         reads=[st.b_nrm, C.b_cpool], writes=[st.b_nrm])
    R.op("scalar", lambda e: e.activation(
        out=st.nrm[:, :], in_=st.nrm[:, :], func=AF.Exp, scale=C.const(-0.5)),
        reads=[st.b_nrm, C.b_cpool], writes=[st.b_nrm])
    for cg in range(8):
        for tc in range(4):
            ps = C.ps_ab[:, tc % 2, :]
            R.op("tensor", lambda e, cg=cg, tc=tc, ps=ps: e.matmul(
                ps, st.EX[:, cg, :], st.nrm[:, tc * 512:(tc + 1) * 512],
                start=True, stop=True),
                reads=[st.b_EX, st.b_nrm], writes=[C.b_ab[tc % 2]])
            gs = (tc + cg) % 2
            R.dma("scalar", st.gstg[:, gs, :],
                  gate_d[:, cg, tc * 512:(tc + 1) * 512], st.c_g,
                  reads=[b_gt, st.b_g], writes=[st.b_g])
            R.op("scalar", lambda e, gs=gs: e.activation(
                out=st.q2[:, 0:512], in_=st.gstg[:, gs, :],
                func=AF.Silu), reads=[st.b_g], writes=[st.b_q2])
            R.op("vector", lambda e, cg=cg, tc=tc, ps=ps: e.tensor_tensor(
                out=o_fm[:, cg, tc * 512:(tc + 1) * 512],
                in0=o_fm[:, cg, tc * 512:(tc + 1) * 512], in1=ps, op=ALU.mult),
                reads=[C.b_ab[tc % 2], b_o], writes=[b_o])
            R.op("vector", lambda e, cg=cg, tc=tc: e.scalar_tensor_tensor(
                out=o_fm[:, cg, tc * 512:(tc + 1) * 512],
                in0=o_fm[:, cg, tc * 512:(tc + 1) * 512],
                scalar=st.smalls[:, 2:3], in1=st.q2[:, 0:512],
                op0=ALU.mult, op1=ALU.mult),
                reads=[st.b_q2, b_o, st.b_smalls], writes=[b_o])

    for nh in range(2):
        loadW(Wo, nh * 512)
        for t in range(16):
            ps = C.ps_ab[:, t % 2, :]
            b_ps = C.b_ab[t % 2]

            def mm_wo(e, t=t, ps=ps):
                last = None
                for cc in range(8):
                    last = e.matmul(ps, o_fm[:, cc, t * 128:(t + 1) * 128],
                                    st.w[:, cc, :], start=(cc == 0), stop=(cc == 7))
                return last
            R.op("tensor", mm_wo, reads=[st.b_w, b_o], writes=[b_ps])
            xadd_fn(t, nh, ps, b_ps)



def build_l0ffn_program():
    nc = bass.Bass("TRN2", target_bir_lowering=False, dynamic_dma_scratch_size=16384)
    x_in = nc.dram_tensor("x_in", [4096, 1024], F32, kind="ExternalInput").ap()
    wg_i = nc.dram_tensor("wgT", [1024, 3072], I32, kind="ExternalInput").ap()
    wu_i = nc.dram_tensor("wuT", [1024, 3072], I32, kind="ExternalInput").ap()
    wd_i = nc.dram_tensor("wdT", [3072, 1024], I32, kind="ExternalInput").ap()
    x_out = nc.dram_tensor("x_out", [4096, 1024], F32, kind="ExternalOutput").ap()
    wg_b = nc.dram_tensor("wg_bf", [1024, 3072], BF16, kind="Internal").ap()
    wu_b = nc.dram_tensor("wu_bf", [1024, 3072], BF16, kind="Internal").ap()
    wd_b = nc.dram_tensor("wd_bf", [3072, 1024], BF16, kind="Internal").ap()

    R = Rec(nc)
    C = emit_common(R)
    C.const(1e-5)
    b_w = Buf("wbf")
    cvt = R.sbuf("cvt", [128, 2, 3072], BF16)
    b_cvt = [Buf(f"cvt{i}") for i in range(2)]
    c_cl = [R.ctr(f"cvt_cl{i}", 16) for i in range(2)]
    c_cs = [R.ctr(f"cvt_cs{i}", 16) for i in range(2)]
    emit_int_to_bf16(R, cvt, b_cvt, c_cl, c_cs, wg_i, wg_b, 1024, 3072, b_w)
    emit_int_to_bf16(R, cvt, b_cvt, c_cl, c_cs, wu_i, wu_b, 1024, 3072, b_w)
    emit_int_to_bf16(R, cvt, b_cvt, c_cl, c_cs, wd_i, wd_b, 3072, 1024, b_w)

    x_sb = R.sbuf("x_sb", [128, 16, 1024], F32)
    b_x = Buf("x")
    c_x = R.ctr("c_x", 16)
    c_xo = R.ctr("c_xo", 16)
    st = l0ffn_state(R, "f")
    b_xout = Buf("xout")
    for p in range(2):
        R.dma("sync", x_sb[:, :, :],
              x_in[p * 2048:(p + 1) * 2048, :].rearrange("(a p) d -> p a d", p=128),
              c_x, reads=[b_x], writes=[b_x])
        emit_l0ffn(R, C, st, x_sb, b_x, wg_b, wu_b, wd_b, b_w)
        R.dma("sync", x_out[p * 2048:(p + 1) * 2048, :].rearrange("(a p) d -> p a d", p=128),
              x_sb[:, :, :], c_xo, reads=[b_x], writes=[b_xout])
        R.full_barrier()
    R.drain("sync", [c_xo])
    R.emit()
    return nc, R


def build_l1attn_program():
    nc = bass.Bass("TRN2", target_bir_lowering=False, dynamic_dma_scratch_size=16384)
    x_in = nc.dram_tensor("x_in", [4096, 1024], F32, kind="ExternalInput").ap()
    wq = nc.dram_tensor("WqT", [1024, 1024], F32, kind="ExternalInput").ap()
    wk = nc.dram_tensor("WkT", [1024, 1024], F32, kind="ExternalInput").ap()
    wv = nc.dram_tensor("WvT", [1024, 1024], F32, kind="ExternalInput").ap()
    wo = nc.dram_tensor("WoT", [1024, 1024], F32, kind="ExternalInput").ap()
    bq = nc.dram_tensor("bq", [1024], F32, kind="ExternalInput").ap()
    bk = nc.dram_tensor("bk", [1024], F32, kind="ExternalInput").ap()
    bv = nc.dram_tensor("bv", [1024], F32, kind="ExternalInput").ap()
    bo = nc.dram_tensor("bo", [1024], F32, kind="ExternalInput").ap()
    x_out = nc.dram_tensor("x_out", [4096, 1024], F32, kind="ExternalOutput").ap()

    R = Rec(nc)
    C = emit_common(R)
    C.const(1e-5)
    x_sb = R.sbuf("x_sb", [128, 16, 1024], F32)
    b_x = Buf("x")
    c_x = R.ctr("c_x", 16)
    c_xo = R.ctr("c_xo", 16)
    st = l1attn_state(R, C, "a")
    emit_l1attn_setup(R, C, st, "a")
    b_xout = Buf("xout")
    def x_ap_fn(i):
        return x_sb[:, i, :]

    def xadd_fn(t, nh, ps, b_ps):
        R.op("vector", lambda e, t=t, nh=nh, ps=ps: e.scalar_tensor_tensor(
            out=x_sb[:, t, nh * 512:(nh + 1) * 512], in0=ps, scalar=0.5,
            in1=x_sb[:, t, nh * 512:(nh + 1) * 512], op0=ALU.mult, op1=ALU.add),
            reads=[b_ps, b_x], writes=[b_x])
    for p in range(2):
        R.dma("sync", x_sb[:, :, :],
              x_in[p * 2048:(p + 1) * 2048, :].rearrange("(a p) d -> p a d", p=128),
              c_x, reads=[b_x], writes=[b_x])
        emit_l1attn(R, C, st, x_ap_fn, b_x, xadd_fn, wq, wk, wv, wo, bq, bk, bv, bo)
        R.dma("sync", x_out[p * 2048:(p + 1) * 2048, :].rearrange("(a p) d -> p a d", p=128),
              x_sb[:, :, :], c_xo, reads=[b_x], writes=[b_xout])
        R.full_barrier()
    R.drain("sync", [c_xo])
    R.emit()
    return nc, R


def kernel_l0ffn_only(x, wg, wu, wd):
    wgT = np.ascontiguousarray(wg.T)
    wuT = np.ascontiguousarray(wu.T)
    wdT = np.ascontiguousarray(wd.T)
    nc, R = build_l0ffn_program()
    in_maps = []
    for c in range(8):
        xs = np.ascontiguousarray(x[c * 2:(c + 1) * 2].reshape(4096, 1024))
        in_maps.append({"x_in": xs, "wgT": wgT, "wuT": wuT, "wdT": wdT})
    res = run_bass_kernel_spmd(nc, in_maps, core_ids=list(range(8)))
    out = np.stack([np.asarray(r["x_out"]).reshape(2, 2048, 1024) for r in res.results])
    return out.reshape(16, 2048, 1024)


def kernel_l1attn_only(x, Wq, bq, Wk, bk, Wv, bv, Wo, bo):
    m = {"WqT": np.ascontiguousarray(Wq.T), "WkT": np.ascontiguousarray(Wk.T),
         "WvT": np.ascontiguousarray(Wv.T), "WoT": np.ascontiguousarray(Wo.T),
         "bq": bq, "bk": bk, "bv": bv, "bo": bo}
    nc, R = build_l1attn_program()
    in_maps = []
    for c in range(8):
        xs = np.ascontiguousarray(x[c * 2:(c + 1) * 2].reshape(4096, 1024))
        in_maps.append(dict(m, x_in=xs))
    res = run_bass_kernel_spmd(nc, in_maps, core_ids=list(range(8)))
    out = np.stack([np.asarray(r["x_out"]).reshape(2, 2048, 1024) for r in res.results])
    return out.reshape(16, 2048, 1024)


def build_l1ffn_program():
    nc = bass.Bass("TRN2", target_bir_lowering=False, dynamic_dma_scratch_size=16384)
    x_in = nc.dram_tensor("x_in", [4096, 1024], F32, kind="ExternalInput").ap()
    w1 = [nc.dram_tensor(f"W1T{e}", [4096, 1024], F32, kind="ExternalInput").ap()
          for e in range(8)]
    w2 = [nc.dram_tensor(f"W2T{e}", [4096, 1024], F32, kind="ExternalInput").ap()
          for e in range(8)]
    gbT = nc.dram_tensor("gbT", [1024, 2, 8], F32, kind="ExternalInput").ap()
    b1T = nc.dram_tensor("b1T", [4096, 8], F32, kind="ExternalInput").ap()
    b2 = nc.dram_tensor("b2", [8, 1024], F32, kind="ExternalInput").ap()
    centT = nc.dram_tensor("centT", [1024, 8], F32, kind="ExternalInput").ap()
    out = nc.dram_tensor("out", [4096, 1024], F32, kind="ExternalOutput").ap()
    fhat_d = nc.dram_tensor("fhat_d", [128, 8, 4096], BF16, kind="Internal").ap()
    w1b = [nc.dram_tensor(f"w1b{e}", [4096, 1024], BF16, kind="Internal").ap()
           for e in range(8)]
    w2b = [nc.dram_tensor(f"w2b{e}", [4096, 1024], BF16, kind="Internal").ap()
           for e in range(8)]

    R = Rec(nc)
    C = emit_common(R)
    C.const(1e-5)
    aE = R.sbuf("aE", [128, 32, 8], F32)
    b_aE = Buf("aE")
    b_fd = Buf("fd")
    BW1 = [[Buf(f"cv1_{e}_{q}") for q in range(4)] for e in range(8)]
    BW2 = [[Buf(f"cv2_{e}_{q}") for q in range(4)] for e in range(8)]
    c_conv = R.ctr("c_conv", 16)
    for e in range(8):
        for q in range(4):
            sl = slice(q * 1024, (q + 1) * 1024)
            R.dma("gpsimd", w1b[e][sl, :], w1[e][sl, :], c_conv, writes=[BW1[e][q]])
            R.dma("gpsimd", w2b[e][sl, :], w2[e][sl, :], c_conv, writes=[BW2[e][q]])
    stp = l1ffn_pre_state(R, "p")
    stm = l1ffn_main_state(R, "m")
    emit_l1ffn_pre_setup(R, C, stp, centT)
    emit_l1ffn_main_setup(R, C, stm, gbT, b1T, b2)
    xt = R.sbuf("xt", [128, 2, 1024], F32)
    b_xt = Buf("xt")
    c_x = R.ctr("c_x", 16)
    b_out = [Buf(f"out{t}") for t in range(32)]
    c_out = R.ctr("c_out", 16)

    def make_xfn(p):
        def xfn(i):
            s2 = i % 2
            R.dma("sync", xt[:, s2, :],
                  x_in[(p * 16 + i) * 128:(p * 16 + i + 1) * 128, :], c_x,
                  reads=[b_xt], writes=[b_xt])
            return xt[:, s2, :]
        return xfn
    for p in range(2):
        emit_l1ffn_pre(R, C, stp, make_xfn(p), b_xt, out, b_out, c_out, p * 16,
                       fhat_d, b_fd, aE, b_aE)
    R.full_barrier()
    emit_l1ffn_main(R, C, stm, w1b, w2b, (BW1, BW2), out, b_out, fhat_d, b_fd,
                    aE, b_aE)
    R.drain("sync", [stm.c_acc, c_out])
    R.emit()
    return nc, R


def kernel_l1ffn_only(x, centroids, gamma, beta, W1, b1, W2, b2):
    m = {"gbT": np.ascontiguousarray(np.stack([gamma.T, beta.T], axis=1)),
         "b1T": np.ascontiguousarray(b1.T), "b2": b2,
         "centT": np.ascontiguousarray(centroids.T)}
    for e in range(8):
        m[f"W1T{e}"] = np.ascontiguousarray(
            W1[e].reshape(32, 128, 8, 128).transpose(0, 3, 2, 1).reshape(4096, 1024))
        m[f"W2T{e}"] = np.ascontiguousarray(W2[e].T)
    nc, R = build_l1ffn_program()
    in_maps = []
    for c in range(8):
        xs = np.ascontiguousarray(x[c * 2:(c + 1) * 2].reshape(4096, 1024))
        in_maps.append(dict(m, x_in=xs))
    res = run_bass_kernel_spmd(nc, in_maps, core_ids=list(range(8)))
    o = np.stack([np.asarray(r["out"]).reshape(2, 2048, 1024) for r in res.results])
    return o.reshape(16, 2048, 1024)



def build_l0attn_program(n_chunks=32):
    nc = bass.Bass("TRN2", target_bir_lowering=False, dynamic_dma_scratch_size=16384)
    x_in = nc.dram_tensor("x_in", [4096, 1024], F32, kind="ExternalInput").ap()
    Wq = nc.dram_tensor("Wq", [1024, 1024], F32, kind="ExternalInput").ap()
    Wk = nc.dram_tensor("Wk", [1024, 1024], F32, kind="ExternalInput").ap()
    Wv = nc.dram_tensor("Wv", [1024, 1024], F32, kind="ExternalInput").ap()
    Wg = nc.dram_tensor("Wg", [1024, 1024], F32, kind="ExternalInput").ap()
    Wo = nc.dram_tensor("Wo", [1024, 1024], F32, kind="ExternalInput").ap()
    Wb = nc.dram_tensor("Wb", [1024, 16], F32, kind="ExternalInput").ap()
    Wa = nc.dram_tensor("Wa", [1024, 16], F32, kind="ExternalInput").ap()
    dtb = nc.dram_tensor("dtb", [16], F32, kind="ExternalInput").ap()
    alog = nc.dram_tensor("alog", [16], F32, kind="ExternalInput").ap()
    nw = nc.dram_tensor("nw", [64], F32, kind="ExternalInput").ap()
    x_out = nc.dram_tensor("x_out", [4096, 1024], F32, kind="ExternalOutput").ap()
    g_d = nc.dram_tensor("g_d", [16, 2048], F32, kind="Internal").ap()
    be_d = nc.dram_tensor("be_d", [16, 2048], F32, kind="Internal").ap()
    gt_d = nc.dram_tensor("gt_d", [128, 8, 2048], BF16, kind="Internal").ap()

    R = Rec(nc)
    C = emit_common(R)
    C.const(1e-5)
    st = l0attn_state(R, "d")
    emit_l0attn_setup(R, C, st)
    xlnt = R.sbuf("d_xlnt", [128, 8, 2048], BF16)
    b_xlnt = Buf("d_xlnt")
    xt = st.fscr.rearrange("p (a b) -> p a b", a=1)
    b_xt = st.b_fscr
    xo = st.S32
    b_xo = st.b_S32
    c_xi = R.ctr("d_cxi", 16)
    c_xo = R.ctr("d_cxo", 16)
    b_gd = Buf("d_gd")
    b_xout = Buf("d_xout")

    for p in range(2):
        def x_ap_fn(i, p=p):
            R.dma("sync", xt[:, 0, :], x_in[(p * 16 + i) * 128:(p * 16 + i + 1) * 128, :],
                  c_xi, reads=[b_xt], writes=[b_xt])
            return xt[:, 0, :]

        def xadd_fn(t, nh, ps, b_ps, p=p):
            sl = t % 2
            rowsl = slice((p * 16 + t) * 128, (p * 16 + t + 1) * 128)
            colsl = slice(nh * 512, (nh + 1) * 512)
            R.dma("sync", xt[:, 0, colsl], x_in[rowsl, colsl],
                  c_xi, reads=[b_xt], writes=[b_xt])
            R.op("vector", lambda e, colsl=colsl, ps=ps: e.tensor_add(
                xo[:, :, :], ps.rearrange("p (a b) -> p a b", b=64),
                xt[:, 0, colsl].rearrange("p (a b) -> p a b", b=64)),
                reads=[b_ps, b_xt], writes=[b_xo])
            R.dma("sync",
                  x_out[rowsl, colsl].rearrange("p (a b) -> p a b", b=64),
                  xo[:, :, :], c_xo, reads=[b_xo], writes=[b_xout])
        emit_l0attn(R, C, st, x_ap_fn, b_xt, xadd_fn, xlnt, b_xlnt, xlnt, b_xlnt,
                    Wq, Wk, Wv, Wg, Wo, Wb, Wa, dtb, alog, nw, g_d, be_d, b_gd,
                    gt_d, Buf("d_gt"), n_chunks=n_chunks)
        R.full_barrier()
    R.drain("sync", [c_xo])
    R.emit()
    return nc, R


def kernel_l0attn_only(x, Wq, Wk, Wv, Wg, Wo, Wb, Wa, dt_bias, A_log, norm_w):
    m = {"Wq": Wq, "Wk": Wk, "Wv": Wv, "Wg": Wg, "Wo": Wo, "Wb": Wb, "Wa": Wa,
         "dtb": dt_bias, "alog": A_log, "nw": norm_w}
    nc, R = build_l0attn_program()
    in_maps = []
    for c in range(8):
        xs = np.ascontiguousarray(x[c * 2:(c + 1) * 2].reshape(4096, 1024))
        in_maps.append(dict(m, x_in=xs))
    res = run_bass_kernel_spmd(nc, in_maps, core_ids=list(range(8)))
    o = np.stack([np.asarray(r["x_out"]).reshape(2, 2048, 1024) for r in res.results])
    return o.reshape(16, 2048, 1024)





def build_fused_program(arena_kb=200):
    nc = bass.Bass("TRN2", target_bir_lowering=False, dynamic_dma_scratch_size=16384)
    DT = nc.dram_tensor
    x_in = DT("x_in", [4096, 1024], F32, kind="ExternalInput").ap()
    Wq = DT("Wq", [1024, 1024], F32, kind="ExternalInput").ap()
    Wk = DT("Wk", [1024, 1024], F32, kind="ExternalInput").ap()
    Wv = DT("Wv", [1024, 1024], F32, kind="ExternalInput").ap()
    Wg = DT("Wg", [1024, 1024], F32, kind="ExternalInput").ap()
    Wo = DT("Wo", [1024, 1024], F32, kind="ExternalInput").ap()
    Wb = DT("Wb", [1024, 16], F32, kind="ExternalInput").ap()
    Wa = DT("Wa", [1024, 16], F32, kind="ExternalInput").ap()
    dtb = DT("dtb", [16], F32, kind="ExternalInput").ap()
    alog = DT("alog", [16], F32, kind="ExternalInput").ap()
    nw = DT("nw", [64], F32, kind="ExternalInput").ap()
    wg_i = DT("wgT", [1024, 3072], I32, kind="ExternalInput").ap()
    wu_i = DT("wuT", [1024, 3072], I32, kind="ExternalInput").ap()
    wd_i = DT("wdT", [3072, 1024], I32, kind="ExternalInput").ap()
    AWq = DT("AWqT", [1024, 1024], F32, kind="ExternalInput").ap()
    AWk = DT("AWkT", [1024, 1024], F32, kind="ExternalInput").ap()
    AWv = DT("AWvT", [1024, 1024], F32, kind="ExternalInput").ap()
    AWo = DT("AWoT", [1024, 1024], F32, kind="ExternalInput").ap()
    bq = DT("bq", [1024], F32, kind="ExternalInput").ap()
    bk = DT("bk", [1024], F32, kind="ExternalInput").ap()
    bv = DT("bv", [1024], F32, kind="ExternalInput").ap()
    bo = DT("bo", [1024], F32, kind="ExternalInput").ap()
    w1 = [DT(f"W1T{e}", [4096, 1024], F32, kind="ExternalInput").ap() for e in range(8)]
    w2 = [DT(f"W2T{e}", [4096, 1024], F32, kind="ExternalInput").ap() for e in range(8)]
    gbT = DT("gbT", [1024, 2, 8], F32, kind="ExternalInput").ap()
    b1T = DT("b1T", [4096, 8], F32, kind="ExternalInput").ap()
    b2 = DT("b2", [8, 1024], F32, kind="ExternalInput").ap()
    centT = DT("centT", [1024, 8], F32, kind="ExternalInput").ap()
    out = DT("out", [4096, 1024], F32, kind="ExternalOutput").ap()
    x_h = DT("x_h", [4096, 1024], F32, kind="Internal").ap()
    wg_b = DT("wg_bf", [1024, 3072], BF16, kind="Internal").ap()
    wu_b = DT("wu_bf", [1024, 3072], BF16, kind="Internal").ap()
    wd_b = DT("wd_bf", [3072, 1024], BF16, kind="Internal").ap()
    g_d = DT("g_d", [16, 2048], F32, kind="Internal").ap()
    be_d = DT("be_d", [16, 2048], F32, kind="Internal").ap()
    fhat_d = DT("fhat_d", [128, 8, 4096], BF16, kind="Internal").ap()
    gt_d = DT("gt_d", [128, 8, 2048], BF16, kind="Internal").ap()
    w1b = [DT(f"w1b{e}", [4096, 1024], BF16, kind="Internal").ap() for e in range(8)]
    w2b = [DT(f"w2b{e}", [4096, 1024], BF16, kind="Internal").ap() for e in range(8)]

    R = Rec(nc)
    R.use_arena(arena_kb * 1024)
    C = emit_common(R)
    C.const(1e-5)
    C.const(1.0 / 127.0)
    aE = R.sbuf("aE", [128, 32, 8], F32)
    b_aE = Buf("aE")
    b_fd, b_gd, b_w, b_xh = Buf("fd"), Buf("gd"), Buf("wbf"), Buf("xh")
    b_out = [Buf(f"out{t}") for t in range(32)]
    c_out = R.ctr("c_out", 16)
    c_xh = R.ctr("c_xh", 16)
    BW1 = [[Buf(f"cv1_{e}_{q}") for q in range(4)] for e in range(8)]
    BW2 = [[Buf(f"cv2_{e}_{q}") for q in range(4)] for e in range(8)]
    c_conv = R.ctr("c_conv", 16)
    conv_jobs = [(wg_b, wg_i, b_w), (wu_b, wu_i, b_w), (wd_b, wd_i, b_w)]
    for e in range(8):
        for q in range(4):
            sl = slice(q * 1024, (q + 1) * 1024)
            conv_jobs.append((w1b[e][sl, :], w1[e][sl, :], BW1[e][q]))
            conv_jobs.append((w2b[e][sl, :], w2[e][sl, :], BW2[e][q]))
    conv_it = iter(conv_jobs)

    def conv_hook():
        try:
            dst, src, bw = next(conv_it)
        except StopIteration:
            return
        R.dma("gpsimd", dst, src, c_conv, writes=[bw])
    R.set_region_base()

    for p in range(2):
        R.open_region()
        st0 = l0attn_state(R, f"a{p}")
        emit_l0attn_setup(R, C, st0)
        xlnt0 = R.sbuf(f"a{p}_xlnt", [128, 8, 2048], BF16)
        b_xlnt0 = Buf(f"a{p}_xlnt")
        xt0 = st0.fscr.rearrange("p (a b) -> p a b", a=1)
        b_xt0 = st0.b_fscr
        xo0 = st0.S32
        b_xo0 = st0.b_S32

        def x_ap0(i, p=p):
            R.dma("sync", xt0[:, 0, :],
                  x_in[(p * 16 + i) * 128:(p * 16 + i + 1) * 128, :],
                  c_xh, reads=[b_xt0], writes=[b_xt0])
            return xt0[:, 0, :]

        def xadd0(t, nh, ps, b_ps, p=p):
            rowsl = slice((p * 16 + t) * 128, (p * 16 + t + 1) * 128)
            colsl = slice(nh * 512, (nh + 1) * 512)
            R.dma("sync", xt0[:, 0, colsl], x_in[rowsl, colsl],
                  c_xh, reads=[b_xt0], writes=[b_xt0])
            R.op("vector", lambda e, colsl=colsl, ps=ps: e.tensor_add(
                xo0[:, :, :], ps.rearrange("p (a b) -> p a b", b=64),
                xt0[:, 0, colsl].rearrange("p (a b) -> p a b", b=64)),
                reads=[b_ps, b_xt0], writes=[b_xo0])
            R.dma("sync", x_h[rowsl, colsl].rearrange("p (a b) -> p a b", b=64),
                  xo0[:, :, :], c_xh, reads=[b_xo0], writes=[b_xh])
        emit_l0attn(R, C, st0, x_ap0, b_xt0, xadd0, xlnt0, b_xlnt0, xlnt0, b_xlnt0,
                    Wq, Wk, Wv, Wg, Wo, Wb, Wa, dtb, alog, nw, g_d, be_d, b_gd,
                    gt_d, Buf(f"gt{p}"), hook=conv_hook)
        R.full_barrier()

        R.open_region()
        stf = l0ffn_state(R, f"f{p}")
        x_sb = R.sbuf(f"f{p}_xsb", [128, 16, 1024], F32)
        b_x = Buf(f"f{p}_x")
        R.dma("sync", x_sb[:, :, :],
              x_h[p * 2048:(p + 1) * 2048, :].rearrange("(a p) d -> p a d", p=128),
              c_xh, reads=[b_xh, b_x], writes=[b_x])
        emit_l0ffn(R, C, stf, x_sb, b_x, wg_b, wu_b, wd_b, b_w)
        R.dma("sync", x_h[p * 2048:(p + 1) * 2048, :].rearrange("(a p) d -> p a d", p=128),
              x_sb[:, :, :], c_xh, reads=[b_x], writes=[b_xh])
        R.full_barrier()

        R.open_region()
        sta = l1attn_state(R, C, f"t{p}")
        emit_l1attn_setup(R, C, sta, f"t{p}")
        xa_st = R.sbuf(f"t{p}_xst", [128, 2, 1024], F32)
        b_xa = Buf(f"t{p}_x")

        def x_ap1(i, p=p):
            R.dma("sync", xa_st[:, 0, :],
                  x_h[(p * 16 + i) * 128:(p * 16 + i + 1) * 128, :], c_xh,
                  reads=[b_xh, b_xa], writes=[b_xa])
            return xa_st[:, 0, :]

        def xadd1(t, nh, ps, b_ps, p=p):
            rowsl = slice((p * 16 + t) * 128, (p * 16 + t + 1) * 128)
            colsl = slice(nh * 512, (nh + 1) * 512)
            R.dma("sync", xa_st[:, 0, colsl], x_h[rowsl, colsl], c_xh,
                  reads=[b_xh, b_xa], writes=[b_xa])
            R.op("vector", lambda e, colsl=colsl, nh=nh, ps=ps: e.scalar_tensor_tensor(
                out=xa_st[:, 1, nh * 512:(nh + 1) * 512], in0=ps, scalar=0.5,
                in1=xa_st[:, 0, colsl], op0=ALU.mult, op1=ALU.add),
                reads=[b_ps, b_xa], writes=[b_xa])
            R.dma("sync", x_h[rowsl, colsl], xa_st[:, 1, colsl], c_xh,
                  reads=[b_xa], writes=[b_xh])
        emit_l1attn(R, C, sta, x_ap1, b_xa, xadd1, AWq, AWk, AWv, AWo, bq, bk, bv, bo)
        R.full_barrier()

        R.open_region()
        stp = l1ffn_pre_state(R, f"q{p}")
        emit_l1ffn_pre_setup(R, C, stp, centT)
        xtq = R.sbuf(f"q{p}_xt", [128, 2, 1024], F32)
        b_xtq = Buf(f"q{p}_xt")

        def x_apq(i, p=p):
            s2 = i % 2
            R.dma("sync", xtq[:, s2, :],
                  x_h[(p * 16 + i) * 128:(p * 16 + i + 1) * 128, :], c_xh,
                  reads=[b_xh, b_xtq], writes=[b_xtq])
            return xtq[:, s2, :]
        emit_l1ffn_pre(R, C, stp, x_apq, b_xtq, out, b_out, c_out, p * 16,
                       fhat_d, b_fd, aE, b_aE)
        R.full_barrier()

    for dst, src, bw in conv_it:
        R.dma("gpsimd", dst, src, c_conv, writes=[bw])
    R.open_region()
    stm = l1ffn_main_state(R, "m")
    emit_l1ffn_main_setup(R, C, stm, gbT, b1T, b2)
    emit_l1ffn_main(R, C, stm, w1b, w2b, (BW1, BW2), out, b_out, fhat_d, b_fd,
                    aE, b_aE)
    R.drain("sync", [stm.c_acc, c_out])
    R.drain("gpsimd", [stm.c_acc])
    R.emit()
    return nc, R


def kernel_fused(inputs):
    ii = inputs
    x = np.asarray(ii["L0_attn_x"])
    ca = np.ascontiguousarray
    m = {"Wq": ca(np.asarray(ii["L0_attn_Wq"])), "Wk": ca(np.asarray(ii["L0_attn_Wk"])),
         "Wv": ca(np.asarray(ii["L0_attn_Wv"])), "Wg": ca(np.asarray(ii["L0_attn_Wg"])),
         "Wo": ca(np.asarray(ii["L0_attn_Wo"])), "Wb": ca(np.asarray(ii["L0_attn_Wb"])),
         "Wa": ca(np.asarray(ii["L0_attn_Wa"])),
         "dtb": ca(np.asarray(ii["L0_attn_dt_bias"])),
         "alog": ca(np.asarray(ii["L0_attn_A_log"])),
         "nw": ca(np.asarray(ii["L0_attn_norm_w"])),
         "wgT": ca(np.asarray(ii["L0_ffn_w_gate"]).T),
         "wuT": ca(np.asarray(ii["L0_ffn_w_up"]).T),
         "wdT": ca(np.asarray(ii["L0_ffn_w_down"]).T),
         "AWqT": ca(np.asarray(ii["L1_attn_Wq"]).T),
         "AWkT": ca(np.asarray(ii["L1_attn_Wk"]).T),
         "AWvT": ca(np.asarray(ii["L1_attn_Wv"]).T),
         "AWoT": ca(np.asarray(ii["L1_attn_Wo"]).T),
         "bq": ca(np.asarray(ii["L1_attn_bq"])), "bk": ca(np.asarray(ii["L1_attn_bk"])),
         "bv": ca(np.asarray(ii["L1_attn_bv"])), "bo": ca(np.asarray(ii["L1_attn_bo"])),
         "gbT": ca(np.stack([np.asarray(ii["L1_ffn_gamma"]).T,
                             np.asarray(ii["L1_ffn_beta"]).T], axis=1)),
         "b1T": ca(np.asarray(ii["L1_ffn_b1"]).T),
         "b2": ca(np.asarray(ii["L1_ffn_b2"])),
         "centT": ca(np.asarray(ii["L1_ffn_centroids"]).T)}
    W1 = np.asarray(ii["L1_ffn_W1"])
    W2 = np.asarray(ii["L1_ffn_W2"])
    for e in range(8):
        m[f"W1T{e}"] = ca(W1[e].reshape(32, 128, 8, 128)
                          .transpose(0, 3, 2, 1).reshape(4096, 1024))
        m[f"W2T{e}"] = ca(W2[e].T)
    nc, R = build_fused_program()
    in_maps = []
    for c in range(8):
        in_maps.append(dict(m, x_in=ca(x[c * 2:(c + 1) * 2].reshape(4096, 1024))))
    res = run_bass_kernel_spmd(nc, in_maps, core_ids=list(range(8)))
    o = np.stack([np.asarray(r["out"]).reshape(2, 2048, 1024) for r in res.results])
    return o.reshape(16, 2048, 1024)


def kernel(**inputs):
    return kernel_fused(inputs)
